# version 04
# baseline (speedup 1.0000x reference)
import math
import numpy as np
import ml_dtypes
import concourse.bass as bass
import concourse.mybir as mybir
from concourse.bass_utils import run_bass_kernel_spmd

F32 = mybir.dt.float32
BF16 = mybir.dt.bfloat16
I32 = mybir.dt.int32
Alu = mybir.AluOpType
Act = mybir.ActivationFunctionType
AX = mybir.AxisListType.X

B, S, H, Dh, DM = 2, 4096, 32, 128, 4096
NCORE = 8
NF = DM // 128
SCALE = 1.0 / Dh
M_BOUND = 16.0
QMAX = 127.0
EPS = 1e-5
BIG_NEG = -3.0e38
C23 = 1.5 * 2.0 ** 23
C52 = 1.5 * 2.0 ** 52

TABW = 4608
AW = 4480
KVQ = 3 * Dh * 512
A1BLK = 4 * B * KVQ
A1SZ = NCORE * A1BLK
NKS = [4 * (s + 1) for s in range(8)]
YCH = 8 * 1024 * 1024
YMC = 4 * 1024 * 1024
NIDX = 48

DEBUG_OUTS = []
PHASES = 5


def _slopes_f32():
    base = 2 ** (-(2 ** (-(math.log2(H) - 3))))
    b32 = float(np.float32(base))
    return np.float32(np.cumprod(np.full(H, b32, dtype=np.float64)))


def _consts_block(core_id):
    c = np.zeros((128, 132), dtype=np.float32)
    c[:, 0:128] = np.eye(128, dtype=np.float32)
    c[:, 128] = 1.0
    c[:, 129] = np.float32(C23)
    sl = _slopes_f32()
    for p in range(32):
        hl, s = p // 8, p % 8
        c[p, 130] = sl[4 * core_id + hl]
        c[p, 131] = np.float32(511 + s * 512)
    return c


class SemPlan:
    def __init__(self):
        self.counts = {}
        self.marks = {}
        self.pass_no = 0

    def begin(self, n):
        self.pass_no = n
        self.counts = {}

    def inc(self, name, n):
        self.counts[name] = self.counts.get(name, 0) + n
        return self.counts[name]

    def mark(self, name, label):
        key = (name, label)
        cur = self.counts.get(name, 0)
        if self.pass_no == 1:
            assert key not in self.marks, f"duplicate mark {key}"
            self.marks[key] = cur
        else:
            assert self.marks.get(key) == cur, (key, self.marks.get(key), cur)

    def at(self, name, label):
        if self.pass_no == 1:
            return 0
        return self.marks[(name, label)]


_NOCHAIN = ("dma_start", "indirect_dma_start", "collective_compute", "wait_ge")


class MockInst:
    def __init__(self, plan, semmap):
        self.plan = plan
        self.semmap = semmap
        self.had_inc = False

    def then_inc(self, sem, n=1):
        self.had_inc = True
        self.plan.inc(self.semmap[id(sem)], n)
        return self


class MockEng:

    def __init__(self, plan, semmap, auto=None):
        self._plan = plan
        self._semmap = semmap
        self._auto = auto
        self._pend = None

    def _resolve(self):
        if self._pend is not None:
            mi = self._pend
            self._pend = None
            if not mi.had_inc:
                self._plan.inc(self._auto, 1)

    def __getattr__(self, name):
        if name.startswith("_"):
            raise AttributeError(name)
        if name == "wait_ge":
            def wgm(*a, **k):
                self._resolve()
            return wgm

        def call(*a, **k):
            self._resolve()
            mi = MockInst(self._plan, self._semmap)
            if self._auto is not None and name not in _NOCHAIN:
                self._pend = mi
            return mi
        return call


TRJOBS = [("wq", i // NF, i % NF) for i in range(128)]
for _b in range(B):
    for _tt in range(4):
        TRJOBS += [("xn", _b, _tt, fc) for fc in range(NF)]
        TRJOBS += [("xq", _b, _tt, fc) for fc in range(NF)]
NTR = len(TRJOBS)

KT = []
UNIT_OF = {}
KT_LAST = {}
for _hl in range(4):
    for _s in range(8):
        for _b in range(B):
            UNIT_OF[(_hl, _s, _b)] = len(UNIT_OF)
            for _ki in range(NKS[_s]):
                KT.append((_hl, _s, _b, _ki))
                KT_LAST[(_hl, _s, _b)] = len(KT) - 1
NKT = len(KT)
UNITS_SEQ = list(UNIT_OF.keys())

QUSERS = []
for _b in range(B):
    QUSERS += [("proj", _b, ui) for ui in range(24)]
    QUSERS += [("ffn", _b, jc) for jc in range(8)]
QIDX = {u: i for i, u in enumerate(QUSERS)}

PUNITS = [("K", g) for g in range(8)] + [("V", g) for g in range(8)] + \
         [("Q", g) for g in range(8)]


def build_nc(debug_outs=(), phases=5):
    nc = bass.Bass("TRN2", target_bir_lowering=False)

    xrows = nc.dram_tensor("xrows", [1024, DM], F32, kind="ExternalInput").ap()
    w_attn = nc.dram_tensor("w_attn", [DM, 3 * DM], F32, kind="ExternalInput").ap()
    w_proj = nc.dram_tensor("w_proj", [DM, DM], F32, kind="ExternalInput").ap()
    w_ffn = nc.dram_tensor("w_ffn", [512, DM], F32, kind="ExternalInput").ap()
    b_attn = nc.dram_tensor("b_attn", [1, 3 * DM], F32, kind="ExternalInput").ap()
    b_attn_t = nc.dram_tensor("b_attn_t", [128, 96], F32, kind="ExternalInput").ap()
    b_proj = nc.dram_tensor("b_proj", [1, DM], F32, kind="ExternalInput").ap()
    consts = nc.dram_tensor("consts", [128, 132], F32, kind="ExternalInput").ap()
    out = nc.dram_tensor("out", [1024, DM], F32, kind="ExternalOutput").ap()

    a1_in = nc.dram_tensor("a1_in", [1, A1BLK * NCORE], F32, kind="Internal").ap()
    a1o = [nc.dram_tensor(f"a1o_{u}", [NCORE, NCORE * KVQ], F32, kind="Internal").ap()
           for u in range(8)]
    a1idx = nc.dram_tensor("a1idx", [128, 8 * NIDX + 64], I32, kind="ExternalInput").ap()
    o_scr = nc.dram_tensor("o_scr", [1, 4 * B * Dh * S], BF16, kind="Internal").ap()
    o_scr_lo = nc.dram_tensor("o_scr_lo", [1, 4 * B * Dh * S], BF16, kind="Internal").ap()
    o_go = nc.dram_tensor("o_go", [NCORE, 4 * B * Dh * S], BF16, kind="Internal").ap()
    o_go_lo = nc.dram_tensor("o_go_lo", [NCORE, 4 * B * Dh * S], BF16, kind="Internal").ap()
    g1_in = nc.dram_tensor("g1_in", [1, DM * 512], BF16, kind="Internal").ap()
    g1_out = nc.dram_tensor("g1_out", [NCORE, DM * 512], BF16, kind="Internal").ap()
    g1s_in = nc.dram_tensor("g1s_in", [1, 512], F32, kind="Internal").ap()
    g1s_out = nc.dram_tensor("g1s_out", [NCORE, 512], F32, kind="Internal").ap()
    htab = nc.dram_tensor("htab", [32, TABW], F32, kind="Internal").ap()
    ffnscr = nc.dram_tensor("ffnscr", [1024, DM], F32, kind="Internal").ap()

    dbg = {}
    shapes = {"a1_in": ([1, A1BLK * NCORE], F32),
              "htab": ([32, TABW], F32), "ffnscr": ([1024, DM], F32),
              "g1_out": ([NCORE, DM * 512], BF16),
              "g1s_out": ([NCORE, 512], F32),
              "o_scr": ([1, 4 * B * Dh * S], BF16),
              "o_scr_lo": ([1, 4 * B * Dh * S], BF16),
              "p3kvq": ([128, 24576], F32), "p3tail": ([128, 16384], F32)}
    src_map = {"a1_in": a1_in, "htab": htab, "ffnscr": ffnscr,
               "g1_out": g1_out, "g1s_out": g1s_out, "o_scr": o_scr,
               "o_scr_lo": o_scr_lo}
    for name in debug_outs:
        shp, dt = shapes[name]
        dbg[name] = nc.dram_tensor("dbg_" + name, shp, dt, kind="ExternalOutput").ap()

    import contextlib
    with contextlib.ExitStack() as _st:
        A = _st.enter_context(nc.sbuf_tensor("A", [128, 48128], F32))
        wbuf = _st.enter_context(nc.sbuf_tensor("wbuf", [128, 2 * 512], F32))
        whl = _st.enter_context(nc.sbuf_tensor("whl", [128, 4 * 512], BF16))
        cst = _st.enter_context(nc.sbuf_tensor("cst", [128, 132], F32))
        cols = _st.enter_context(nc.sbuf_tensor("cols", [128, 32], F32))
        bcol = _st.enter_context(nc.sbuf_tensor("bcol", [128, 96], F32))
        wqs = _st.enter_context(nc.sbuf_tensor("wqs", [128, 4 * 512], BF16))
        wtr = _st.enter_context(nc.sbuf_tensor("wtr", [128, 2 * 128], BF16))
        itmp = _st.enter_context(nc.sbuf_tensor("itmp", [128, 1536], I32))
        pq0 = _st.enter_context(nc.psum_tensor("pq0", [128, 512], F32))
        pq1 = _st.enter_context(nc.psum_tensor("pq1", [128, 512], F32))
        pq2 = _st.enter_context(nc.psum_tensor("pq2", [128, 512], F32))
        pq3 = _st.enter_context(nc.psum_tensor("pq3", [128, 512], F32))
        ps0 = _st.enter_context(nc.psum_tensor("ps0", [128, 512], F32))
        ps1 = _st.enter_context(nc.psum_tensor("ps1", [128, 512], F32))
        po = _st.enter_context(nc.psum_tensor("po", [128, 512], F32))
        pd = _st.enter_context(nc.psum_tensor("pd", [128, 512], F32))
        DMA_SEMS = ["sP0", "sX0", "sX1", "sW0", "sW1", "sWQ0", "sWQ1", "sWQ2", "sWQ3",
                    "sWX0", "sWX1", "sAS0", "sAS1", "sKV0", "sKV1", "sOL",
                    "sBP", "sWP0", "sWP1", "sP50", "sP51",
                    "sWT0", "sWT1", "sG1S", "sHT", "sST0", "sST1",
                    "sOS", "sOS1", "sY0", "sY1", "sY2", "sY3", "sO0", "sO1", "sDBG"]
        SEM = {nm: _st.enter_context(nc.semaphore(name=nm)) for nm in DMA_SEMS}
        sGP = _st.enter_context(nc.semaphore(name="sGP"))
        sV = _st.enter_context(nc.semaphore(name="sV"))
        sS = _st.enter_context(nc.semaphore(name="sS"))
        sT = _st.enter_context(nc.semaphore(name="sT"))
        block = _st.enter_context(nc.Block())
        quad = [pq0, pq1, pq2, pq3]
        psc = [ps0, ps1]
        p3pool = [ps0, ps1, pq0, pq1, pq2, pq3]
        SEMS = {id(h): nm for nm, h in SEM.items()}
        SEMS.update({id(sGP): "sGP", id(sV): "sV", id(sS): "sS", id(sT): "sT"})

        def W_(e, nm, label):
            e.wait_ge(SEM[nm], SP.at(nm, label))
        SP = SemPlan()

        STORE_SEMS = ["sWT0", "sWT1", "sG1S", "sHT", "sST0", "sST1",
                      "sOS", "sOS1", "sY0", "sY1", "sY2", "sY3", "sO0", "sO1"]

        def mark_final():
            for nm in STORE_SEMS:
                SP.mark(nm, "fin")

        def wait_final(e):
            for nm in STORE_SEMS:
                e.wait_ge(SEM[nm], SP.at(nm, "fin"))

        def av(off, n):
            return A[:, off:off + n]

        def avb(off, nbf):
            return A.bitcast(BF16)[:, 2 * off:2 * off + nbf]

        TB_D, TB_V = 16384, 21120
        ITAB = 25728
        O_XT, O_XC, O_XNT, O_XNT_LO, O_XQT = 0, 8192, 12288, 20480, 28672
        O_EV, O_R1, O_WSX = 36864, 40960, 46592
        P3_KVQ, P3_A, P3_P, P3_SB = 0, 24576, 33536, 38656
        P3_DA, P3_RD, P3_OE, P3_ONE = 45056, 45568, 36608, 47616
        P4_OH, P4_OL, P4_WH, P4_WL = 0, 16384, 32768, 33792
        P4_EV = 34816
        P5_Y, P5_X, P5_F, P5_EV = 36864, 37888, 38912, 39936

        CL_FLAG, CL_CALL, CL_C52, CL_M, CL_T0, CL_R, CL_R2, CL_RA = 0, 1, 2, 3, 4, 5, 6, 7
        CL_AS = 8
        CL_WS = 16
        CL_WR = 20

        def col(i):
            return cols[:, i:i + 1]

        ident = cst[:, 0:128]
        onescol = cst[:, 128:129]

        def a1_off(dst, hl, b, part):
            return (hl * B + b) * (NCORE * KVQ) + dst * KVQ + part * Dh * 512

        def a2_off(dst, hl, b, which):
            return dst * (4 * B * OBLK) + (hl * B + b) * OBLK + which * Dh * 512

        def xnt_hi(fc, lo=0, n=512):
            return avb(O_XNT, 16384)[:, fc * 512 + lo:fc * 512 + lo + n]

        def xnt_lo(fc, lo=0, n=512):
            return avb(O_XNT_LO, 16384)[:, fc * 512 + lo:fc * 512 + lo + n]

        def xqt(fc, tt):
            return avb(O_XQT, 16384)[:, fc * 512 + tt * 128:fc * 512 + tt * 128 + 128]

        def emit_sync(e):
            e.dma_start(out=cst[:], in_=consts[:, :]).then_inc(SEM["sP0"], 16)
            e.dma_start(out=bcol[:], in_=b_attn_t[:, :]).then_inc(SEM["sP0"], 16)
            for rt in range(4):
                e.dma_start(out=av(O_XT + rt * 4096, 4096),
                            in_=w_ffn[rt * 128:(rt + 1) * 128, :]).then_inc(SEM["sP0"], 16)
            e.dma_start(out=itmp[:, 1024:1024 + 8 * NIDX + 64], in_=a1idx[:, :]).then_inc(SEM["sP0"], 16)
            e.dma_start(out=av(O_R1, 4096),
                        in_=bass.AP(tensor=b_attn.tensor, offset=2 * DM,
                                    ap=[[0, 128], [1, DM]])).then_inc(SEM["sP0"], 16)
            SP.mark("sP0", "p0")
            for b in range(B):
                for tt in range(4):
                    it = b * 4 + tt
                    if it < 2:
                        e.wait_ge(sT, SP.at("sT", "ttr_127"))
                    else:
                        pb_, pt = divmod(it - 2, 4)
                        e.wait_ge(sT, SP.at("sT", f"trq_{pb_}_{pt}"))
                    e.dma_start(out=av(O_XT + (it % 2) * 4096, 4096),
                                in_=xrows[b * 512 + tt * 128: b * 512 + (tt + 1) * 128, :]
                                ).then_inc(SEM[f"sX{it % 2}"], 16)
                    SP.mark(f"sX{it % 2}", f"x_{b}_{tt}")
                for ui, (kind, g) in enumerate(PUNITS):
                    base_col = {"Q": 0, "K": DM, "V": 2 * DM}[kind] + g * 512
                    for fc in range(NF):
                        i = (b * 24 + ui) * NF + fc
                        if i >= 2:
                            j = i - 2
                            pb_, rem = divmod(j, 24 * NF)
                            pui, pfc = divmod(rem, NF)
                            e.wait_ge(sV, SP.at("sV", f"wsp_{pb_}_{pui}_{pfc}"))
                        e.dma_start(out=wbuf[:, (i % 2) * 512:(i % 2 + 1) * 512],
                                    in_=w_attn[fc * 128:(fc + 1) * 128,
                                               base_col:base_col + 512]).then_inc(SEM[f"sW{i % 2}"], 16)
                        SP.mark(f"sW{i % 2}", f"w_{b}_{ui}_{fc}")
                if b == 0:
                    e.wait_ge(sGP, SP.at("sGP", "g1"))
                for jc in range(8):
                    i = b * 8 + jc
                    if i >= 2:
                        j = i - 2
                        e.wait_ge(sV, SP.at("sV", f"fevl_{j // 8}_{j % 8}"))
                    e.dma_start(out=av(O_WSX + (i % 2) * 512, 512),
                                in_=bass.AP(tensor=g1s_out.tensor, offset=jc * 512,
                                            ap=[[0, 128], [1, 512]])).then_inc(SEM[f"sWX{i % 2}"], 16)
                    SP.mark(f"sWX{i % 2}", f"wsx_{b}_{jc}")
                    for fc in range(NF):
                        iw = (b * 8 + jc) * NF + fc
                        if iw >= 4:
                            jw = iw - 4
                            pb_, rem = divmod(jw, 8 * NF)
                            pjc, pfc = divmod(rem, NF)
                            e.wait_ge(sT, SP.at("sT", f"fmm_{pb_}_{pjc}_{pfc}"))
                        e.dma_start(out=wqs[:, (iw % 4) * 512:(iw % 4 + 1) * 512],
                                    in_=bass.AP(tensor=g1_out.tensor,
                                                offset=jc * (DM * 512) + fc * 128 * 512,
                                                ap=[[512, 128], [1, 512]])).then_inc(SEM[f"sWQ{iw % 4}"], 16)
                        SP.mark(f"sWQ{iw % 4}", f"wq_{b}_{jc}_{fc}")
            for hl in range(4 if phases >= 3 else 0):
                for s in range(8):
                    if s >= 2:
                        e.wait_ge(sV, SP.at("sV", f"sb_{KT_LAST[(hl, s - 2, 1)]}"))
                    elif hl > 0:
                        e.wait_ge(sV, SP.at("sV", f"sb_{KT_LAST[(hl - 1, 6 + s, 1)]}"))
                    else:
                        e.wait_ge(sT, SP.at("sT", "fgrp_1_7"))
                        W_(e, "sHT", "htabst")
                    e.dma_start(out=av(P3_A + (s % 2) * AW, AW),
                                in_=bass.AP(tensor=htab.tensor,
                                            offset=(hl * 8 + s) * TABW,
                                            ap=[[1, 128], [1, AW]])).then_inc(SEM[f"sAS{s % 2}"], 16)
                    SP.mark(f"sAS{s % 2}", f"astrip_{hl}_{s}")
            e.wait_ge(sV, SP.at("sV", "r1free"))
            if phases < 4:
                wait_final(e)
                return
            e.dma_start(out=av(O_R1, 4096),
                        in_=bass.AP(tensor=b_proj.tensor, offset=0,
                                    ap=[[0, 128], [1, DM]])).then_inc(SEM["sBP"], 16)
            SP.mark("sBP", "bproj")
            first_tiles = True
            for bbp in range(B):
                for mc in range(8):
                    for ch in range(32):
                        i = (bbp * 8 + mc) * 32 + ch
                        if i >= 2:
                            e.wait_ge(sV, SP.at("sV", f"wpsp_{i - 2}"))
                        e.dma_start(out=wbuf[:, (i % 2) * 512:(i % 2 + 1) * 512],
                                    in_=w_proj[ch * 128:(ch + 1) * 128,
                                               mc * 512:(mc + 1) * 512]).then_inc(SEM[f"sWP{i % 2}"], 16)
                        SP.mark(f"sWP{i % 2}", f"wp_{i}")
                    for tt in range(4):
                        iv = (bbp * 8 + mc) * 4 + tt
                        if first_tiles:
                            e.wait_ge(sS, SP.at("sS", f"exp_{NKT - 1}"))
                            first_tiles = False
                        if iv >= 2:
                            e.wait_ge(sV, SP.at("sV", f"pe_{iv - 2}"))
                        row0 = bbp * 512 + tt * 128
                        e.dma_start(out=av(P5_X + (iv % 2) * 512, 512),
                                    in_=xrows[row0:row0 + 128, mc * 512:(mc + 1) * 512]).then_inc(SEM[f"sP5{iv % 2}"], 16)
                        e.dma_start(out=av(P5_F + (iv % 2) * 512, 512),
                                    in_=ffnscr[row0:row0 + 128, mc * 512:(mc + 1) * 512]).then_inc(SEM[f"sP5{iv % 2}"], 16)
                        SP.mark(f"sP5{iv % 2}", f"p4in_{iv}")
            wait_final(e)

        def emit_gpsimd(e):
            W_(e, "sP0", "p0")
            e.iota(out=A.bitcast(I32)[0:32, ITAB:ITAB + TABW], pattern=[[1, TABW]], base=0, channel_multiplier=0)
            e.tensor_copy(out=av(TB_D, TABW)[0:32, :], in_=A.bitcast(I32)[0:32, ITAB:ITAB + TABW]).then_inc(sGP, 1)
            SP.mark("sGP", "iota")
            W_(e, "sWT0", "wqst_126")
            W_(e, "sWT1", "wqst_127")
            W_(e, "sG1S", "g1ready")
            e.collective_compute("AllGather", Alu.bypass,
                                 replica_groups=[list(range(NCORE))],
                                 ins=[g1_in[:, :]], outs=[g1_out[:, :]]).then_inc(sGP, 1)
            e.collective_compute("AllGather", Alu.bypass,
                                 replica_groups=[list(range(NCORE))],
                                 ins=[g1s_in[:, :]], outs=[g1s_out[:, :]]).then_inc(sGP, 1)
            SP.mark("sGP", "g1")
            if phases >= 2:
                for b in range(B):
                    W_(e, "sST0", f"st_{b}_22_3")
                    W_(e, "sST1", f"st_{b}_23_3")
                    for h in range(4):
                        u = h * B + b
                        e.collective_compute("AllGather", Alu.bypass,
                                             replica_groups=[list(range(NCORE))],
                                             ins=[a1_in[:, u * (NCORE * KVQ):(u + 1) * (NCORE * KVQ)]],
                                             outs=[a1o[u][:, :]]).then_inc(sGP, 1)
                        SP.mark("sGP", f"a1g_{h}_{b}")
            if phases < 3:
                if dbg:
                    wait_final(e)
                    SP.mark("sGP", "gpend")
                    e.wait_ge(sGP, SP.at("sGP", "gpend"))
                    for name, ap_out in dbg.items():
                        sap = A[:, 0:24576] if name == "p3kvq" else (
                            A[:, 24576:40960] if name == "p3tail" else src_map[name][:, :])
                        e.dma_start(out=ap_out[:, :], in_=sap).then_inc(SEM["sDBG"], 16)
                    SP.mark("sDBG", "dbg")
                    W_(e, "sDBG", "dbg")
                return
            v512s = [a1o[u].rearrange("a b -> (a b)").rearrange("(n r) -> n r", r=512) for u in range(8)]
            v128s = [a1o[u].rearrange("a b -> (a b)").rearrange("(n r) -> n r", r=128) for u in range(8)]
            for hl in range(4):
                for bb in range(B):
                    u = hl * B + bb
                    e.wait_ge(sGP, SP.at("sGP", f"a1g_{hl}_{bb}"))
                    if hl > 0:
                        e.wait_ge(sT, SP.at("sT", f"pv_{KT_LAST[(hl - 1, 7, bb)]}"))
                    elif bb == 0:
                        e.wait_ge(sT, SP.at("sT", "trq_1_3"))
                    else:
                        e.wait_ge(sT, SP.at("sT", "grp_1_23"))
                    base = P3_KVQ + bb * 12288
                    ic0 = 1024 + u * NIDX
                    for r in range(8):
                        e.indirect_dma_start(
                            out=av(base, 4096)[:, r * 512:(r + 1) * 512],
                            out_offset=None, in_=v512s[u][:, :],
                            in_offset=bass.IndirectOffsetOnAxis(
                                ap=itmp[:, ic0 + r:ic0 + r + 1], axis=0)).then_inc(SEM[f"sKV{bb}"], 16)
                        e.indirect_dma_start(
                            out=av(base + 8192, 4096)[:, r * 512:(r + 1) * 512],
                            out_offset=None, in_=v512s[u][:, :],
                            in_offset=bass.IndirectOffsetOnAxis(
                                ap=itmp[:, ic0 + 8 + r:ic0 + 9 + r], axis=0)).then_inc(SEM[f"sKV{bb}"], 16)
                        for kt in range(4):
                            e.indirect_dma_start(
                                out=av(base + 4096, 4096)[:, (r * 4 + kt) * 128:(r * 4 + kt + 1) * 128],
                                out_offset=None, in_=v128s[u][:, :],
                                in_offset=bass.IndirectOffsetOnAxis(
                                    ap=itmp[:, ic0 + 16 + r * 4 + kt:ic0 + 17 + r * 4 + kt],
                                    axis=0)).then_inc(SEM[f"sKV{bb}"], 16)
                    SP.mark(f"sKV{bb}", f"kvq_{hl}_{bb}")
            if phases < 4:
                if dbg:
                    wait_final(e)
                    SP.mark("sGP", "gpend")
                    e.wait_ge(sGP, SP.at("sGP", "gpend"))
                    for name, ap_out in dbg.items():
                        sap = A[:, 0:24576] if name == "p3kvq" else (
                            A[:, 24576:40960] if name == "p3tail" else src_map[name][:, :])
                        e.dma_start(out=ap_out[:, :], in_=sap).then_inc(SEM["sDBG"], 16)
                    SP.mark("sDBG", "dbg")
                    W_(e, "sDBG", "dbg")
                return
            W_(e, "sOS", "oscr_done")
            W_(e, "sOS1", "oscr_done")
            e.collective_compute("AllGather", Alu.bypass,
                                 replica_groups=[list(range(NCORE))],
                                 ins=[o_scr[:, :]], outs=[o_go[:, :]]).then_inc(sGP, 1)
            e.collective_compute("AllGather", Alu.bypass,
                                 replica_groups=[list(range(NCORE))],
                                 ins=[o_scr_lo[:, :]], outs=[o_go_lo[:, :]]).then_inc(sGP, 1)
            SP.mark("sGP", "og")
            e.wait_ge(sGP, SP.at("sGP", "og"))
            e.wait_ge(sT, SP.at("sT", f"pv_{NKT - 1}"))
            vgo = o_go.rearrange("a b -> (a b)").rearrange("(n r) -> n r", r=512)
            vgol = o_go_lo.rearrange("a b -> (a b)").rearrange("(n r) -> n r", r=512)
            for bbp in range(B):
                if bbp == 1:
                    e.wait_ge(sT, SP.at("sT", "pmw_255"))
                for ch in range(32):
                    icx = 1024 + 8 * NIDX + bbp * 32 + ch
                    e.indirect_dma_start(
                        out=A.bitcast(BF16)[:, ch * 512:(ch + 1) * 512],
                        out_offset=None, in_=vgo[:, :],
                        in_offset=bass.IndirectOffsetOnAxis(
                            ap=itmp[:, icx:icx + 1], axis=0)).then_inc(SEM[f"sKV{bbp}"], 16)
                    e.indirect_dma_start(
                        out=A.bitcast(BF16)[:, 16384 + ch * 512:16384 + (ch + 1) * 512],
                        out_offset=None, in_=vgol[:, :],
                        in_offset=bass.IndirectOffsetOnAxis(
                            ap=itmp[:, icx:icx + 1], axis=0)).then_inc(SEM[f"sKV{bbp}"], 16)
                SP.mark(f"sKV{bbp}", f"old_{bbp}")
            if dbg:
                wait_final(e)
                SP.mark("sGP", "gpend")
                e.wait_ge(sGP, SP.at("sGP", "gpend"))
                for name, ap_out in dbg.items():
                    sap = A[:, 0:24576] if name == "p3kvq" else (
                        A[:, 24576:40960] if name == "p3tail" else src_map[name][:, :])
                    e.dma_start(out=ap_out[:, :], in_=sap).then_inc(SEM["sDBG"], 16)
                SP.mark("sDBG", "dbg")
                W_(e, "sDBG", "dbg")

        def emit_scalar(e):
            for i in range(128):
                rt, fc = divmod(i, NF)
                e.wait_ge(sV, SP.at("sV", f"tev_{i}"))
                e.dma_start(out=bass.AP(tensor=g1_in.tensor,
                                        offset=fc * 128 * 512 + rt * 128,
                                        ap=[[512, 128], [1, 128]]),
                            in_=wtr[:, (i % 2) * 128:(i % 2) * 128 + 128]).then_inc(SEM[f"sWT{i % 2}"], 16)
                SP.mark(f"sWT{i % 2}", f"wqst_{i}")
            e.wait_ge(sV, SP.at("sV", "wscale"))
            for rt in range(4):
                e.dma_start(out=bass.AP(tensor=g1s_in.tensor, offset=rt * 128,
                                        ap=[[1, 128], [1, 1]]),
                            in_=col(CL_WS + rt)).then_inc(SEM["sG1S"], 16)
            SP.mark("sG1S", "g1ready")
            e.wait_ge(sV, SP.at("sV", "htab"))
            e.dma_start(out=htab[:, :], in_=av(TB_V, TABW)[0:32, :]).then_inc(SEM["sHT"], 16)
            SP.mark("sHT", "htabst")
            for b in range(B):
                for tt in range(4):
                    e.wait_ge(sV, SP.at("sV", f"var_{b}_{tt}"))
                    e.activation(out=col(CL_R), in_=col(CL_T0), func=Act.Sqrt).then_inc(sS, 1)
                    SP.mark("sS", f"sqrt_{b}_{tt}")
                for ui, (kind, g) in enumerate(PUNITS):
                    for q4 in range(4):
                        e.wait_ge(sV, SP.at("sV", f"ev_{b}_{ui}_{q4}"))
                        src = av(O_EV + (ui % 2) * 2048 + q4 * 512, 512)
                        if kind in ("K", "Q"):
                            hh = g * 4 + q4
                            part = 0 if kind == "K" else 2
                            e.dma_start(out=bass.AP(tensor=a1_in.tensor,
                                                    offset=a1_off(hh // 4, hh % 4, b, part),
                                                    ap=[[512, 128], [1, 512]]),
                                        in_=src).then_inc(SEM[f"sST{ui % 2}"], 16)
                        else:
                            tt = q4
                            for h4 in range(4):
                                hh = g * 4 + h4
                                e.dma_start(out=bass.AP(tensor=a1_in.tensor,
                                                        offset=a1_off(hh // 4, hh % 4, b, 1) + tt * 128 * Dh,
                                                        ap=[[Dh, 128], [1, 128]]),
                                            in_=src[:, h4 * 128:(h4 + 1) * 128]).then_inc(SEM[f"sST{ui % 2}"], 16)
                        SP.mark(f"sST{ui % 2}", f"st_{b}_{ui}_{q4}")
                for jc in range(8):
                    for tt in range(4):
                        e.wait_ge(sV, SP.at("sV", f"fev_{b}_{jc}_{tt}"))
                        e.dma_start(out=ffnscr[b * 512 + tt * 128: b * 512 + (tt + 1) * 128,
                                               jc * 512:(jc + 1) * 512],
                                    in_=av(O_EV + (jc % 2) * 2048 + tt * 512, 512)
                                    ).then_inc(SEM[f"sST{jc % 2}"], 16)
                        SP.mark(f"sST{jc % 2}", f"fst_{b}_{jc}_{tt}")
            if phases < 3:
                mark_final()
                return
            pend_ostu = []

            def ostu_op(u, hl, s, bb):
                e.wait_ge(sV, SP.at("sV", f"osp_{u}"))
                off = (hl * B + bb) * Dh * S + s * 512
                oeo = P3_OE + (u % 2) * 1024
                osem = "sOS" if u % 2 == 0 else "sOS1"
                e.dma_start(out=bass.AP(tensor=o_scr.tensor, offset=off,
                                        ap=[[S, 128], [1, 512]]),
                            in_=avb(oeo + 512, 512)).then_inc(SEM[osem], 16)
                e.dma_start(out=bass.AP(tensor=o_scr_lo.tensor, offset=off,
                                        ap=[[S, 128], [1, 512]]),
                            in_=avb(oeo + 768, 512)).then_inc(SEM[osem], 16)
                SP.mark(osem, f"ostu_{u}")

            for n, (hl, s, bb, ki) in enumerate(KT):
                while pend_ostu and n >= pend_ostu[0][4] + 8:
                    ostu_op(*pend_ostu.pop(0)[:4])
                e.wait_ge(sV, SP.at("sV", f"sb_{n}"))
                if n >= 6:
                    e.wait_ge(sV, SP.at("sV", f"da_{n - 6}"))
                    e.wait_ge(sT, SP.at("sT", f"pv_{n - 6}"))
                e.activation(out=av(P3_P + (n % 6) * 512, 512),
                             in_=av(P3_SB + (n % 2) * 512, 512),
                             func=Act.Exp).then_inc(sS, 1)
                SP.mark("sS", f"exp_{n}")
                if ki == NKS[s] - 1:
                    pend_ostu.append((UNIT_OF[(hl, s, bb)], hl, s, bb, n))
            for it in pend_ostu:
                ostu_op(*it[:4])
            SP.mark("sOS", "oscr_done")
            SP.mark("sOS1", "oscr_done")
            if phases < 4:
                mark_final()
                return
            for i in range(64):
                bbp, r8 = divmod(i, 32)
                mc, tt = divmod(r8, 4)
                row0 = bbp * 512 + tt * 128
                e.wait_ge(sV, SP.at("sV", f"pe_{i}"))
                e.dma_start(out=out[row0:row0 + 128, mc * 512:(mc + 1) * 512],
                            in_=av(P4_EV + (i % 4) * 512, 512)).then_inc(SEM[f"sY{i % 4}"], 16)
                SP.mark(f"sY{i % 4}", f"p4st_{i}")
            mark_final()

        def fp16_round(e, v):
            t1 = itmp[:, 0:512]
            t1f = itmp.bitcast(F32)[:, 0:512]
            t2 = itmp[:, 512:1024]
            t2f = itmp.bitcast(F32)[:, 512:1024]
            e.tensor_scalar(out=t1, in0=v.bitcast(I32), scalar1=23, scalar2=255,
                            op0=Alu.logical_shift_right, op1=Alu.bitwise_and)
            e.tensor_scalar(out=t2, in0=t1, scalar1=-1, scalar2=150, op0=Alu.mult, op1=Alu.add)
            e.tensor_scalar(out=t2, in0=t2, scalar1=23, scalar2=None, op0=Alu.logical_shift_left)
            e.tensor_tensor(out=v, in0=v, in1=t2f, op=Alu.mult)
            e.tensor_scalar(out=v, in0=v, scalar1=col(CL_C52), scalar2=None, op0=Alu.add)
            e.tensor_scalar(out=v, in0=v, scalar1=col(CL_C52), scalar2=None, op0=Alu.subtract)
            e.tensor_scalar(out=t2, in0=t1, scalar1=104, scalar2=None, op0=Alu.add)
            e.tensor_scalar(out=t2, in0=t2, scalar1=23, scalar2=None, op0=Alu.logical_shift_left)
            e.tensor_tensor(out=v, in0=v, in1=t2f, op=Alu.mult)
            e.tensor_scalar(out=t1, in0=v.bitcast(I32), scalar1=23, scalar2=255,
                            op0=Alu.logical_shift_right, op1=Alu.bitwise_and)
            e.tensor_scalar(out=t1, in0=t1, scalar1=10, scalar2=103,
                            op0=Alu.subtract, op1=Alu.max)
            e.tensor_scalar(out=t2, in0=t1, scalar1=-1, scalar2=254, op0=Alu.mult, op1=Alu.add)
            e.tensor_scalar(out=t2, in0=t2, scalar1=23, scalar2=None, op0=Alu.logical_shift_left)
            e.tensor_scalar(out=t1, in0=t1, scalar1=23, scalar2=None, op0=Alu.logical_shift_left)
            e.tensor_tensor(out=v, in0=v, in1=t2f, op=Alu.mult)
            e.tensor_scalar(out=v, in0=v, scalar1=col(CL_CALL), scalar2=None, op0=Alu.add)
            e.tensor_scalar(out=v, in0=v, scalar1=col(CL_CALL), scalar2=None, op0=Alu.subtract)
            e.tensor_tensor(out=v, in0=v, in1=t1f, op=Alu.mult)

        def emit_vector(e):
            W_(e, "sP0", "p0")
            e.tensor_scalar(out=col(CL_FLAG), in0=onescol, scalar1=float(2.0 ** -30),
                            scalar2=1.0, op0=Alu.add, op1=Alu.is_gt)
            e.tensor_scalar(out=col(CL_C52), in0=col(CL_FLAG), scalar1=float(C52),
                            scalar2=None, op0=Alu.mult)
            e.tensor_scalar(out=col(CL_CALL), in0=col(CL_FLAG), scalar1=-1.0,
                            scalar2=1.0, op0=Alu.mult, op1=Alu.add)
            e.tensor_tensor(out=col(CL_CALL), in0=col(CL_CALL), in1=cst[:, 129:130], op=Alu.mult)
            e.tensor_tensor(out=col(CL_CALL), in0=col(CL_CALL), in1=col(CL_C52),
                            op=Alu.add)
            e.memset(av(P3_ONE, 128)[0:1, :], 1.0).then_inc(sV, 1)
            SP.mark("sV", "setup")
            e.wait_ge(sV, SP.at("sV", "setup"))
            W_(e, "sP0", "p0")
            for rt in range(4):
                e.tensor_reduce(out=col(CL_WS + rt), in_=av(O_XT + rt * 4096, 4096),
                                axis=AX, op=Alu.max, apply_absolute_value=True)
            e.tensor_scalar(out=cols[:, CL_WS:CL_WS + 4], in0=cols[:, CL_WS:CL_WS + 4],
                            scalar1=float(1.0 / QMAX), scalar2=None, op0=Alu.mult).then_inc(sV, 1)
            SP.mark("sV", "wscale")
            e.wait_ge(sV, SP.at("sV", "wscale"))
            e.reciprocal(out=cols[:, CL_WR:CL_WR + 4], in_=cols[:, CL_WS:CL_WS + 4]).then_inc(sV, 1)
            SP.mark("sV", "wsrec")
            e.wait_ge(sV, SP.at("sV", "wsrec"))
            for rt in range(4):
                wt = av(O_XT + rt * 4096, 4096)
                e.tensor_scalar(out=wt, in0=wt, scalar1=col(CL_WR + rt), scalar2=None, op0=Alu.mult)
                e.tensor_scalar(out=wt, in0=wt, scalar1=col(CL_CALL), scalar2=None, op0=Alu.add)
                e.tensor_scalar(out=wt, in0=wt, scalar1=col(CL_CALL), scalar2=None,
                                op0=Alu.subtract).then_inc(sV, 1)
            SP.mark("sV", "wqvals")
            e.wait_ge(sGP, SP.at("sGP", "iota"))
            td = av(TB_D, TABW)[0:32, :]
            tv = av(TB_V, TABW)[0:32, :]
            e.tensor_scalar(out=td, in0=td, scalar1=cst[0:32, 131:132], scalar2=None,
                            op0=Alu.subtract)
            e.tensor_scalar(out=tv, in0=td, scalar1=cst[0:32, 130:131], scalar2=None,
                            op0=Alu.mult).then_inc(sV, 1)
            SP.mark("sV", "rawbias")
            e.wait_ge(sV, SP.at("sV", "rawbias"))
            ib = A.bitcast(I32)[0:32, ITAB:ITAB + TABW]
            ibf = av(ITAB, TABW)[0:32, :]
            e.tensor_scalar(out=ib, in0=tv.bitcast(I32), scalar1=23, scalar2=255,
                            op0=Alu.logical_shift_right, op1=Alu.bitwise_and)
            e.tensor_scalar(out=ib, in0=ib, scalar1=-1, scalar2=150, op0=Alu.mult, op1=Alu.add)
            e.tensor_scalar(out=ib, in0=ib, scalar1=23, scalar2=None, op0=Alu.logical_shift_left)
            e.tensor_tensor(out=tv, in0=tv, in1=ibf, op=Alu.mult)
            e.tensor_scalar(out=tv, in0=tv, scalar1=col(CL_C52)[0:32, :], scalar2=None, op0=Alu.add)
            e.tensor_scalar(out=tv, in0=tv, scalar1=col(CL_C52)[0:32, :], scalar2=None,
                            op0=Alu.subtract)
            e.tensor_scalar(out=ib, in0=ib, scalar1=23, scalar2=None, op0=Alu.logical_shift_right)
            e.tensor_scalar(out=ib, in0=ib, scalar1=-1, scalar2=254, op0=Alu.mult, op1=Alu.add)
            e.tensor_scalar(out=ib, in0=ib, scalar1=23, scalar2=None, op0=Alu.logical_shift_left)
            e.tensor_tensor(out=tv, in0=tv, in1=ibf, op=Alu.mult)
            e.tensor_scalar(out=tv, in0=tv, scalar1=-M_BOUND, scalar2=None, op0=Alu.add)
            e.tensor_scalar(out=td, in0=td, scalar1=0.0, scalar2=None, op0=Alu.is_gt)
            e.scalar_tensor_tensor(out=tv, in0=td, scalar=BIG_NEG, in1=tv,
                                   op0=Alu.mult, op1=Alu.add).then_inc(sV, 1)
            SP.mark("sV", "htab")
            for g, job in enumerate(TRJOBS):
                if job[0] == "wq":
                    e.wait_ge(sT, SP.at("sT", f"ttr_{g}"))
                    if g >= 2:
                        W_(e, f"sWT{g % 2}", f"wqst_{g - 2}")
                    e.tensor_copy(out=wtr[:, (g % 2) * 128:(g % 2) * 128 + 128],
                                  in_=psc[g % 2][:, 0:128]).then_inc(sV, 1)
                    SP.mark("sV", f"tev_{g}")
                elif job[0] == "xn":
                    _, b, tt, fc = job
                    if fc == 0:
                        emit_ln_tile(e, b, tt)
                    e.wait_ge(sT, SP.at("sT", f"ttr_{g}"))
                    if g == 128:
                        W_(e, "sHT", "htabst")
                    hi = xnt_hi(fc, tt * 128, 128)
                    lo = xnt_lo(fc, tt * 128, 128)
                    e.tensor_copy(out=hi, in_=psc[g % 2][:, 0:128])
                    e.tensor_tensor(out=lo, in0=psc[g % 2][:, 0:128], in1=hi,
                                    op=Alu.subtract).then_inc(sV, 1)
                    SP.mark("sV", f"tev_{g}")
                else:
                    _, b, tt, fc = job
                    e.wait_ge(sT, SP.at("sT", f"ttr_{g}"))
                    e.tensor_copy(out=xqt(fc, tt), in_=psc[g % 2][:, 0:128]).then_inc(sV, 1)
                    SP.mark("sV", f"tev_{g}")
                if job[0] == "xq" and job[2] == 3 and job[3] == NF - 1:
                    b = job[1]
                    emit_p1_proj_vector(e, b)
                    emit_ffn_vector(e, b)
            SP.mark("sV", "r1free")
            if phases >= 3:
                emit_p3_vector(e)
            if phases >= 4:
                emit_p4_vector(e)

        def emit_ln_tile(e, b, tt):
            it = b * 4 + tt
            xt = av(O_XT + (it % 2) * 4096, 4096)
            xc = av(O_XC, 4096)
            W_(e, f"sX{(b * 4 + tt) % 2}", f"x_{b}_{tt}")
            e.reduce_sum(out=col(CL_M), in_=xt, axis=AX)
            e.tensor_scalar(out=col(CL_M), in0=col(CL_M), scalar1=float(1.0 / DM),
                            scalar2=None, op0=Alu.mult).then_inc(sV, 1)
            SP.mark("sV", f"mean_{b}_{tt}")
            e.wait_ge(sV, SP.at("sV", f"mean_{b}_{tt}"))
            if it > 0:
                pb_, pt = divmod(it - 1, 4)
                e.wait_ge(sT, SP.at("sT", f"trn_{pb_}_{pt}"))
            e.tensor_scalar(out=xc, in0=xt, scalar1=col(CL_M), scalar2=None, op0=Alu.subtract)
            e.tensor_tensor(out=xt, in0=xc, in1=xc, op=Alu.mult)
            e.reduce_sum(out=col(CL_T0), in_=xt, axis=AX)
            e.tensor_scalar(out=col(CL_T0), in0=col(CL_T0), scalar1=float(1.0 / DM),
                            scalar2=float(EPS), op0=Alu.mult, op1=Alu.add).then_inc(sV, 1)
            SP.mark("sV", f"var_{b}_{tt}")
            e.wait_ge(sS, SP.at("sS", f"sqrt_{b}_{tt}"))
            e.reciprocal(out=col(CL_R2), in_=col(CL_R)).then_inc(sV, 1)
            SP.mark("sV", f"rec_{b}_{tt}")
            e.wait_ge(sV, SP.at("sV", f"rec_{b}_{tt}"))
            e.tensor_scalar(out=xc, in0=xc, scalar1=col(CL_R2), scalar2=None,
                            op0=Alu.mult).then_inc(sV, 1)
            SP.mark("sV", f"xn_{b}_{tt}")
            e.tensor_reduce(out=col(CL_RA), in_=xc, axis=AX, op=Alu.max,
                            apply_absolute_value=True)
            e.tensor_scalar(out=col(CL_AS + it), in0=col(CL_RA),
                            scalar1=float(1.0 / QMAX), scalar2=None,
                            op0=Alu.mult).then_inc(sV, 1)
            SP.mark("sV", f"amax_{b}_{tt}")
            e.wait_ge(sV, SP.at("sV", f"amax_{b}_{tt}"))
            e.reciprocal(out=col(CL_RA), in_=col(CL_RA))
            e.tensor_scalar(out=col(CL_RA), in0=col(CL_RA), scalar1=float(QMAX),
                            scalar2=None, op0=Alu.mult).then_inc(sV, 1)
            SP.mark("sV", f"ra_{b}_{tt}")
            e.wait_ge(sV, SP.at("sV", f"ra_{b}_{tt}"))
            e.tensor_scalar(out=xt, in0=xc, scalar1=col(CL_RA), scalar2=None, op0=Alu.mult)
            e.tensor_scalar(out=xt, in0=xt, scalar1=col(CL_CALL), scalar2=None, op0=Alu.add)
            e.tensor_scalar(out=xt, in0=xt, scalar1=col(CL_CALL), scalar2=None,
                            op0=Alu.subtract).then_inc(sV, 1)
            SP.mark("sV", f"xq_{b}_{tt}")

        def emit_p1_proj_vector(e, b):
            for ui, (kind, g) in enumerate(PUNITS):
                for fc in range(NF):
                    i = (b * 24 + ui) * NF + fc
                    W_(e, f"sW{i % 2}", f"w_{b}_{ui}_{fc}")
                    if i >= 2:
                        j = i - 2
                        pb2, rem2 = divmod(j, 24 * NF)
                        pui2, pfc2 = divmod(rem2, NF)
                        e.wait_ge(sT, SP.at("sT", f"pmm_{pb2}_{pui2}_{pfc2}"))
                    wb = wbuf[:, (i % 2) * 512:(i % 2 + 1) * 512]
                    wh = whl[:, (i % 2) * 512:(i % 2 + 1) * 512]
                    wl = whl[:, 1024 + (i % 2) * 512:1024 + (i % 2 + 1) * 512]
                    e.tensor_copy(out=wh, in_=wb)
                    e.tensor_tensor(out=wl, in0=wb, in1=wh, op=Alu.subtract).then_inc(sV, 1)
                    SP.mark("sV", f"wsp_{b}_{ui}_{fc}")
                e.wait_ge(sT, SP.at("sT", f"grp_{b}_{ui}"))
                if kind == "V":
                    W_(e, "sP0", "p0")
                qi = QIDX[("proj", b, ui)]
                if qi >= 1:
                    pass
                evi = b * 32 + ui
                if evi >= 2:
                    prev = evi - 2
                    pb_, pui = divmod(prev, 32)
                    if pui < 24:
                        W_(e, f"sST{pui % 2}", f"st_{pb_}_{pui}_3")
                    else:
                        W_(e, f"sST{pui % 2}", f"fst_{pb_}_{pui - 24}_3")
                for q4 in range(4):
                    dv = av(O_EV + (ui % 2) * 2048 + q4 * 512, 512)
                    if kind in ("K", "Q"):
                        hh = g * 4 + q4
                        bci = {"Q": hh, "K": 32 + hh}[kind]
                        e.tensor_scalar(out=dv, in0=quad[q4][:],
                                        scalar1=bcol[:, bci:bci + 1], scalar2=None,
                                        op0=Alu.add).then_inc(sV, 1)
                    else:
                        e.tensor_tensor(out=dv, in0=quad[q4][:],
                                        in1=av(O_R1, 4096)[:, g * 512:(g + 1) * 512],
                                        op=Alu.add).then_inc(sV, 1)
                    SP.mark("sV", f"ev_{b}_{ui}_{q4}")

        def emit_ffn_vector(e, b):
            for jc in range(8):
                e.wait_ge(sT, SP.at("sT", f"fgrp_{b}_{jc}"))
                W_(e, f"sWX{(b * 8 + jc) % 2}", f"wsx_{b}_{jc}")
                evi = b * 32 + 24 + jc
                prev = evi - 2
                pb_, pui = divmod(prev, 32)
                if pui < 24:
                    W_(e, f"sST{pui % 2}", f"st_{pb_}_{pui}_3")
                else:
                    W_(e, f"sST{pui % 2}", f"fst_{pb_}_{pui - 24}_3")
                for tt in range(4):
                    ev = av(O_EV + (jc % 2) * 2048 + tt * 512, 512)
                    e.tensor_scalar(out=ev, in0=quad[tt][:],
                                    scalar1=col(CL_AS + b * 4 + tt), scalar2=None, op0=Alu.mult)
                    e.tensor_tensor(out=ev, in0=ev,
                                    in1=av(O_WSX + ((b * 8 + jc) % 2) * 512, 512), op=Alu.mult)
                    fp16_round(e, ev)
                    e.tensor_scalar(out=ev, in0=ev, scalar1=1.0, scalar2=None,
                                    op0=Alu.mult).then_inc(sV, 1)
                    SP.mark("sV", f"fev_{b}_{jc}_{tt}")
                SP.mark("sV", f"fevl_{b}_{jc}")

        def emit_p3_vector(e):
            def da_op(m):
                hl, s, b, ki = KT[m]
                u = UNIT_OF[(hl, s, b)]
                e.wait_ge(sS, SP.at("sS", f"exp_{m}"))
                if ki == 0:
                    if u >= 1:
                        e.wait_ge(sT, SP.at("sT", f"dmm_{u - 1}"))
                    e.tensor_copy(out=av(P3_DA, 512),
                                  in_=av(P3_P + (m % 6) * 512, 512)).then_inc(sV, 1)
                else:
                    e.tensor_tensor(out=av(P3_DA, 512), in0=av(P3_DA, 512),
                                    in1=av(P3_P + (m % 6) * 512, 512), op=Alu.add).then_inc(sV, 1)
                SP.mark("sV", f"da_{m}")
                if ki == NKS[s] - 1:
                    unit_tail(u)

            def unit_tail(u):
                e.wait_ge(sT, SP.at("sT", f"dmm_{u}"))
                e.reciprocal(out=av(P3_RD, 512)[0:1, :], in_=pd[0:1, :]).then_inc(sV, 1)
                SP.mark("sV", f"rdr_{u}")
                e.wait_ge(sT, SP.at("sT", f"bmm_{u}"))
                e.tensor_copy(out=av(P3_RD, 512), in_=pd[:]).then_inc(sV, 1)
                SP.mark("sV", f"rdb_{u}")
                if u >= 2:
                    W_(e, "sOS" if u % 2 == 0 else "sOS1", f"ostu_{u - 2}")
                elif u == 0:
                    W_(e, "sST0", "fst_1_6_3")
                    W_(e, "sST1", "fst_1_7_3")
                oeo = P3_OE + (u % 2) * 1024
                oe = av(oeo, 512)
                e.tensor_tensor(out=oe, in0=po[:], in1=av(P3_RD, 512), op=Alu.mult)
                e.tensor_copy(out=avb(oeo + 512, 512), in_=oe)
                e.tensor_tensor(out=avb(oeo + 768, 512), in0=oe,
                                in1=avb(oeo + 512, 512), op=Alu.subtract).then_inc(sV, 1)
                SP.mark("sV", f"osp_{u}")

            for n, (hl, s, b, ki) in enumerate(KT):
                if ki == 0 and b == 0:
                    W_(e, f"sAS{s % 2}", f"astrip_{hl}_{s}")
                if n == 0:
                    W_(e, "sST0", "fst_1_6_3")
                    W_(e, "sST1", "fst_1_7_3")
                e.wait_ge(sT, SP.at("sT", f"smm_{n}"))
                if n >= 2:
                    e.wait_ge(sS, SP.at("sS", f"exp_{n - 2}"))
                astrip = av(P3_A + (s % 2) * AW, AW)
                u0 = ki * 128 + 511
                slv = astrip[:, 511::-1] if u0 == 511 else astrip[:, u0:u0 - 512:-1]
                e.scalar_tensor_tensor(out=av(P3_SB + (n % 2) * 512, 512),
                                       in0=p3pool[n % 6][:], scalar=float(SCALE),
                                       in1=slv, op0=Alu.mult, op1=Alu.add).then_inc(sV, 1)
                SP.mark("sV", f"sb_{n}")
                if n >= 1:
                    da_op(n - 1)
            da_op(NKT - 1)

        def emit_p4_vector(e):
            for bbp in range(B):
                for mc in range(8):
                    for ch in range(32):
                        i = (bbp * 8 + mc) * 32 + ch
                        W_(e, f"sWP{i % 2}", f"wp_{i}")
                        if i >= 2:
                            e.wait_ge(sT, SP.at("sT", f"pmw_{i - 2}"))
                        wb = wbuf[:, (i % 2) * 512:(i % 2 + 1) * 512]
                        wh = whl[:, (i % 2) * 512:(i % 2 + 1) * 512]
                        wl = whl[:, 1024 + (i % 2) * 512:1024 + (i % 2 + 1) * 512]
                        e.tensor_copy(out=wh, in_=wb)
                        e.tensor_tensor(out=wl, in0=wb, in1=wh, op=Alu.subtract).then_inc(sV, 1)
                        SP.mark("sV", f"wpsp_{i}")
                    for tt in range(4):
                        iv = (bbp * 8 + mc) * 4 + tt
                        e.wait_ge(sT, SP.at("sT", f"ymm_{iv}"))
                        W_(e, f"sP5{iv % 2}", f"p4in_{iv}")
                        W_(e, "sBP", "bproj")
                        if iv >= 4:
                            W_(e, f"sY{iv % 4}", f"p4st_{iv - 4}")
                        ev = av(P4_EV + (iv % 4) * 512, 512)
                        e.tensor_tensor(out=ev, in0=quad[tt][:],
                                        in1=av(O_R1, 4096)[:, mc * 512:(mc + 1) * 512], op=Alu.add)
                        e.tensor_tensor(out=ev, in0=av(P5_X + (iv % 2) * 512, 512), in1=ev, op=Alu.add)
                        e.tensor_tensor(out=ev, in0=ev, in1=av(P5_F + (iv % 2) * 512, 512),
                                        op=Alu.add).then_inc(sV, 1)
                        SP.mark("sV", f"pe_{iv}")

        def emit_tensor(e):
            e.wait_ge(sV, SP.at("sV", "wqvals"))
            for g, job in enumerate(TRJOBS):
                if g >= 2:
                    e.wait_ge(sV, SP.at("sV", f"tev_{g - 2}"))
                if job[0] == "wq":
                    rt, fc = job[1], job[2]
                    src = av(O_XT + rt * 4096, 4096)[:, fc * 128:(fc + 1) * 128]
                elif job[0] == "xn":
                    _, b, tt, fc = job
                    if fc == 0:
                        e.wait_ge(sV, SP.at("sV", f"xn_{b}_{tt}"))
                    src = av(O_XC, 4096)[:, fc * 128:(fc + 1) * 128]
                else:
                    _, b, tt, fc = job
                    it = b * 4 + tt
                    if fc == 0:
                        e.wait_ge(sV, SP.at("sV", f"xq_{b}_{tt}"))
                    src = av(O_XT + (it % 2) * 4096, 4096)[:, fc * 128:(fc + 1) * 128]
                e.transpose(psc[g % 2][:, 0:128], src, ident).then_inc(sT, 1)
                SP.mark("sT", f"ttr_{g}")
                if job[0] == "xn" and job[3] == NF - 1:
                    SP.mark("sT", f"trn_{job[1]}_{job[2]}")
                if job[0] == "xq" and job[3] == NF - 1:
                    SP.mark("sT", f"trq_{job[1]}_{job[2]}")
                    if job[2] == 3:
                        b = job[1]
                        emit_p1_proj_tensor(e, b)
                        emit_ffn_tensor(e, b)
            if phases >= 3:
                emit_p3_tensor(e)
            if phases >= 4:
                emit_p4_tensor(e)

        def emit_p1_proj_tensor(e, b):
            for ui, (kind, g) in enumerate(PUNITS):
                qi = QIDX[("proj", b, ui)]
                if qi >= 1:
                    pu = QUSERS[qi - 1]
                    wait_quad_evacs(e, pu)
                for fc in range(NF):
                    i = (b * 24 + ui) * NF + fc
                    e.wait_ge(sV, SP.at("sV", f"wsp_{b}_{ui}_{fc}"))
                    wh = whl[:, (i % 2) * 512:(i % 2 + 1) * 512]
                    wl = whl[:, 1024 + (i % 2) * 512:1024 + (i % 2 + 1) * 512]
                    first, last = fc == 0, fc == NF - 1
                    if kind in ("K", "Q"):
                        rh = xnt_hi(fc)
                        rl = xnt_lo(fc)
                        for q4 in range(4):
                            lwh = wh[:, q4 * 128:(q4 + 1) * 128]
                            lwl = wl[:, q4 * 128:(q4 + 1) * 128]
                            e.matmul(quad[q4][:], lwh, rh, start=first, stop=False)
                            e.matmul(quad[q4][:], lwh, rl, start=False, stop=False)
                            mm = e.matmul(quad[q4][:], lwl, rh, start=False, stop=last)
                            if q4 == 3:
                                mm.then_inc(sT, 1)
                    else:
                        for q4 in range(4):
                            lh = xnt_hi(fc, q4 * 128, 128)
                            ll = xnt_lo(fc, q4 * 128, 128)
                            e.matmul(quad[q4][:], lh, wh, start=first, stop=False)
                            e.matmul(quad[q4][:], lh, wl, start=False, stop=False)
                            mm = e.matmul(quad[q4][:], ll, wh, start=False, stop=last)
                            if q4 == 3:
                                mm.then_inc(sT, 1)
                    SP.mark("sT", f"pmm_{b}_{ui}_{fc}")
                SP.mark("sT", f"grp_{b}_{ui}")

        def wait_quad_evacs(e, user):
            tag = user[0]
            if tag == "proj":
                _, b, ui = user
                for q4 in range(4):
                    e.wait_ge(sV, SP.at("sV", f"ev_{b}_{ui}_{q4}"))
            elif tag == "ffn":
                _, b, jc = user
                for q4 in range(4):
                    e.wait_ge(sV, SP.at("sV", f"fev_{b}_{jc}_{q4}"))
            else:
                _, si, mc = user
                for q4 in range(4):
                    e.wait_ge(sV, SP.at("sV", f"yev_{si}_{mc}_{q4}"))

        def emit_ffn_tensor(e, b):
            for jc in range(8):
                qi = QIDX[("ffn", b, jc)]
                pu = QUSERS[qi - 1]
                wait_quad_evacs(e, pu)
                for fc in range(NF):
                    i = (b * 8 + jc) * NF + fc
                    W_(e, f"sWQ{i % 4}", f"wq_{b}_{jc}_{fc}")
                    rhsw = wqs[:, (i % 4) * 512:(i % 4 + 1) * 512]
                    first, last = fc == 0, fc == NF - 1
                    for q4 in range(4):
                        mm = e.matmul(quad[q4][:], xqt(fc, q4), rhsw, start=first, stop=last)
                    mm.then_inc(sT, 1)
                    SP.mark("sT", f"fmm_{b}_{jc}_{fc}")
                SP.mark("sT", f"fgrp_{b}_{jc}")

        def emit_p3_tensor(e):
            LB = 4

            def pv_op(m):
                hl, s, b, ki = KT[m]
                u = UNIT_OF[(hl, s, b)]
                nk = NKS[s]
                vb = av(P3_KVQ + b * 12288 + 4096, 4096)
                if ki == 0 and u >= 1:
                    e.wait_ge(sV, SP.at("sV", f"osp_{u - 1}"))
                e.wait_ge(sS, SP.at("sS", f"exp_{m}"))
                e.matmul(po[:], vb[:, ki * 128:(ki + 1) * 128],
                         av(P3_P + (m % 6) * 512, 512),
                         start=(ki == 0), stop=(ki == nk - 1)).then_inc(sT, 1)
                SP.mark("sT", f"pv_{m}")
                if ki == nk - 1:
                    e.wait_ge(sV, SP.at("sV", f"da_{m}"))
                    if u >= 1:
                        e.wait_ge(sV, SP.at("sV", f"rdb_{u - 1}"))
                    e.matmul(pd[0:1, :], onescol, av(P3_DA, 512),
                             start=True, stop=True).then_inc(sT, 1)
                    SP.mark("sT", f"dmm_{u}")
                    e.wait_ge(sV, SP.at("sV", f"rdr_{u}"))
                    e.matmul(pd[:], av(P3_ONE, 128)[0:1, :], av(P3_RD, 512)[0:1, :],
                             start=True, stop=True).then_inc(sT, 1)
                    SP.mark("sT", f"bmm_{u}")

            for n, (hl, s, b, ki) in enumerate(KT):
                if n - LB >= 0:
                    pv_op(n - LB)
                base = P3_KVQ + b * 12288
                kb = av(base, 4096)
                qs = av(base + 8192, 4096)[:, s * 512:(s + 1) * 512]
                if ki == 0:
                    W_(e, f"sKV{b}", f"kvq_{hl}_{b}")
                if n >= 6:
                    e.wait_ge(sV, SP.at("sV", f"sb_{n - 6}"))
                elif n < 2:
                    e.wait_ge(sV, SP.at("sV", f"tev_{NTR - 2 + n}"))
                else:
                    e.wait_ge(sV, SP.at("sV", f"fev_1_7_{n - 2}"))
                e.matmul(p3pool[n % 6][:], kb[:, ki * 128:(ki + 1) * 128], qs,
                         start=True, stop=True).then_inc(sT, 1)
                SP.mark("sT", f"smm_{n}")
            for m in range(NKT - LB, NKT):
                pv_op(m)

        def emit_p4_tensor(e):
            for bbp in range(B):
                W_(e, f"sKV{bbp}", f"old_{bbp}")
                for mc in range(8):
                    for ch in range(32):
                        i = (bbp * 8 + mc) * 32 + ch
                        e.wait_ge(sV, SP.at("sV", f"wpsp_{i}"))
                        if ch == 0:
                            for tt in range(4):
                                iv = (bbp * 8 + mc) * 4 + tt
                                if iv >= 4:
                                    e.wait_ge(sV, SP.at("sV", f"pe_{iv - 4}"))
                        wh = whl[:, (i % 2) * 512:(i % 2 + 1) * 512]
                        wl = whl[:, 1024 + (i % 2) * 512:1024 + (i % 2 + 1) * 512]
                        first, last = ch == 0, ch == 31
                        for tt in range(4):
                            co = ch * 512 + tt * 128
                            oh = A.bitcast(BF16)[:, co:co + 128]
                            ol = A.bitcast(BF16)[:, 16384 + co:16384 + co + 128]
                            e.matmul(quad[tt][:], oh, wh, start=first, stop=False)
                            e.matmul(quad[tt][:], oh, wl, start=False, stop=False)
                            mm = e.matmul(quad[tt][:], ol, wh, start=False, stop=last)
                            if last:
                                mm.then_inc(sT, 1)
                                SP.mark("sT", f"ymm_{(bbp * 8 + mc) * 4 + tt}")
                        if not last:
                            mm.then_inc(sT, 1)
                        SP.mark("sT", f"pmw_{i}")

        SP.begin(1)
        emit_sync(MockEng(SP, SEMS))
        emit_gpsimd(MockEng(SP, SEMS, auto="sGP"))
        emit_scalar(MockEng(SP, SEMS, auto="sS"))
        emit_vector(MockEng(SP, SEMS, auto="sV"))
        emit_tensor(MockEng(SP, SEMS))

        SP.begin(2)

        AUTOSEM = {"sGP": sGP, "sS": sS, "sV": sV}

        def wrap(eng, auto=None):
            ah = AUTOSEM.get(auto)
            state = {"pend": None}

            def resolve():
                p = state["pend"]
                if p is not None:
                    state["pend"] = None
                    if not p[1][0]:
                        SP.inc(auto, 1)
                        p[0].then_inc(ah, 1)

            class W:
                def __getattr__(self2, name):
                    if name == "wait_ge":
                        def wg(sem, val):
                            resolve()
                            if val > 0:
                                eng.wait_ge(sem, val)
                        return wg
                    meth = getattr(eng, name)

                    def call(*a, **k):
                        resolve()
                        if ah is not None and name not in _NOCHAIN:
                            cnt = SP.counts.get(auto, 0)
                            if cnt > 0:
                                eng.wait_ge(ah, cnt)
                        inst = meth(*a, **k)
                        had = [False]
                        if ah is not None and name not in _NOCHAIN:
                            state["pend"] = (inst, had)

                        class Chain:
                            def then_inc(self3, sem, n=1):
                                had[0] = True
                                SP.inc(SEMS[id(sem)], n)
                                inst.then_inc(sem, n)
                                return self3
                        return Chain()
                    return call
            return W()

        block.sync(lambda eng: emit_sync(wrap(eng)))
        block.gpsimd(lambda eng: emit_gpsimd(wrap(eng, "sGP")))
        block.scalar(lambda eng: emit_scalar(wrap(eng, "sS")))
        block.vector(lambda eng: emit_vector(wrap(eng, "sV")))
        block.tensor(lambda eng: emit_tensor(wrap(eng)))

    return nc


def kernel(L0_attn_hidden_states, L0_attn_w_attn, L0_attn_b_attn,
           L0_attn_w_proj, L0_attn_b_proj, L0_ffn_weight):
    x = np.ascontiguousarray(np.asarray(L0_attn_hidden_states, dtype=np.float32))
    w_attn = np.ascontiguousarray(np.asarray(L0_attn_w_attn, dtype=np.float32))
    b_attn = np.ascontiguousarray(np.asarray(L0_attn_b_attn, dtype=np.float32)).reshape(1, -1)
    w_proj = np.ascontiguousarray(np.asarray(L0_attn_w_proj, dtype=np.float32))
    b_proj = np.ascontiguousarray(np.asarray(L0_attn_b_proj, dtype=np.float32)).reshape(1, -1)
    w_ffn = np.ascontiguousarray(np.asarray(L0_ffn_weight, dtype=np.float32))

    nc = build_nc(tuple(DEBUG_OUTS), PHASES)
    in_maps = []
    for c in range(NCORE):
        xr = np.concatenate([x[0, 512 * c:512 * c + 512, :],
                             x[1, 512 * c:512 * c + 512, :]], axis=0)
        idx = np.zeros((128, 8 * NIDX + 64), dtype=np.int32)
        p = np.arange(128, dtype=np.int64)
        for bbp in range(B):
            for ch in range(32):
                idx[:, 8 * NIDX + bbp * 32 + ch] = \
                    (ch // 4) * 8192 + ((ch % 4) * B + bbp) * 1024 + c + 8 * p
        for hl in range(4):
            for b in range(B):
                u = hl * B + b
                for r in range(NCORE):
                    base = r * (NCORE * KVQ) + c * KVQ
                    idx[:, u * NIDX + r] = base // 512 + p
                    idx[:, u * NIDX + 8 + r] = (base + 2 * Dh * 512) // 512 + p
                    for kt in range(4):
                        idx[:, u * NIDX + 16 + r * 4 + kt] = \
                            (base + Dh * 512) // 128 + kt * 128 + p
        in_maps.append({
            "xrows": np.ascontiguousarray(xr),
            "w_attn": w_attn,
            "w_proj": w_proj,
            "w_ffn": np.ascontiguousarray(w_ffn[512 * c:512 * c + 512, :]),
            "b_attn": b_attn,
            "b_attn_t": np.ascontiguousarray(b_attn.reshape(96, 128).T),
            "b_proj": b_proj,
            "consts": _consts_block(c),
            "a1idx": idx,
        })
    res = run_bass_kernel_spmd(nc, in_maps, core_ids=list(range(NCORE)))
    outp = np.empty((2, 4096, 4096), dtype=np.float32)
    for c in range(NCORE):
        o = res.results[c]["out"]
        outp[0, 512 * c:512 * c + 512, :] = o[0:512]
        outp[1, 512 * c:512 * c + 512, :] = o[512:1024]
    if DEBUG_OUTS:
        kernel.debug = [{k: np.asarray(res.results[c]["dbg_" + k]) for k in DEBUG_OUTS}
                        for c in range(NCORE)]
    return outp


# version 05 (final)
# speedup vs baseline: 1.0423x; 1.0423x over previous
import math
import numpy as np
import ml_dtypes
import concourse.bass as bass
import concourse.mybir as mybir
from concourse.bass_utils import run_bass_kernel_spmd

F32 = mybir.dt.float32
BF16 = mybir.dt.bfloat16
I32 = mybir.dt.int32
Alu = mybir.AluOpType
Act = mybir.ActivationFunctionType
AX = mybir.AxisListType.X

B, S, H, Dh, DM = 2, 4096, 32, 128, 4096
NCORE = 8
NF = DM // 128
SCALE = 1.0 / Dh
M_BOUND = 16.0
QMAX = 127.0
EPS = 1e-5
BIG_NEG = -3.0e38
C23 = 1.5 * 2.0 ** 23
C52 = 1.5 * 2.0 ** 52

TABW = 4608
AW = 4480
KVQ = 3 * Dh * 512
A1BLK = 4 * B * KVQ
A1SZ = NCORE * A1BLK
NKS = [4 * (s + 1) for s in range(8)]
YCH = 8 * 1024 * 1024
YMC = 4 * 1024 * 1024
NIDX = 48

DEBUG_OUTS = []
PHASES = 5


def _slopes_f32():
    base = 2 ** (-(2 ** (-(math.log2(H) - 3))))
    b32 = float(np.float32(base))
    return np.float32(np.cumprod(np.full(H, b32, dtype=np.float64)))


def _consts_block(core_id):
    c = np.zeros((128, 132), dtype=np.float32)
    c[:, 0:128] = np.eye(128, dtype=np.float32)
    c[:, 128] = 1.0
    c[:, 129] = np.float32(C23)
    sl = _slopes_f32()
    for p in range(32):
        hl, s = p // 8, p % 8
        c[p, 130] = sl[4 * core_id + hl]
        c[p, 131] = np.float32(511 + s * 512)
    return c


class SemPlan:
    def __init__(self):
        self.counts = {}
        self.marks = {}
        self.pass_no = 0

    def begin(self, n):
        self.pass_no = n
        self.counts = {}

    def inc(self, name, n):
        self.counts[name] = self.counts.get(name, 0) + n
        return self.counts[name]

    def mark(self, name, label):
        key = (name, label)
        cur = self.counts.get(name, 0)
        if self.pass_no == 1:
            assert key not in self.marks, f"duplicate mark {key}"
            self.marks[key] = cur
        else:
            assert self.marks.get(key) == cur, (key, self.marks.get(key), cur)

    def at(self, name, label):
        if self.pass_no == 1:
            return 0
        return self.marks[(name, label)]


_NOCHAIN = ("dma_start", "indirect_dma_start", "collective_compute", "wait_ge")


class MockInst:
    def __init__(self, plan, semmap):
        self.plan = plan
        self.semmap = semmap
        self.had_inc = False

    def then_inc(self, sem, n=1):
        self.had_inc = True
        self.plan.inc(self.semmap[id(sem)], n)
        return self


class MockEng:

    def __init__(self, plan, semmap, auto=None):
        self._plan = plan
        self._semmap = semmap
        self._auto = auto
        self._pend = None

    def _resolve(self):
        if self._pend is not None:
            mi = self._pend
            self._pend = None
            if not mi.had_inc:
                self._plan.inc(self._auto, 1)

    def __getattr__(self, name):
        if name.startswith("_"):
            raise AttributeError(name)
        if name == "wait_ge":
            def wgm(*a, **k):
                self._resolve()
            return wgm

        def call(*a, **k):
            self._resolve()
            mi = MockInst(self._plan, self._semmap)
            if self._auto is not None and name not in _NOCHAIN:
                self._pend = mi
            return mi
        return call


TRJOBS = [("wq", i // NF, i % NF) for i in range(128)]
for _b in range(B):
    for _tt in range(4):
        TRJOBS += [("xn", _b, _tt, fc) for fc in range(NF)]
        TRJOBS += [("xq", _b, _tt, fc) for fc in range(NF)]
NTR = len(TRJOBS)

KT = []
UNIT_OF = {}
KT_LAST = {}
for _hl in range(4):
    for _s in range(8):
        for _b in range(B):
            UNIT_OF[(_hl, _s, _b)] = len(UNIT_OF)
            for _ki in range(NKS[_s]):
                KT.append((_hl, _s, _b, _ki))
                KT_LAST[(_hl, _s, _b)] = len(KT) - 1
NKT = len(KT)
UNITS_SEQ = list(UNIT_OF.keys())

QUSERS = []
for _b in range(B):
    QUSERS += [("proj", _b, ui) for ui in range(24)]
    QUSERS += [("ffn", _b, jc) for jc in range(8)]
QIDX = {u: i for i, u in enumerate(QUSERS)}

PUNITS = [("K", g) for g in range(8)] + [("V", g) for g in range(8)] + \
         [("Q", g) for g in range(8)]


def build_nc(debug_outs=(), phases=5):
    nc = bass.Bass("TRN2", target_bir_lowering=False)

    xrows = nc.dram_tensor("xrows", [1024, DM], F32, kind="ExternalInput").ap()
    w_attn = nc.dram_tensor("w_attn", [DM, 3 * DM], F32, kind="ExternalInput").ap()
    w_proj = nc.dram_tensor("w_proj", [DM, DM], F32, kind="ExternalInput").ap()
    w_ffn = nc.dram_tensor("w_ffn", [512, DM], F32, kind="ExternalInput").ap()
    b_attn = nc.dram_tensor("b_attn", [1, 3 * DM], F32, kind="ExternalInput").ap()
    b_attn_t = nc.dram_tensor("b_attn_t", [128, 96], F32, kind="ExternalInput").ap()
    b_proj = nc.dram_tensor("b_proj", [1, DM], F32, kind="ExternalInput").ap()
    consts = nc.dram_tensor("consts", [128, 132], F32, kind="ExternalInput").ap()
    out = nc.dram_tensor("out", [1024, DM], F32, kind="ExternalOutput").ap()

    a1_in = nc.dram_tensor("a1_in", [1, A1BLK * NCORE], F32, kind="Internal").ap()
    a1o = [nc.dram_tensor(f"a1o_{u}", [NCORE, NCORE * KVQ], F32, kind="Internal").ap()
           for u in range(8)]
    a1idx = nc.dram_tensor("a1idx", [128, 8 * NIDX + 64], I32, kind="ExternalInput").ap()
    o_scr = nc.dram_tensor("o_scr", [1, 4 * B * Dh * S], BF16, kind="Internal").ap()
    o_scr_lo = nc.dram_tensor("o_scr_lo", [1, 4 * B * Dh * S], BF16, kind="Internal").ap()
    o_gp = [nc.dram_tensor(f"o_gp_{h}", [NCORE, B * Dh * S], BF16, kind="Internal").ap()
            for h in range(4)]
    o_gpl = [nc.dram_tensor(f"o_gpl_{h}", [NCORE, B * Dh * S], BF16, kind="Internal").ap()
             for h in range(4)]
    g1_in = nc.dram_tensor("g1_in", [1, DM * 512], BF16, kind="Internal").ap()
    g1_out = nc.dram_tensor("g1_out", [NCORE, DM * 512], BF16, kind="Internal").ap()
    g1s_in = nc.dram_tensor("g1s_in", [1, 512], F32, kind="Internal").ap()
    g1s_out = nc.dram_tensor("g1s_out", [NCORE, 512], F32, kind="Internal").ap()
    htab = nc.dram_tensor("htab", [32, TABW], F32, kind="Internal").ap()
    ffnscr = nc.dram_tensor("ffnscr", [1024, DM], F32, kind="Internal").ap()

    dbg = {}
    shapes = {"a1_in": ([1, A1BLK * NCORE], F32),
              "htab": ([32, TABW], F32), "ffnscr": ([1024, DM], F32),
              "g1_out": ([NCORE, DM * 512], BF16),
              "g1s_out": ([NCORE, 512], F32),
              "o_scr": ([1, 4 * B * Dh * S], BF16),
              "o_scr_lo": ([1, 4 * B * Dh * S], BF16),
              "p3kvq": ([128, 24576], F32), "p3tail": ([128, 16384], F32)}
    src_map = {"a1_in": a1_in, "htab": htab, "ffnscr": ffnscr,
               "g1_out": g1_out, "g1s_out": g1s_out, "o_scr": o_scr,
               "o_scr_lo": o_scr_lo}
    for name in debug_outs:
        shp, dt = shapes[name]
        dbg[name] = nc.dram_tensor("dbg_" + name, shp, dt, kind="ExternalOutput").ap()

    import contextlib
    with contextlib.ExitStack() as _st:
        A = _st.enter_context(nc.sbuf_tensor("A", [128, 48128], F32))
        wbuf = _st.enter_context(nc.sbuf_tensor("wbuf", [128, 2 * 512], F32))
        whl = _st.enter_context(nc.sbuf_tensor("whl", [128, 4 * 512], BF16))
        cst = _st.enter_context(nc.sbuf_tensor("cst", [128, 132], F32))
        cols = _st.enter_context(nc.sbuf_tensor("cols", [128, 32], F32))
        bcol = _st.enter_context(nc.sbuf_tensor("bcol", [128, 96], F32))
        wqs = _st.enter_context(nc.sbuf_tensor("wqs", [128, 4 * 512], BF16))
        wtr = _st.enter_context(nc.sbuf_tensor("wtr", [128, 2 * 128], BF16))
        itmp = _st.enter_context(nc.sbuf_tensor("itmp", [128, 1536], I32))
        pq0 = _st.enter_context(nc.psum_tensor("pq0", [128, 512], F32))
        pq1 = _st.enter_context(nc.psum_tensor("pq1", [128, 512], F32))
        pq2 = _st.enter_context(nc.psum_tensor("pq2", [128, 512], F32))
        pq3 = _st.enter_context(nc.psum_tensor("pq3", [128, 512], F32))
        ps0 = _st.enter_context(nc.psum_tensor("ps0", [128, 512], F32))
        ps1 = _st.enter_context(nc.psum_tensor("ps1", [128, 512], F32))
        po = _st.enter_context(nc.psum_tensor("po", [128, 512], F32))
        pd = _st.enter_context(nc.psum_tensor("pd", [128, 512], F32))
        DMA_SEMS = ["sP0", "sX0", "sX1", "sW0", "sW1", "sWQ0", "sWQ1", "sWQ2", "sWQ3",
                    "sWX0", "sWX1", "sAS0", "sAS1", "sKV0", "sKV1", "sOL",
                    "sBP", "sWP0", "sWP1", "sP50", "sP51",
                    "sWT0", "sWT1", "sG1S", "sHT", "sST0", "sST1",
                    "sOS", "sOS1", "sY0", "sY1", "sY2", "sY3", "sO0", "sO1", "sDBG"]
        SEM = {nm: _st.enter_context(nc.semaphore(name=nm)) for nm in DMA_SEMS}
        sGP = _st.enter_context(nc.semaphore(name="sGP"))
        sV = _st.enter_context(nc.semaphore(name="sV"))
        sS = _st.enter_context(nc.semaphore(name="sS"))
        sT = _st.enter_context(nc.semaphore(name="sT"))
        block = _st.enter_context(nc.Block())
        quad = [pq0, pq1, pq2, pq3]
        psc = [ps0, ps1]
        p3pool = [ps0, ps1, pq0, pq1, pq2, pq3]
        SEMS = {id(h): nm for nm, h in SEM.items()}
        SEMS.update({id(sGP): "sGP", id(sV): "sV", id(sS): "sS", id(sT): "sT"})

        def W_(e, nm, label):
            e.wait_ge(SEM[nm], SP.at(nm, label))
        SP = SemPlan()

        STORE_SEMS = ["sWT0", "sWT1", "sG1S", "sHT", "sST0", "sST1",
                      "sOS", "sOS1", "sY0", "sY1", "sY2", "sY3", "sO0", "sO1"]

        def mark_final():
            for nm in STORE_SEMS:
                SP.mark(nm, "fin")

        def wait_final(e):
            for nm in STORE_SEMS:
                e.wait_ge(SEM[nm], SP.at(nm, "fin"))

        def av(off, n):
            return A[:, off:off + n]

        def avb(off, nbf):
            return A.bitcast(BF16)[:, 2 * off:2 * off + nbf]

        TB_D, TB_V = 16384, 21120
        ITAB = 25728
        O_XT, O_XC, O_XNT, O_XNT_LO, O_XQT = 0, 8192, 12288, 20480, 28672
        O_EV, O_R1, O_WSX = 36864, 40960, 46592
        P3_KVQ, P3_A, P3_P, P3_SB = 0, 24576, 33536, 38656
        P3_DA, P3_RD, P3_OE, P3_ONE = 45056, 45568, 36608, 47616
        P4_OH, P4_OL, P4_WH, P4_WL = 0, 16384, 32768, 33792
        P4_EV = 34816
        P5_Y, P5_X, P5_F, P5_EV = 36864, 37888, 38912, 39936

        CL_FLAG, CL_CALL, CL_C52, CL_M, CL_T0, CL_R, CL_R2, CL_RA = 0, 1, 2, 3, 4, 5, 6, 7
        CL_AS = 8
        CL_WS = 16
        CL_WR = 20

        def col(i):
            return cols[:, i:i + 1]

        ident = cst[:, 0:128]
        onescol = cst[:, 128:129]

        def a1_off(dst, hl, b, part):
            return (hl * B + b) * (NCORE * KVQ) + dst * KVQ + part * Dh * 512

        def a2_off(dst, hl, b, which):
            return dst * (4 * B * OBLK) + (hl * B + b) * OBLK + which * Dh * 512

        def xnt_hi(fc, lo=0, n=512):
            return avb(O_XNT, 16384)[:, fc * 512 + lo:fc * 512 + lo + n]

        def xnt_lo(fc, lo=0, n=512):
            return avb(O_XNT_LO, 16384)[:, fc * 512 + lo:fc * 512 + lo + n]

        def xqt(fc, tt):
            return avb(O_XQT, 16384)[:, fc * 512 + tt * 128:fc * 512 + tt * 128 + 128]

        def emit_sync(e):
            e.dma_start(out=cst[:], in_=consts[:, :]).then_inc(SEM["sP0"], 16)
            e.dma_start(out=bcol[:], in_=b_attn_t[:, :]).then_inc(SEM["sP0"], 16)
            for rt in range(4):
                e.dma_start(out=av(O_XT + rt * 4096, 4096),
                            in_=w_ffn[rt * 128:(rt + 1) * 128, :]).then_inc(SEM["sP0"], 16)
            e.dma_start(out=itmp[:, 1024:1024 + 8 * NIDX + 64], in_=a1idx[:, :]).then_inc(SEM["sP0"], 16)
            e.dma_start(out=av(O_R1, 4096),
                        in_=bass.AP(tensor=b_attn.tensor, offset=2 * DM,
                                    ap=[[0, 128], [1, DM]])).then_inc(SEM["sP0"], 16)
            SP.mark("sP0", "p0")
            for b in range(B):
                for tt in range(4):
                    it = b * 4 + tt
                    if it < 2:
                        e.wait_ge(sT, SP.at("sT", "ttr_127"))
                    else:
                        pb_, pt = divmod(it - 2, 4)
                        e.wait_ge(sT, SP.at("sT", f"trq_{pb_}_{pt}"))
                    e.dma_start(out=av(O_XT + (it % 2) * 4096, 4096),
                                in_=xrows[b * 512 + tt * 128: b * 512 + (tt + 1) * 128, :]
                                ).then_inc(SEM[f"sX{it % 2}"], 16)
                    SP.mark(f"sX{it % 2}", f"x_{b}_{tt}")
                for ui, (kind, g) in enumerate(PUNITS):
                    base_col = {"Q": 0, "K": DM, "V": 2 * DM}[kind] + g * 512
                    for fc in range(NF):
                        i = (b * 24 + ui) * NF + fc
                        if i >= 2:
                            j = i - 2
                            pb_, rem = divmod(j, 24 * NF)
                            pui, pfc = divmod(rem, NF)
                            e.wait_ge(sV, SP.at("sV", f"wsp_{pb_}_{pui}_{pfc}"))
                        e.dma_start(out=wbuf[:, (i % 2) * 512:(i % 2 + 1) * 512],
                                    in_=w_attn[fc * 128:(fc + 1) * 128,
                                               base_col:base_col + 512]).then_inc(SEM[f"sW{i % 2}"], 16)
                        SP.mark(f"sW{i % 2}", f"w_{b}_{ui}_{fc}")
                if b == 0:
                    e.wait_ge(sGP, SP.at("sGP", "g1"))
                for jc in range(8):
                    i = b * 8 + jc
                    if i >= 2:
                        j = i - 2
                        e.wait_ge(sV, SP.at("sV", f"fevl_{j // 8}_{j % 8}"))
                    e.dma_start(out=av(O_WSX + (i % 2) * 512, 512),
                                in_=bass.AP(tensor=g1s_out.tensor, offset=jc * 512,
                                            ap=[[0, 128], [1, 512]])).then_inc(SEM[f"sWX{i % 2}"], 16)
                    SP.mark(f"sWX{i % 2}", f"wsx_{b}_{jc}")
                    for fc in range(NF):
                        iw = (b * 8 + jc) * NF + fc
                        if iw >= 4:
                            jw = iw - 4
                            pb_, rem = divmod(jw, 8 * NF)
                            pjc, pfc = divmod(rem, NF)
                            e.wait_ge(sT, SP.at("sT", f"fmm_{pb_}_{pjc}_{pfc}"))
                        e.dma_start(out=wqs[:, (iw % 4) * 512:(iw % 4 + 1) * 512],
                                    in_=bass.AP(tensor=g1_out.tensor,
                                                offset=jc * (DM * 512) + fc * 128 * 512,
                                                ap=[[512, 128], [1, 512]])).then_inc(SEM[f"sWQ{iw % 4}"], 16)
                        SP.mark(f"sWQ{iw % 4}", f"wq_{b}_{jc}_{fc}")
            for hl in range(4 if phases >= 3 else 0):
                for s in range(8):
                    if s >= 2:
                        e.wait_ge(sV, SP.at("sV", f"sb_{KT_LAST[(hl, s - 2, 1)]}"))
                    elif hl > 0:
                        e.wait_ge(sV, SP.at("sV", f"sb_{KT_LAST[(hl - 1, 6 + s, 1)]}"))
                    else:
                        e.wait_ge(sT, SP.at("sT", "fgrp_1_7"))
                        W_(e, "sHT", "htabst")
                    e.dma_start(out=av(P3_A + (s % 2) * AW, AW),
                                in_=bass.AP(tensor=htab.tensor,
                                            offset=(hl * 8 + s) * TABW,
                                            ap=[[1, 128], [1, AW]])).then_inc(SEM[f"sAS{s % 2}"], 16)
                    SP.mark(f"sAS{s % 2}", f"astrip_{hl}_{s}")
            e.wait_ge(sV, SP.at("sV", "r1free"))
            if phases < 4:
                wait_final(e)
                return
            e.dma_start(out=av(O_R1, 4096),
                        in_=bass.AP(tensor=b_proj.tensor, offset=0,
                                    ap=[[0, 128], [1, DM]])).then_inc(SEM["sBP"], 16)
            SP.mark("sBP", "bproj")
            first_tiles = True
            for bbp in range(B):
                for mc in range(8):
                    for ch in range(32):
                        i = (bbp * 8 + mc) * 32 + ch
                        if i >= 2:
                            e.wait_ge(sV, SP.at("sV", f"wpsp_{i - 2}"))
                        e.dma_start(out=wbuf[:, (i % 2) * 512:(i % 2 + 1) * 512],
                                    in_=w_proj[ch * 128:(ch + 1) * 128,
                                               mc * 512:(mc + 1) * 512]).then_inc(SEM[f"sWP{i % 2}"], 16)
                        SP.mark(f"sWP{i % 2}", f"wp_{i}")
                    for tt in range(4):
                        iv = (bbp * 8 + mc) * 4 + tt
                        if first_tiles:
                            e.wait_ge(sS, SP.at("sS", f"exp_{NKT - 1}"))
                            first_tiles = False
                        if iv >= 2:
                            e.wait_ge(sV, SP.at("sV", f"pe_{iv - 2}"))
                        row0 = bbp * 512 + tt * 128
                        e.dma_start(out=av(P5_X + (iv % 2) * 512, 512),
                                    in_=xrows[row0:row0 + 128, mc * 512:(mc + 1) * 512]).then_inc(SEM[f"sP5{iv % 2}"], 16)
                        e.dma_start(out=av(P5_F + (iv % 2) * 512, 512),
                                    in_=ffnscr[row0:row0 + 128, mc * 512:(mc + 1) * 512]).then_inc(SEM[f"sP5{iv % 2}"], 16)
                        SP.mark(f"sP5{iv % 2}", f"p4in_{iv}")
            wait_final(e)

        def emit_gpsimd(e):
            W_(e, "sP0", "p0")
            e.iota(out=A.bitcast(I32)[0:32, ITAB:ITAB + TABW], pattern=[[1, TABW]], base=0, channel_multiplier=0)
            e.tensor_copy(out=av(TB_D, TABW)[0:32, :], in_=A.bitcast(I32)[0:32, ITAB:ITAB + TABW]).then_inc(sGP, 1)
            SP.mark("sGP", "iota")
            W_(e, "sWT0", "wqst_126")
            W_(e, "sWT1", "wqst_127")
            W_(e, "sG1S", "g1ready")
            e.collective_compute("AllGather", Alu.bypass,
                                 replica_groups=[list(range(NCORE))],
                                 ins=[g1_in[:, :]], outs=[g1_out[:, :]]).then_inc(sGP, 1)
            e.collective_compute("AllGather", Alu.bypass,
                                 replica_groups=[list(range(NCORE))],
                                 ins=[g1s_in[:, :]], outs=[g1s_out[:, :]]).then_inc(sGP, 1)
            SP.mark("sGP", "g1")
            if phases >= 2:
                for b in range(B):
                    W_(e, "sST0", f"st_{b}_22_3")
                    W_(e, "sST1", f"st_{b}_23_3")
                    for h in range(4):
                        u = h * B + b
                        e.collective_compute("AllGather", Alu.bypass,
                                             replica_groups=[list(range(NCORE))],
                                             ins=[a1_in[:, u * (NCORE * KVQ):(u + 1) * (NCORE * KVQ)]],
                                             outs=[a1o[u][:, :]]).then_inc(sGP, 1)
                        SP.mark("sGP", f"a1g_{h}_{b}")
            if phases < 3:
                if dbg:
                    wait_final(e)
                    SP.mark("sGP", "gpend")
                    e.wait_ge(sGP, SP.at("sGP", "gpend"))
                    for name, ap_out in dbg.items():
                        sap = A[:, 0:24576] if name == "p3kvq" else (
                            A[:, 24576:40960] if name == "p3tail" else src_map[name][:, :])
                        e.dma_start(out=ap_out[:, :], in_=sap).then_inc(SEM["sDBG"], 16)
                    SP.mark("sDBG", "dbg")
                    W_(e, "sDBG", "dbg")
                return

            v512s = [a1o[u2].rearrange("a b -> (a b)").rearrange("(n r) -> n r", r=512) for u2 in range(8)]
            v128s = [a1o[u2].rearrange("a b -> (a b)").rearrange("(n r) -> n r", r=128) for u2 in range(8)]
            for hl in range(4):
                for bb in range(B):
                    u = hl * B + bb
                    e.wait_ge(sGP, SP.at("sGP", f"a1g_{hl}_{bb}"))
                    if hl > 0:
                        e.wait_ge(sT, SP.at("sT", f"pv_{KT_LAST[(hl - 1, 7, bb)]}"))
                    elif bb == 0:
                        e.wait_ge(sT, SP.at("sT", "trq_1_3"))
                    else:
                        e.wait_ge(sT, SP.at("sT", "grp_1_23"))
                    base = P3_KVQ + bb * 12288
                    ic0 = 1024 + u * NIDX
                    for r in range(8):
                        e.indirect_dma_start(
                            out=av(base, 4096)[:, r * 512:(r + 1) * 512],
                            out_offset=None, in_=v512s[u][:, :],
                            in_offset=bass.IndirectOffsetOnAxis(
                                ap=itmp[:, ic0 + r:ic0 + r + 1], axis=0)).then_inc(SEM[f"sKV{bb}"], 16)
                        e.indirect_dma_start(
                            out=av(base + 8192, 4096)[:, r * 512:(r + 1) * 512],
                            out_offset=None, in_=v512s[u][:, :],
                            in_offset=bass.IndirectOffsetOnAxis(
                                ap=itmp[:, ic0 + 8 + r:ic0 + 9 + r], axis=0)).then_inc(SEM[f"sKV{bb}"], 16)
                        for kt in range(4):
                            e.indirect_dma_start(
                                out=av(base + 4096, 4096)[:, (r * 4 + kt) * 128:(r * 4 + kt + 1) * 128],
                                out_offset=None, in_=v128s[u][:, :],
                                in_offset=bass.IndirectOffsetOnAxis(
                                    ap=itmp[:, ic0 + 16 + r * 4 + kt:ic0 + 17 + r * 4 + kt],
                                    axis=0)).then_inc(SEM[f"sKV{bb}"], 16)
                    SP.mark(f"sKV{bb}", f"kvq_{hl}_{bb}")
            if phases < 4:
                if dbg:
                    wait_final(e)
                    SP.mark("sGP", "gpend")
                    e.wait_ge(sGP, SP.at("sGP", "gpend"))
                    for name, ap_out in dbg.items():
                        sap = A[:, 0:24576] if name == "p3kvq" else (
                            A[:, 24576:40960] if name == "p3tail" else src_map[name][:, :])
                        e.dma_start(out=ap_out[:, :], in_=sap).then_inc(SEM["sDBG"], 16)
                    SP.mark("sDBG", "dbg")
                    W_(e, "sDBG", "dbg")
                return
            HSZ = B * Dh * S
            for hl in range(4):
                W_(e, "sOS", f"ostu_{hl * 16 + 14}")
                W_(e, "sOS1", f"ostu_{hl * 16 + 15}")
                e.collective_compute("AllGather", Alu.bypass,
                                     replica_groups=[list(range(NCORE))],
                                     ins=[o_scr[:, hl * HSZ:(hl + 1) * HSZ]],
                                     outs=[o_gp[hl][:, :]]).then_inc(sGP, 1)
                e.collective_compute("AllGather", Alu.bypass,
                                     replica_groups=[list(range(NCORE))],
                                     ins=[o_scr_lo[:, hl * HSZ:(hl + 1) * HSZ]],
                                     outs=[o_gpl[hl][:, :]]).then_inc(sGP, 1)
                SP.mark("sGP", f"og_{hl}")
            vgos = [o_gp[h].rearrange("a b -> (a b)").rearrange("(n r) -> n r", r=512) for h in range(4)]
            vgols = [o_gpl[h].rearrange("a b -> (a b)").rearrange("(n r) -> n r", r=512) for h in range(4)]
            for bbp in range(B):
                e.wait_ge(sGP, SP.at("sGP", "og_3"))
                e.wait_ge(sT, SP.at("sT", f"pv_{NKT - 1}" if bbp == 0 else "pmw_255"))
                for ch in range(32):
                    hl = ch % 4
                    icx = 1024 + 8 * NIDX + bbp * 32 + ch
                    e.indirect_dma_start(
                        out=A.bitcast(BF16)[:, ch * 512:(ch + 1) * 512],
                        out_offset=None, in_=vgos[hl][:, :],
                        in_offset=bass.IndirectOffsetOnAxis(
                            ap=itmp[:, icx:icx + 1], axis=0)).then_inc(SEM[f"sKV{bbp}"], 16)
                    e.indirect_dma_start(
                        out=A.bitcast(BF16)[:, 16384 + ch * 512:16384 + (ch + 1) * 512],
                        out_offset=None, in_=vgols[hl][:, :],
                        in_offset=bass.IndirectOffsetOnAxis(
                            ap=itmp[:, icx:icx + 1], axis=0)).then_inc(SEM[f"sKV{bbp}"], 16)
                SP.mark(f"sKV{bbp}", f"old_{bbp}")
            if dbg:
                wait_final(e)
                SP.mark("sGP", "gpend")
                e.wait_ge(sGP, SP.at("sGP", "gpend"))
                for name, ap_out in dbg.items():
                    sap = A[:, 0:24576] if name == "p3kvq" else (
                        A[:, 24576:40960] if name == "p3tail" else src_map[name][:, :])
                    e.dma_start(out=ap_out[:, :], in_=sap).then_inc(SEM["sDBG"], 16)
                SP.mark("sDBG", "dbg")
                W_(e, "sDBG", "dbg")

        def emit_scalar(e):
            for i in range(128):
                rt, fc = divmod(i, NF)
                e.wait_ge(sV, SP.at("sV", f"tev_{i}"))
                e.dma_start(out=bass.AP(tensor=g1_in.tensor,
                                        offset=fc * 128 * 512 + rt * 128,
                                        ap=[[512, 128], [1, 128]]),
                            in_=wtr[:, (i % 2) * 128:(i % 2) * 128 + 128]).then_inc(SEM[f"sWT{i % 2}"], 16)
                SP.mark(f"sWT{i % 2}", f"wqst_{i}")
            e.wait_ge(sV, SP.at("sV", "wscale"))
            for rt in range(4):
                e.dma_start(out=bass.AP(tensor=g1s_in.tensor, offset=rt * 128,
                                        ap=[[1, 128], [1, 1]]),
                            in_=col(CL_WS + rt)).then_inc(SEM["sG1S"], 16)
            SP.mark("sG1S", "g1ready")
            e.wait_ge(sV, SP.at("sV", "htab"))
            e.dma_start(out=htab[:, :], in_=av(TB_V, TABW)[0:32, :]).then_inc(SEM["sHT"], 16)
            SP.mark("sHT", "htabst")
            for b in range(B):
                for tt in range(4):
                    e.wait_ge(sV, SP.at("sV", f"var_{b}_{tt}"))
                    e.activation(out=col(CL_R), in_=col(CL_T0), func=Act.Sqrt).then_inc(sS, 1)
                    SP.mark("sS", f"sqrt_{b}_{tt}")
                for ui, (kind, g) in enumerate(PUNITS):
                    for q4 in range(4):
                        e.wait_ge(sV, SP.at("sV", f"ev_{b}_{ui}_{q4}"))
                        src = av(O_EV + (ui % 2) * 2048 + q4 * 512, 512)
                        if kind in ("K", "Q"):
                            hh = g * 4 + q4
                            part = 0 if kind == "K" else 2
                            e.dma_start(out=bass.AP(tensor=a1_in.tensor,
                                                    offset=a1_off(hh // 4, hh % 4, b, part),
                                                    ap=[[512, 128], [1, 512]]),
                                        in_=src).then_inc(SEM[f"sST{ui % 2}"], 16)
                        else:
                            tt = q4
                            for h4 in range(4):
                                hh = g * 4 + h4
                                e.dma_start(out=bass.AP(tensor=a1_in.tensor,
                                                        offset=a1_off(hh // 4, hh % 4, b, 1) + tt * 128 * Dh,
                                                        ap=[[Dh, 128], [1, 128]]),
                                            in_=src[:, h4 * 128:(h4 + 1) * 128]).then_inc(SEM[f"sST{ui % 2}"], 16)
                        SP.mark(f"sST{ui % 2}", f"st_{b}_{ui}_{q4}")
                for jc in range(8):
                    for tt in range(4):
                        e.wait_ge(sV, SP.at("sV", f"fev_{b}_{jc}_{tt}"))
                        e.dma_start(out=ffnscr[b * 512 + tt * 128: b * 512 + (tt + 1) * 128,
                                               jc * 512:(jc + 1) * 512],
                                    in_=av(O_EV + (jc % 2) * 2048 + tt * 512, 512)
                                    ).then_inc(SEM[f"sST{jc % 2}"], 16)
                        SP.mark(f"sST{jc % 2}", f"fst_{b}_{jc}_{tt}")
            if phases < 3:
                mark_final()
                return
            pend_ostu = []

            def ostu_op(u, hl, s, bb):
                e.wait_ge(sV, SP.at("sV", f"osp_{u}"))
                off = (hl * B + bb) * Dh * S + s * 512
                oeo = P3_OE + (u % 2) * 1024
                osem = "sOS" if u % 2 == 0 else "sOS1"
                e.dma_start(out=bass.AP(tensor=o_scr.tensor, offset=off,
                                        ap=[[S, 128], [1, 512]]),
                            in_=avb(oeo + 512, 512)).then_inc(SEM[osem], 16)
                e.dma_start(out=bass.AP(tensor=o_scr_lo.tensor, offset=off,
                                        ap=[[S, 128], [1, 512]]),
                            in_=avb(oeo + 768, 512)).then_inc(SEM[osem], 16)
                SP.mark(osem, f"ostu_{u}")

            for n, (hl, s, bb, ki) in enumerate(KT):
                while pend_ostu and n >= pend_ostu[0][4] + 8:
                    ostu_op(*pend_ostu.pop(0)[:4])
                e.wait_ge(sV, SP.at("sV", f"sb_{n}"))
                if n >= 6:
                    e.wait_ge(sV, SP.at("sV", f"da_{n - 6}"))
                    e.wait_ge(sT, SP.at("sT", f"pv_{n - 6}"))
                e.activation(out=av(P3_P + (n % 6) * 512, 512),
                             in_=av(P3_SB + (n % 2) * 512, 512),
                             func=Act.Exp).then_inc(sS, 1)
                SP.mark("sS", f"exp_{n}")
                if ki == NKS[s] - 1:
                    pend_ostu.append((UNIT_OF[(hl, s, bb)], hl, s, bb, n))
            for it in pend_ostu:
                ostu_op(*it[:4])
            SP.mark("sOS", "oscr_done")
            SP.mark("sOS1", "oscr_done")
            if phases < 4:
                mark_final()
                return
            for i in range(64):
                bbp, r8 = divmod(i, 32)
                mc, tt = divmod(r8, 4)
                row0 = bbp * 512 + tt * 128
                e.wait_ge(sV, SP.at("sV", f"pe_{i}"))
                e.dma_start(out=out[row0:row0 + 128, mc * 512:(mc + 1) * 512],
                            in_=av(P4_EV + (i % 4) * 512, 512)).then_inc(SEM[f"sY{i % 4}"], 16)
                SP.mark(f"sY{i % 4}", f"p4st_{i}")
            mark_final()

        def fp16_round(e, v):
            t1 = itmp[:, 0:512]
            t1f = itmp.bitcast(F32)[:, 0:512]
            t2 = itmp[:, 512:1024]
            t2f = itmp.bitcast(F32)[:, 512:1024]
            e.tensor_scalar(out=t1, in0=v.bitcast(I32), scalar1=23, scalar2=255,
                            op0=Alu.logical_shift_right, op1=Alu.bitwise_and)
            e.tensor_scalar(out=t2, in0=t1, scalar1=-1, scalar2=150, op0=Alu.mult, op1=Alu.add)
            e.tensor_scalar(out=t2, in0=t2, scalar1=23, scalar2=None, op0=Alu.logical_shift_left)
            e.tensor_tensor(out=v, in0=v, in1=t2f, op=Alu.mult)
            e.tensor_scalar(out=v, in0=v, scalar1=col(CL_C52), scalar2=None, op0=Alu.add)
            e.tensor_scalar(out=v, in0=v, scalar1=col(CL_C52), scalar2=None, op0=Alu.subtract)
            e.tensor_scalar(out=t2, in0=t1, scalar1=104, scalar2=None, op0=Alu.add)
            e.tensor_scalar(out=t2, in0=t2, scalar1=23, scalar2=None, op0=Alu.logical_shift_left)
            e.tensor_tensor(out=v, in0=v, in1=t2f, op=Alu.mult)
            e.tensor_scalar(out=t1, in0=v.bitcast(I32), scalar1=23, scalar2=255,
                            op0=Alu.logical_shift_right, op1=Alu.bitwise_and)
            e.tensor_scalar(out=t1, in0=t1, scalar1=10, scalar2=103,
                            op0=Alu.subtract, op1=Alu.max)
            e.tensor_scalar(out=t2, in0=t1, scalar1=-1, scalar2=254, op0=Alu.mult, op1=Alu.add)
            e.tensor_scalar(out=t2, in0=t2, scalar1=23, scalar2=None, op0=Alu.logical_shift_left)
            e.tensor_scalar(out=t1, in0=t1, scalar1=23, scalar2=None, op0=Alu.logical_shift_left)
            e.tensor_tensor(out=v, in0=v, in1=t2f, op=Alu.mult)
            e.tensor_scalar(out=v, in0=v, scalar1=col(CL_CALL), scalar2=None, op0=Alu.add)
            e.tensor_scalar(out=v, in0=v, scalar1=col(CL_CALL), scalar2=None, op0=Alu.subtract)
            e.tensor_tensor(out=v, in0=v, in1=t1f, op=Alu.mult)

        def emit_vector(e):
            W_(e, "sP0", "p0")
            e.tensor_scalar(out=col(CL_FLAG), in0=onescol, scalar1=float(2.0 ** -30),
                            scalar2=1.0, op0=Alu.add, op1=Alu.is_gt)
            e.tensor_scalar(out=col(CL_C52), in0=col(CL_FLAG), scalar1=float(C52),
                            scalar2=None, op0=Alu.mult)
            e.tensor_scalar(out=col(CL_CALL), in0=col(CL_FLAG), scalar1=-1.0,
                            scalar2=1.0, op0=Alu.mult, op1=Alu.add)
            e.tensor_tensor(out=col(CL_CALL), in0=col(CL_CALL), in1=cst[:, 129:130], op=Alu.mult)
            e.tensor_tensor(out=col(CL_CALL), in0=col(CL_CALL), in1=col(CL_C52),
                            op=Alu.add)
            e.memset(av(P3_ONE, 128)[0:1, :], 1.0).then_inc(sV, 1)
            SP.mark("sV", "setup")
            e.wait_ge(sV, SP.at("sV", "setup"))
            W_(e, "sP0", "p0")
            for rt in range(4):
                e.tensor_reduce(out=col(CL_WS + rt), in_=av(O_XT + rt * 4096, 4096),
                                axis=AX, op=Alu.max, apply_absolute_value=True)
            e.tensor_scalar(out=cols[:, CL_WS:CL_WS + 4], in0=cols[:, CL_WS:CL_WS + 4],
                            scalar1=float(1.0 / QMAX), scalar2=None, op0=Alu.mult).then_inc(sV, 1)
            SP.mark("sV", "wscale")
            e.wait_ge(sV, SP.at("sV", "wscale"))
            e.reciprocal(out=cols[:, CL_WR:CL_WR + 4], in_=cols[:, CL_WS:CL_WS + 4]).then_inc(sV, 1)
            SP.mark("sV", "wsrec")
            e.wait_ge(sV, SP.at("sV", "wsrec"))
            for rt in range(4):
                wt = av(O_XT + rt * 4096, 4096)
                e.tensor_scalar(out=wt, in0=wt, scalar1=col(CL_WR + rt), scalar2=None, op0=Alu.mult)
                e.tensor_scalar(out=wt, in0=wt, scalar1=col(CL_CALL), scalar2=None, op0=Alu.add)
                e.tensor_scalar(out=wt, in0=wt, scalar1=col(CL_CALL), scalar2=None,
                                op0=Alu.subtract).then_inc(sV, 1)
            SP.mark("sV", "wqvals")
            e.wait_ge(sGP, SP.at("sGP", "iota"))
            td = av(TB_D, TABW)[0:32, :]
            tv = av(TB_V, TABW)[0:32, :]
            e.tensor_scalar(out=td, in0=td, scalar1=cst[0:32, 131:132], scalar2=None,
                            op0=Alu.subtract)
            e.tensor_scalar(out=tv, in0=td, scalar1=cst[0:32, 130:131], scalar2=None,
                            op0=Alu.mult).then_inc(sV, 1)
            SP.mark("sV", "rawbias")
            e.wait_ge(sV, SP.at("sV", "rawbias"))
            ib = A.bitcast(I32)[0:32, ITAB:ITAB + TABW]
            ibf = av(ITAB, TABW)[0:32, :]
            e.tensor_scalar(out=ib, in0=tv.bitcast(I32), scalar1=23, scalar2=255,
                            op0=Alu.logical_shift_right, op1=Alu.bitwise_and)
            e.tensor_scalar(out=ib, in0=ib, scalar1=-1, scalar2=150, op0=Alu.mult, op1=Alu.add)
            e.tensor_scalar(out=ib, in0=ib, scalar1=23, scalar2=None, op0=Alu.logical_shift_left)
            e.tensor_tensor(out=tv, in0=tv, in1=ibf, op=Alu.mult)
            e.tensor_scalar(out=tv, in0=tv, scalar1=col(CL_C52)[0:32, :], scalar2=None, op0=Alu.add)
            e.tensor_scalar(out=tv, in0=tv, scalar1=col(CL_C52)[0:32, :], scalar2=None,
                            op0=Alu.subtract)
            e.tensor_scalar(out=ib, in0=ib, scalar1=23, scalar2=None, op0=Alu.logical_shift_right)
            e.tensor_scalar(out=ib, in0=ib, scalar1=-1, scalar2=254, op0=Alu.mult, op1=Alu.add)
            e.tensor_scalar(out=ib, in0=ib, scalar1=23, scalar2=None, op0=Alu.logical_shift_left)
            e.tensor_tensor(out=tv, in0=tv, in1=ibf, op=Alu.mult)
            e.tensor_scalar(out=tv, in0=tv, scalar1=-M_BOUND, scalar2=None, op0=Alu.add)
            e.tensor_scalar(out=td, in0=td, scalar1=0.0, scalar2=None, op0=Alu.is_gt)
            e.scalar_tensor_tensor(out=tv, in0=td, scalar=BIG_NEG, in1=tv,
                                   op0=Alu.mult, op1=Alu.add).then_inc(sV, 1)
            SP.mark("sV", "htab")
            for g, job in enumerate(TRJOBS):
                if job[0] == "wq":
                    e.wait_ge(sT, SP.at("sT", f"ttr_{g}"))
                    if g >= 2:
                        W_(e, f"sWT{g % 2}", f"wqst_{g - 2}")
                    e.tensor_copy(out=wtr[:, (g % 2) * 128:(g % 2) * 128 + 128],
                                  in_=psc[g % 2][:, 0:128]).then_inc(sV, 1)
                    SP.mark("sV", f"tev_{g}")
                elif job[0] == "xn":
                    _, b, tt, fc = job
                    if fc == 0:
                        emit_ln_tile(e, b, tt)
                    e.wait_ge(sT, SP.at("sT", f"ttr_{g}"))
                    if g == 128:
                        W_(e, "sHT", "htabst")
                    hi = xnt_hi(fc, tt * 128, 128)
                    lo = xnt_lo(fc, tt * 128, 128)
                    e.tensor_copy(out=hi, in_=psc[g % 2][:, 0:128])
                    e.tensor_tensor(out=lo, in0=psc[g % 2][:, 0:128], in1=hi,
                                    op=Alu.subtract).then_inc(sV, 1)
                    SP.mark("sV", f"tev_{g}")
                else:
                    _, b, tt, fc = job
                    e.wait_ge(sT, SP.at("sT", f"ttr_{g}"))
                    e.tensor_copy(out=xqt(fc, tt), in_=psc[g % 2][:, 0:128]).then_inc(sV, 1)
                    SP.mark("sV", f"tev_{g}")
                if job[0] == "xq" and job[2] == 3 and job[3] == NF - 1:
                    b = job[1]
                    emit_p1_proj_vector(e, b)
                    emit_ffn_vector(e, b)
            SP.mark("sV", "r1free")
            if phases >= 3:
                emit_p3_vector(e)
            if phases >= 4:
                emit_p4_vector(e)

        def emit_ln_tile(e, b, tt):
            it = b * 4 + tt
            xt = av(O_XT + (it % 2) * 4096, 4096)
            xc = av(O_XC, 4096)
            W_(e, f"sX{(b * 4 + tt) % 2}", f"x_{b}_{tt}")
            e.reduce_sum(out=col(CL_M), in_=xt, axis=AX)
            e.tensor_scalar(out=col(CL_M), in0=col(CL_M), scalar1=float(1.0 / DM),
                            scalar2=None, op0=Alu.mult).then_inc(sV, 1)
            SP.mark("sV", f"mean_{b}_{tt}")
            e.wait_ge(sV, SP.at("sV", f"mean_{b}_{tt}"))
            if it > 0:
                pb_, pt = divmod(it - 1, 4)
                e.wait_ge(sT, SP.at("sT", f"trn_{pb_}_{pt}"))
            e.tensor_scalar(out=xc, in0=xt, scalar1=col(CL_M), scalar2=None, op0=Alu.subtract)
            e.tensor_tensor(out=xt, in0=xc, in1=xc, op=Alu.mult)
            e.reduce_sum(out=col(CL_T0), in_=xt, axis=AX)
            e.tensor_scalar(out=col(CL_T0), in0=col(CL_T0), scalar1=float(1.0 / DM),
                            scalar2=float(EPS), op0=Alu.mult, op1=Alu.add).then_inc(sV, 1)
            SP.mark("sV", f"var_{b}_{tt}")
            e.wait_ge(sS, SP.at("sS", f"sqrt_{b}_{tt}"))
            e.reciprocal(out=col(CL_R2), in_=col(CL_R)).then_inc(sV, 1)
            SP.mark("sV", f"rec_{b}_{tt}")
            e.wait_ge(sV, SP.at("sV", f"rec_{b}_{tt}"))
            e.tensor_scalar(out=xc, in0=xc, scalar1=col(CL_R2), scalar2=None,
                            op0=Alu.mult).then_inc(sV, 1)
            SP.mark("sV", f"xn_{b}_{tt}")
            e.tensor_reduce(out=col(CL_RA), in_=xc, axis=AX, op=Alu.max,
                            apply_absolute_value=True)
            e.tensor_scalar(out=col(CL_AS + it), in0=col(CL_RA),
                            scalar1=float(1.0 / QMAX), scalar2=None,
                            op0=Alu.mult).then_inc(sV, 1)
            SP.mark("sV", f"amax_{b}_{tt}")
            e.wait_ge(sV, SP.at("sV", f"amax_{b}_{tt}"))
            e.reciprocal(out=col(CL_RA), in_=col(CL_RA))
            e.tensor_scalar(out=col(CL_RA), in0=col(CL_RA), scalar1=float(QMAX),
                            scalar2=None, op0=Alu.mult).then_inc(sV, 1)
            SP.mark("sV", f"ra_{b}_{tt}")
            e.wait_ge(sV, SP.at("sV", f"ra_{b}_{tt}"))
            e.tensor_scalar(out=xt, in0=xc, scalar1=col(CL_RA), scalar2=None, op0=Alu.mult)
            e.tensor_scalar(out=xt, in0=xt, scalar1=col(CL_CALL), scalar2=None, op0=Alu.add)
            e.tensor_scalar(out=xt, in0=xt, scalar1=col(CL_CALL), scalar2=None,
                            op0=Alu.subtract).then_inc(sV, 1)
            SP.mark("sV", f"xq_{b}_{tt}")

        def emit_p1_proj_vector(e, b):
            for ui, (kind, g) in enumerate(PUNITS):
                for fc in range(NF):
                    i = (b * 24 + ui) * NF + fc
                    W_(e, f"sW{i % 2}", f"w_{b}_{ui}_{fc}")
                    if i >= 2:
                        j = i - 2
                        pb2, rem2 = divmod(j, 24 * NF)
                        pui2, pfc2 = divmod(rem2, NF)
                        e.wait_ge(sT, SP.at("sT", f"pmm_{pb2}_{pui2}_{pfc2}"))
                    wb = wbuf[:, (i % 2) * 512:(i % 2 + 1) * 512]
                    wh = whl[:, (i % 2) * 512:(i % 2 + 1) * 512]
                    wl = whl[:, 1024 + (i % 2) * 512:1024 + (i % 2 + 1) * 512]
                    e.tensor_copy(out=wh, in_=wb)
                    e.tensor_tensor(out=wl, in0=wb, in1=wh, op=Alu.subtract).then_inc(sV, 1)
                    SP.mark("sV", f"wsp_{b}_{ui}_{fc}")
                e.wait_ge(sT, SP.at("sT", f"grp_{b}_{ui}"))
                if kind == "V":
                    W_(e, "sP0", "p0")
                qi = QIDX[("proj", b, ui)]
                if qi >= 1:
                    pass
                evi = b * 32 + ui
                if evi >= 2:
                    prev = evi - 2
                    pb_, pui = divmod(prev, 32)
                    if pui < 24:
                        W_(e, f"sST{pui % 2}", f"st_{pb_}_{pui}_3")
                    else:
                        W_(e, f"sST{pui % 2}", f"fst_{pb_}_{pui - 24}_3")
                for q4 in range(4):
                    dv = av(O_EV + (ui % 2) * 2048 + q4 * 512, 512)
                    if kind in ("K", "Q"):
                        hh = g * 4 + q4
                        bci = {"Q": hh, "K": 32 + hh}[kind]
                        e.tensor_scalar(out=dv, in0=quad[q4][:],
                                        scalar1=bcol[:, bci:bci + 1], scalar2=None,
                                        op0=Alu.add).then_inc(sV, 1)
                    else:
                        e.tensor_tensor(out=dv, in0=quad[q4][:],
                                        in1=av(O_R1, 4096)[:, g * 512:(g + 1) * 512],
                                        op=Alu.add).then_inc(sV, 1)
                    SP.mark("sV", f"ev_{b}_{ui}_{q4}")

        def emit_ffn_vector(e, b):
            for jc in range(8):
                e.wait_ge(sT, SP.at("sT", f"fgrp_{b}_{jc}"))
                W_(e, f"sWX{(b * 8 + jc) % 2}", f"wsx_{b}_{jc}")
                evi = b * 32 + 24 + jc
                prev = evi - 2
                pb_, pui = divmod(prev, 32)
                if pui < 24:
                    W_(e, f"sST{pui % 2}", f"st_{pb_}_{pui}_3")
                else:
                    W_(e, f"sST{pui % 2}", f"fst_{pb_}_{pui - 24}_3")
                for tt in range(4):
                    ev = av(O_EV + (jc % 2) * 2048 + tt * 512, 512)
                    e.tensor_scalar(out=ev, in0=quad[tt][:],
                                    scalar1=col(CL_AS + b * 4 + tt), scalar2=None, op0=Alu.mult)
                    e.tensor_tensor(out=ev, in0=ev,
                                    in1=av(O_WSX + ((b * 8 + jc) % 2) * 512, 512), op=Alu.mult)
                    fp16_round(e, ev)
                    e.tensor_scalar(out=ev, in0=ev, scalar1=1.0, scalar2=None,
                                    op0=Alu.mult).then_inc(sV, 1)
                    SP.mark("sV", f"fev_{b}_{jc}_{tt}")
                SP.mark("sV", f"fevl_{b}_{jc}")

        def emit_p3_vector(e):
            def da_op(m):
                hl, s, b, ki = KT[m]
                u = UNIT_OF[(hl, s, b)]
                e.wait_ge(sS, SP.at("sS", f"exp_{m}"))
                if ki == 0:
                    if u >= 1:
                        e.wait_ge(sT, SP.at("sT", f"dmm_{u - 1}"))
                    e.tensor_copy(out=av(P3_DA, 512),
                                  in_=av(P3_P + (m % 6) * 512, 512)).then_inc(sV, 1)
                else:
                    e.tensor_tensor(out=av(P3_DA, 512), in0=av(P3_DA, 512),
                                    in1=av(P3_P + (m % 6) * 512, 512), op=Alu.add).then_inc(sV, 1)
                SP.mark("sV", f"da_{m}")
                if ki == NKS[s] - 1:
                    unit_tail(u)

            def unit_tail(u):
                e.wait_ge(sT, SP.at("sT", f"dmm_{u}"))
                e.reciprocal(out=av(P3_RD, 512)[0:1, :], in_=pd[0:1, :]).then_inc(sV, 1)
                SP.mark("sV", f"rdr_{u}")
                e.wait_ge(sT, SP.at("sT", f"bmm_{u}"))
                e.tensor_copy(out=av(P3_RD, 512), in_=pd[:]).then_inc(sV, 1)
                SP.mark("sV", f"rdb_{u}")
                if u >= 2:
                    W_(e, "sOS" if u % 2 == 0 else "sOS1", f"ostu_{u - 2}")
                elif u == 0:
                    W_(e, "sST0", "fst_1_6_3")
                    W_(e, "sST1", "fst_1_7_3")
                oeo = P3_OE + (u % 2) * 1024
                oe = av(oeo, 512)
                e.tensor_tensor(out=oe, in0=po[:], in1=av(P3_RD, 512), op=Alu.mult)
                e.tensor_copy(out=avb(oeo + 512, 512), in_=oe)
                e.tensor_tensor(out=avb(oeo + 768, 512), in0=oe,
                                in1=avb(oeo + 512, 512), op=Alu.subtract).then_inc(sV, 1)
                SP.mark("sV", f"osp_{u}")

            for n, (hl, s, b, ki) in enumerate(KT):
                if ki == 0 and b == 0:
                    W_(e, f"sAS{s % 2}", f"astrip_{hl}_{s}")
                if n == 0:
                    W_(e, "sST0", "fst_1_6_3")
                    W_(e, "sST1", "fst_1_7_3")
                e.wait_ge(sT, SP.at("sT", f"smm_{n}"))
                if n >= 2:
                    e.wait_ge(sS, SP.at("sS", f"exp_{n - 2}"))
                astrip = av(P3_A + (s % 2) * AW, AW)
                u0 = ki * 128 + 511
                slv = astrip[:, 511::-1] if u0 == 511 else astrip[:, u0:u0 - 512:-1]
                e.scalar_tensor_tensor(out=av(P3_SB + (n % 2) * 512, 512),
                                       in0=p3pool[n % 6][:], scalar=float(SCALE),
                                       in1=slv, op0=Alu.mult, op1=Alu.add).then_inc(sV, 1)
                SP.mark("sV", f"sb_{n}")
                if n >= 1:
                    da_op(n - 1)
            da_op(NKT - 1)

        def emit_p4_vector(e):
            for bbp in range(B):
                for mc in range(8):
                    for ch in range(32):
                        i = (bbp * 8 + mc) * 32 + ch
                        W_(e, f"sWP{i % 2}", f"wp_{i}")
                        if i >= 2:
                            e.wait_ge(sT, SP.at("sT", f"pmw_{i - 2}"))
                        wb = wbuf[:, (i % 2) * 512:(i % 2 + 1) * 512]
                        wh = whl[:, (i % 2) * 512:(i % 2 + 1) * 512]
                        wl = whl[:, 1024 + (i % 2) * 512:1024 + (i % 2 + 1) * 512]
                        e.tensor_copy(out=wh, in_=wb)
                        e.tensor_tensor(out=wl, in0=wb, in1=wh, op=Alu.subtract).then_inc(sV, 1)
                        SP.mark("sV", f"wpsp_{i}")
                    for tt in range(4):
                        iv = (bbp * 8 + mc) * 4 + tt
                        e.wait_ge(sT, SP.at("sT", f"ymm_{iv}"))
                        W_(e, f"sP5{iv % 2}", f"p4in_{iv}")
                        W_(e, "sBP", "bproj")
                        if iv >= 4:
                            W_(e, f"sY{iv % 4}", f"p4st_{iv - 4}")
                        ev = av(P4_EV + (iv % 4) * 512, 512)
                        e.tensor_tensor(out=ev, in0=quad[tt][:],
                                        in1=av(O_R1, 4096)[:, mc * 512:(mc + 1) * 512], op=Alu.add)
                        e.tensor_tensor(out=ev, in0=av(P5_X + (iv % 2) * 512, 512), in1=ev, op=Alu.add)
                        e.tensor_tensor(out=ev, in0=ev, in1=av(P5_F + (iv % 2) * 512, 512),
                                        op=Alu.add).then_inc(sV, 1)
                        SP.mark("sV", f"pe_{iv}")

        def emit_tensor(e):
            e.wait_ge(sV, SP.at("sV", "wqvals"))
            for g, job in enumerate(TRJOBS):
                if g >= 2:
                    e.wait_ge(sV, SP.at("sV", f"tev_{g - 2}"))
                if job[0] == "wq":
                    rt, fc = job[1], job[2]
                    src = av(O_XT + rt * 4096, 4096)[:, fc * 128:(fc + 1) * 128]
                elif job[0] == "xn":
                    _, b, tt, fc = job
                    if fc == 0:
                        e.wait_ge(sV, SP.at("sV", f"xn_{b}_{tt}"))
                    src = av(O_XC, 4096)[:, fc * 128:(fc + 1) * 128]
                else:
                    _, b, tt, fc = job
                    it = b * 4 + tt
                    if fc == 0:
                        e.wait_ge(sV, SP.at("sV", f"xq_{b}_{tt}"))
                    src = av(O_XT + (it % 2) * 4096, 4096)[:, fc * 128:(fc + 1) * 128]
                e.transpose(psc[g % 2][:, 0:128], src, ident).then_inc(sT, 1)
                SP.mark("sT", f"ttr_{g}")
                if job[0] == "xn" and job[3] == NF - 1:
                    SP.mark("sT", f"trn_{job[1]}_{job[2]}")
                if job[0] == "xq" and job[3] == NF - 1:
                    SP.mark("sT", f"trq_{job[1]}_{job[2]}")
                    if job[2] == 3:
                        b = job[1]
                        emit_p1_proj_tensor(e, b)
                        emit_ffn_tensor(e, b)
            if phases >= 3:
                emit_p3_tensor(e)
            if phases >= 4:
                emit_p4_tensor(e)

        def emit_p1_proj_tensor(e, b):
            for ui, (kind, g) in enumerate(PUNITS):
                qi = QIDX[("proj", b, ui)]
                if qi >= 1:
                    pu = QUSERS[qi - 1]
                    wait_quad_evacs(e, pu)
                for fc in range(NF):
                    i = (b * 24 + ui) * NF + fc
                    e.wait_ge(sV, SP.at("sV", f"wsp_{b}_{ui}_{fc}"))
                    wh = whl[:, (i % 2) * 512:(i % 2 + 1) * 512]
                    wl = whl[:, 1024 + (i % 2) * 512:1024 + (i % 2 + 1) * 512]
                    first, last = fc == 0, fc == NF - 1
                    if kind in ("K", "Q"):
                        rh = xnt_hi(fc)
                        rl = xnt_lo(fc)
                        for q4 in range(4):
                            lwh = wh[:, q4 * 128:(q4 + 1) * 128]
                            lwl = wl[:, q4 * 128:(q4 + 1) * 128]
                            e.matmul(quad[q4][:], lwh, rh, start=first, stop=False)
                            e.matmul(quad[q4][:], lwh, rl, start=False, stop=False)
                            mm = e.matmul(quad[q4][:], lwl, rh, start=False, stop=last)
                            if q4 == 3:
                                mm.then_inc(sT, 1)
                    else:
                        for q4 in range(4):
                            lh = xnt_hi(fc, q4 * 128, 128)
                            ll = xnt_lo(fc, q4 * 128, 128)
                            e.matmul(quad[q4][:], lh, wh, start=first, stop=False)
                            e.matmul(quad[q4][:], lh, wl, start=False, stop=False)
                            mm = e.matmul(quad[q4][:], ll, wh, start=False, stop=last)
                            if q4 == 3:
                                mm.then_inc(sT, 1)
                    SP.mark("sT", f"pmm_{b}_{ui}_{fc}")
                SP.mark("sT", f"grp_{b}_{ui}")

        def wait_quad_evacs(e, user):
            tag = user[0]
            if tag == "proj":
                _, b, ui = user
                for q4 in range(4):
                    e.wait_ge(sV, SP.at("sV", f"ev_{b}_{ui}_{q4}"))
            elif tag == "ffn":
                _, b, jc = user
                for q4 in range(4):
                    e.wait_ge(sV, SP.at("sV", f"fev_{b}_{jc}_{q4}"))
            else:
                _, si, mc = user
                for q4 in range(4):
                    e.wait_ge(sV, SP.at("sV", f"yev_{si}_{mc}_{q4}"))

        def emit_ffn_tensor(e, b):
            for jc in range(8):
                qi = QIDX[("ffn", b, jc)]
                pu = QUSERS[qi - 1]
                wait_quad_evacs(e, pu)
                for fc in range(NF):
                    i = (b * 8 + jc) * NF + fc
                    W_(e, f"sWQ{i % 4}", f"wq_{b}_{jc}_{fc}")
                    rhsw = wqs[:, (i % 4) * 512:(i % 4 + 1) * 512]
                    first, last = fc == 0, fc == NF - 1
                    for q4 in range(4):
                        mm = e.matmul(quad[q4][:], xqt(fc, q4), rhsw, start=first, stop=last)
                    mm.then_inc(sT, 1)
                    SP.mark("sT", f"fmm_{b}_{jc}_{fc}")
                SP.mark("sT", f"fgrp_{b}_{jc}")

        def emit_p3_tensor(e):
            LB = 4

            def pv_op(m):
                hl, s, b, ki = KT[m]
                u = UNIT_OF[(hl, s, b)]
                nk = NKS[s]
                vb = av(P3_KVQ + b * 12288 + 4096, 4096)
                if ki == 0 and u >= 1:
                    e.wait_ge(sV, SP.at("sV", f"osp_{u - 1}"))
                e.wait_ge(sS, SP.at("sS", f"exp_{m}"))
                e.matmul(po[:], vb[:, ki * 128:(ki + 1) * 128],
                         av(P3_P + (m % 6) * 512, 512),
                         start=(ki == 0), stop=(ki == nk - 1)).then_inc(sT, 1)
                SP.mark("sT", f"pv_{m}")
                if ki == nk - 1:
                    e.wait_ge(sV, SP.at("sV", f"da_{m}"))
                    if u >= 1:
                        e.wait_ge(sV, SP.at("sV", f"rdb_{u - 1}"))
                    e.matmul(pd[0:1, :], onescol, av(P3_DA, 512),
                             start=True, stop=True).then_inc(sT, 1)
                    SP.mark("sT", f"dmm_{u}")
                    e.wait_ge(sV, SP.at("sV", f"rdr_{u}"))
                    e.matmul(pd[:], av(P3_ONE, 128)[0:1, :], av(P3_RD, 512)[0:1, :],
                             start=True, stop=True).then_inc(sT, 1)
                    SP.mark("sT", f"bmm_{u}")

            for n, (hl, s, b, ki) in enumerate(KT):
                if n - LB >= 0:
                    pv_op(n - LB)
                base = P3_KVQ + b * 12288
                kb = av(base, 4096)
                qs = av(base + 8192, 4096)[:, s * 512:(s + 1) * 512]
                if ki == 0:
                    W_(e, f"sKV{b}", f"kvq_{hl}_{b}")
                if n >= 6:
                    e.wait_ge(sV, SP.at("sV", f"sb_{n - 6}"))
                elif n < 2:
                    e.wait_ge(sV, SP.at("sV", f"tev_{NTR - 2 + n}"))
                else:
                    e.wait_ge(sV, SP.at("sV", f"fev_1_7_{n - 2}"))
                e.matmul(p3pool[n % 6][:], kb[:, ki * 128:(ki + 1) * 128], qs,
                         start=True, stop=True).then_inc(sT, 1)
                SP.mark("sT", f"smm_{n}")
            for m in range(NKT - LB, NKT):
                pv_op(m)

        def emit_p4_tensor(e):
            for bbp in range(B):
                W_(e, f"sKV{bbp}", f"old_{bbp}")
                for mc in range(8):
                    for ch in range(32):
                        i = (bbp * 8 + mc) * 32 + ch
                        e.wait_ge(sV, SP.at("sV", f"wpsp_{i}"))
                        if ch == 0:
                            for tt in range(4):
                                iv = (bbp * 8 + mc) * 4 + tt
                                if iv >= 4:
                                    e.wait_ge(sV, SP.at("sV", f"pe_{iv - 4}"))
                        wh = whl[:, (i % 2) * 512:(i % 2 + 1) * 512]
                        wl = whl[:, 1024 + (i % 2) * 512:1024 + (i % 2 + 1) * 512]
                        first, last = ch == 0, ch == 31
                        for tt in range(4):
                            co = ch * 512 + tt * 128
                            oh = A.bitcast(BF16)[:, co:co + 128]
                            ol = A.bitcast(BF16)[:, 16384 + co:16384 + co + 128]
                            e.matmul(quad[tt][:], oh, wh, start=first, stop=False)
                            e.matmul(quad[tt][:], oh, wl, start=False, stop=False)
                            mm = e.matmul(quad[tt][:], ol, wh, start=False, stop=last)
                            if last:
                                mm.then_inc(sT, 1)
                                SP.mark("sT", f"ymm_{(bbp * 8 + mc) * 4 + tt}")
                        if not last:
                            mm.then_inc(sT, 1)
                        SP.mark("sT", f"pmw_{i}")

        SP.begin(1)
        emit_sync(MockEng(SP, SEMS))
        emit_gpsimd(MockEng(SP, SEMS, auto="sGP"))
        emit_scalar(MockEng(SP, SEMS, auto="sS"))
        emit_vector(MockEng(SP, SEMS, auto="sV"))
        emit_tensor(MockEng(SP, SEMS))

        SP.begin(2)

        AUTOSEM = {"sGP": sGP, "sS": sS, "sV": sV}

        def wrap(eng, auto=None):
            ah = AUTOSEM.get(auto)
            state = {"pend": None}

            def resolve():
                p = state["pend"]
                if p is not None:
                    state["pend"] = None
                    if not p[1][0]:
                        SP.inc(auto, 1)
                        p[0].then_inc(ah, 1)

            class W:
                def __getattr__(self2, name):
                    if name == "wait_ge":
                        def wg(sem, val):
                            resolve()
                            if val > 0:
                                eng.wait_ge(sem, val)
                        return wg
                    meth = getattr(eng, name)

                    def call(*a, **k):
                        resolve()
                        if ah is not None and name not in _NOCHAIN:
                            cnt = SP.counts.get(auto, 0)
                            if cnt > 0:
                                eng.wait_ge(ah, cnt)
                        inst = meth(*a, **k)
                        had = [False]
                        if ah is not None and name not in _NOCHAIN:
                            state["pend"] = (inst, had)

                        class Chain:
                            def then_inc(self3, sem, n=1):
                                had[0] = True
                                SP.inc(SEMS[id(sem)], n)
                                inst.then_inc(sem, n)
                                return self3
                        return Chain()
                    return call
            return W()

        block.sync(lambda eng: emit_sync(wrap(eng)))
        block.gpsimd(lambda eng: emit_gpsimd(wrap(eng, "sGP")))
        block.scalar(lambda eng: emit_scalar(wrap(eng, "sS")))
        block.vector(lambda eng: emit_vector(wrap(eng, "sV")))
        block.tensor(lambda eng: emit_tensor(wrap(eng)))

    return nc


def kernel(L0_attn_hidden_states, L0_attn_w_attn, L0_attn_b_attn,
           L0_attn_w_proj, L0_attn_b_proj, L0_ffn_weight):
    x = np.ascontiguousarray(np.asarray(L0_attn_hidden_states, dtype=np.float32))
    w_attn = np.ascontiguousarray(np.asarray(L0_attn_w_attn, dtype=np.float32))
    b_attn = np.ascontiguousarray(np.asarray(L0_attn_b_attn, dtype=np.float32)).reshape(1, -1)
    w_proj = np.ascontiguousarray(np.asarray(L0_attn_w_proj, dtype=np.float32))
    b_proj = np.ascontiguousarray(np.asarray(L0_attn_b_proj, dtype=np.float32)).reshape(1, -1)
    w_ffn = np.ascontiguousarray(np.asarray(L0_ffn_weight, dtype=np.float32))

    nc = build_nc(tuple(DEBUG_OUTS), PHASES)
    in_maps = []
    for c in range(NCORE):
        xr = np.concatenate([x[0, 512 * c:512 * c + 512, :],
                             x[1, 512 * c:512 * c + 512, :]], axis=0)
        idx = np.zeros((128, 8 * NIDX + 64), dtype=np.int32)
        p = np.arange(128, dtype=np.int64)
        for bbp in range(B):
            for ch in range(32):
                idx[:, 8 * NIDX + bbp * 32 + ch] = \
                    (ch // 4) * 2048 + bbp * 1024 + c + 8 * p
        for hl in range(4):
            for b in range(B):
                u = hl * B + b
                for r in range(NCORE):
                    base = r * (NCORE * KVQ) + c * KVQ
                    idx[:, u * NIDX + r] = base // 512 + p
                    idx[:, u * NIDX + 8 + r] = (base + 2 * Dh * 512) // 512 + p
                    for kt in range(4):
                        idx[:, u * NIDX + 16 + r * 4 + kt] = \
                            (base + Dh * 512) // 128 + kt * 128 + p
        in_maps.append({
            "xrows": np.ascontiguousarray(xr),
            "w_attn": w_attn,
            "w_proj": w_proj,
            "w_ffn": np.ascontiguousarray(w_ffn[512 * c:512 * c + 512, :]),
            "b_attn": b_attn,
            "b_attn_t": np.ascontiguousarray(b_attn.reshape(96, 128).T),
            "b_proj": b_proj,
            "consts": _consts_block(c),
            "a1idx": idx,
        })
    res = run_bass_kernel_spmd(nc, in_maps, core_ids=list(range(NCORE)))
    outp = np.empty((2, 4096, 4096), dtype=np.float32)
    for c in range(NCORE):
        o = res.results[c]["out"]
        outp[0, 512 * c:512 * c + 512, :] = o[0:512]
        outp[1, 512 * c:512 * c + 512, :] = o[512:1024]
    if DEBUG_OUTS:
        kernel.debug = [{k: np.asarray(res.results[c]["dbg_" + k]) for k in DEBUG_OUTS}
                        for c in range(NCORE)]
    return outp


# version 07
# speedup vs baseline: 1.1305x; 1.0847x over previous
import math
import numpy as np
import ml_dtypes
import concourse.bass as bass
import concourse.mybir as mybir
from concourse.bass_utils import run_bass_kernel_spmd

F32 = mybir.dt.float32
BF16 = mybir.dt.bfloat16
I32 = mybir.dt.int32
Alu = mybir.AluOpType
Act = mybir.ActivationFunctionType
AX = mybir.AxisListType.X

B, S, H, Dh, DM = 2, 4096, 32, 128, 4096
NCORE = 8
NF = DM // 128
SCALE = 1.0 / Dh
M_BOUND = 16.0
QMAX = 127.0
EPS = 1e-5
BIG_NEG = -3.0e38
C23 = 1.5 * 2.0 ** 23
C52 = 1.5 * 2.0 ** 52

TABW = 4608
AW = 4480
KVQ = 3 * Dh * 512
A1BLK = 4 * B * KVQ
A1SZ = NCORE * A1BLK
NKS = [4 * (s + 1) for s in range(8)]
YCH = 8 * 1024 * 1024
YMC = 4 * 1024 * 1024
NIDX = 48

DEBUG_OUTS = []
PHASES = 5


def _slopes_f32():
    base = 2 ** (-(2 ** (-(math.log2(H) - 3))))
    b32 = float(np.float32(base))
    return np.float32(np.cumprod(np.full(H, b32, dtype=np.float64)))


def _consts_block(core_id):
    c = np.zeros((128, 132), dtype=np.float32)
    c[:, 0:128] = np.eye(128, dtype=np.float32)
    c[:, 128] = 1.0
    c[:, 129] = np.float32(C23)
    sl = _slopes_f32()
    for p in range(32):
        hl, s = p // 8, p % 8
        c[p, 130] = sl[4 * core_id + hl]
        c[p, 131] = np.float32(511 + s * 512)
    return c


class SemPlan:
    def __init__(self):
        self.counts = {}
        self.marks = {}
        self.pass_no = 0

    def begin(self, n):
        self.pass_no = n
        self.counts = {}

    def inc(self, name, n):
        self.counts[name] = self.counts.get(name, 0) + n
        return self.counts[name]

    def mark(self, name, label):
        key = (name, label)
        cur = self.counts.get(name, 0)
        if self.pass_no == 1:
            assert key not in self.marks, f"duplicate mark {key}"
            self.marks[key] = cur
        else:
            assert self.marks.get(key) == cur, (key, self.marks.get(key), cur)

    def at(self, name, label):
        if self.pass_no == 1:
            return 0
        return self.marks[(name, label)]


_NOCHAIN = ("dma_start", "indirect_dma_start", "collective_compute", "wait_ge")


class MockInst:
    def __init__(self, plan, semmap):
        self.plan = plan
        self.semmap = semmap
        self.had_inc = False

    def then_inc(self, sem, n=1):
        self.had_inc = True
        self.plan.inc(self.semmap[id(sem)], n)
        return self


class MockEng:

    def __init__(self, plan, semmap, auto=None):
        self._plan = plan
        self._semmap = semmap
        self._auto = auto
        self._pend = None

    def _resolve(self):
        if self._pend is not None:
            mi = self._pend
            self._pend = None
            if not mi.had_inc:
                self._plan.inc(self._auto, 1)

    def __getattr__(self, name):
        if name.startswith("_"):
            raise AttributeError(name)
        if name == "wait_ge":
            def wgm(*a, **k):
                self._resolve()
            return wgm

        def call(*a, **k):
            self._resolve()
            mi = MockInst(self._plan, self._semmap)
            if self._auto is not None and name not in _NOCHAIN:
                self._pend = mi
            return mi
        return call


TRJOBS = [("wq", i // NF, i % NF) for i in range(128)]
for _b in range(B):
    for _tt in range(4):
        TRJOBS += [("xn", _b, _tt, fc) for fc in range(NF)]
        TRJOBS += [("xq", _b, _tt, fc) for fc in range(NF)]
NTR = len(TRJOBS)

KT = []
UNIT_OF = {}
KT_LAST = {}
for _hl in range(4):
    for _s in range(8):
        for _b in range(B):
            UNIT_OF[(_hl, _s, _b)] = len(UNIT_OF)
            for _ki in range(NKS[_s]):
                KT.append((_hl, _s, _b, _ki))
                KT_LAST[(_hl, _s, _b)] = len(KT) - 1
NKT = len(KT)
UNITS_SEQ = list(UNIT_OF.keys())

QUSERS = []
for _b in range(B):
    QUSERS += [("proj", _b, ui) for ui in range(24)]
    QUSERS += [("ffn", _b, jc) for jc in range(8)]
QIDX = {u: i for i, u in enumerate(QUSERS)}

PUNITS = [("K", g) for g in range(8)] + [("V", g) for g in range(8)] + \
         [("Q", g) for g in range(8)]


def build_nc(debug_outs=(), phases=5):
    nc = bass.Bass("TRN2", target_bir_lowering=False)

    xrows = nc.dram_tensor("xrows", [1024, DM], F32, kind="ExternalInput").ap()
    w_attn = nc.dram_tensor("w_attn", [DM, 3 * DM], F32, kind="ExternalInput").ap()
    w_proj = nc.dram_tensor("w_proj", [DM, DM], F32, kind="ExternalInput").ap()
    w_ffn = nc.dram_tensor("w_ffn", [512, DM], F32, kind="ExternalInput").ap()
    b_attn = nc.dram_tensor("b_attn", [1, 3 * DM], F32, kind="ExternalInput").ap()
    b_attn_t = nc.dram_tensor("b_attn_t", [128, 96], F32, kind="ExternalInput").ap()
    b_proj = nc.dram_tensor("b_proj", [1, DM], F32, kind="ExternalInput").ap()
    consts = nc.dram_tensor("consts", [128, 132], F32, kind="ExternalInput").ap()
    out = nc.dram_tensor("out", [1024, DM], F32, kind="ExternalOutput").ap()

    a1_in = nc.dram_tensor("a1_in", [1, A1BLK * NCORE], F32, kind="Internal").ap()
    a1o = [nc.dram_tensor(f"a1o_{u}", [NCORE, NCORE * KVQ], F32, kind="Internal",
                          addr_space="Shared").ap()
           for u in range(8)]
    a1idx = nc.dram_tensor("a1idx", [128, 8 * NIDX + 64], I32, kind="ExternalInput").ap()
    o_scr = nc.dram_tensor("o_scr", [1, 4 * B * Dh * S], BF16, kind="Internal").ap()
    o_scr_lo = nc.dram_tensor("o_scr_lo", [1, 4 * B * Dh * S], BF16, kind="Internal").ap()
    o_gp = [nc.dram_tensor(f"o_gp_{h}", [NCORE, B * Dh * S], BF16, kind="Internal",
                           addr_space="Shared").ap()
            for h in range(4)]
    o_gpl = [nc.dram_tensor(f"o_gpl_{h}", [NCORE, B * Dh * S], BF16, kind="Internal",
                            addr_space="Shared").ap()
             for h in range(4)]
    g1_in = nc.dram_tensor("g1_in", [1, DM * 512], BF16, kind="Internal").ap()
    g1_out = nc.dram_tensor("g1_out", [NCORE, DM * 512], BF16, kind="Internal",
                            addr_space="Shared").ap()
    g1s_in = nc.dram_tensor("g1s_in", [1, 512], F32, kind="Internal").ap()
    g1s_out = nc.dram_tensor("g1s_out", [NCORE, 512], F32, kind="Internal",
                             addr_space="Shared").ap()
    htab = nc.dram_tensor("htab", [32, TABW], F32, kind="Internal").ap()
    ffnscr = nc.dram_tensor("ffnscr", [1024, DM], F32, kind="Internal").ap()

    dbg = {}
    shapes = {"a1_in": ([1, A1BLK * NCORE], F32),
              "htab": ([32, TABW], F32), "ffnscr": ([1024, DM], F32),
              "g1_out": ([NCORE, DM * 512], BF16),
              "g1s_out": ([NCORE, 512], F32),
              "o_scr": ([1, 4 * B * Dh * S], BF16),
              "o_scr_lo": ([1, 4 * B * Dh * S], BF16),
              "p3kvq": ([128, 24576], F32), "p3tail": ([128, 16384], F32)}
    src_map = {"a1_in": a1_in, "htab": htab, "ffnscr": ffnscr,
               "g1_out": g1_out, "g1s_out": g1s_out, "o_scr": o_scr,
               "o_scr_lo": o_scr_lo}
    for name in debug_outs:
        shp, dt = shapes[name]
        dbg[name] = nc.dram_tensor("dbg_" + name, shp, dt, kind="ExternalOutput").ap()

    import contextlib
    with contextlib.ExitStack() as _st:
        A = _st.enter_context(nc.sbuf_tensor("A", [128, 48128], F32))
        wbuf = _st.enter_context(nc.sbuf_tensor("wbuf", [128, 2 * 512], F32))
        whl = _st.enter_context(nc.sbuf_tensor("whl", [128, 4 * 512], BF16))
        cst = _st.enter_context(nc.sbuf_tensor("cst", [128, 132], F32))
        cols = _st.enter_context(nc.sbuf_tensor("cols", [128, 32], F32))
        bcol = _st.enter_context(nc.sbuf_tensor("bcol", [128, 96], F32))
        wqs = _st.enter_context(nc.sbuf_tensor("wqs", [128, 4 * 512], BF16))
        wtr = _st.enter_context(nc.sbuf_tensor("wtr", [128, 2 * 128], BF16))
        itmp = _st.enter_context(nc.sbuf_tensor("itmp", [128, 1536], I32))
        pq0 = _st.enter_context(nc.psum_tensor("pq0", [128, 512], F32))
        pq1 = _st.enter_context(nc.psum_tensor("pq1", [128, 512], F32))
        pq2 = _st.enter_context(nc.psum_tensor("pq2", [128, 512], F32))
        pq3 = _st.enter_context(nc.psum_tensor("pq3", [128, 512], F32))
        ps0 = _st.enter_context(nc.psum_tensor("ps0", [128, 512], F32))
        ps1 = _st.enter_context(nc.psum_tensor("ps1", [128, 512], F32))
        po = _st.enter_context(nc.psum_tensor("po", [128, 512], F32))
        pd = _st.enter_context(nc.psum_tensor("pd", [128, 512], F32))
        DMA_SEMS = ["sP0", "sX0", "sX1", "sW0", "sW1", "sWQ0", "sWQ1", "sWQ2", "sWQ3",
                    "sWX0", "sWX1", "sAS0", "sAS1", "sKV0", "sKV1", "sOL",
                    "sBP", "sWP0", "sWP1", "sP50", "sP51",
                    "sWT0", "sWT1", "sG1S", "sHT", "sST0", "sST1",
                    "sOS", "sOS1", "sY0", "sY1", "sY2", "sY3", "sO0", "sO1", "sDBG"]
        SEM = {nm: _st.enter_context(nc.semaphore(name=nm)) for nm in DMA_SEMS}
        sGP = _st.enter_context(nc.semaphore(name="sGP"))
        sV = _st.enter_context(nc.semaphore(name="sV"))
        sS = _st.enter_context(nc.semaphore(name="sS"))
        sT = _st.enter_context(nc.semaphore(name="sT"))
        block = _st.enter_context(nc.Block())
        quad = [pq0, pq1, pq2, pq3]
        psc = [ps0, ps1]
        p3pool = [ps0, ps1, pq0, pq1, pq2, pq3]
        SEMS = {id(h): nm for nm, h in SEM.items()}
        SEMS.update({id(sGP): "sGP", id(sV): "sV", id(sS): "sS", id(sT): "sT"})

        def W_(e, nm, label):
            e.wait_ge(SEM[nm], SP.at(nm, label))
        SP = SemPlan()

        STORE_SEMS = ["sWT0", "sWT1", "sG1S", "sHT", "sST0", "sST1",
                      "sOS", "sOS1", "sY0", "sY1", "sY2", "sY3", "sO0", "sO1"]

        def mark_final():
            for nm in STORE_SEMS:
                SP.mark(nm, "fin")

        def wait_final(e):
            for nm in STORE_SEMS:
                e.wait_ge(SEM[nm], SP.at(nm, "fin"))

        def av(off, n):
            return A[:, off:off + n]

        def avb(off, nbf):
            return A.bitcast(BF16)[:, 2 * off:2 * off + nbf]

        TB_D, TB_V = 16384, 21120
        ITAB = 25728
        O_XT, O_XC, O_XNT, O_XNT_LO, O_XQT = 0, 8192, 12288, 20480, 28672
        O_EV, O_R1, O_WSX = 36864, 40960, 46592
        P3_KVQ, P3_A, P3_P, P3_SB = 0, 24576, 33536, 38656
        P3_DA, P3_RD, P3_OE, P3_ONE = 45056, 45568, 36608, 47616
        P4_OH, P4_OL, P4_WH, P4_WL = 0, 16384, 32768, 33792
        P4_EV = 34816
        P5_Y, P5_X, P5_F, P5_EV = 36864, 37888, 38912, 39936

        CL_FLAG, CL_CALL, CL_C52, CL_M, CL_T0, CL_R, CL_R2, CL_RA = 0, 1, 2, 3, 4, 5, 6, 7
        CL_AS = 8
        CL_WS = 16
        CL_WR = 20

        def col(i):
            return cols[:, i:i + 1]

        ident = cst[:, 0:128]
        onescol = cst[:, 128:129]

        def a1_off(dst, hl, b, part):
            return (hl * B + b) * (NCORE * KVQ) + dst * KVQ + part * Dh * 512

        def a2_off(dst, hl, b, which):
            return dst * (4 * B * OBLK) + (hl * B + b) * OBLK + which * Dh * 512

        def xnt_hi(fc, lo=0, n=512):
            return avb(O_XNT, 16384)[:, fc * 512 + lo:fc * 512 + lo + n]

        def xnt_lo(fc, lo=0, n=512):
            return avb(O_XNT_LO, 16384)[:, fc * 512 + lo:fc * 512 + lo + n]

        def xqt(fc, tt):
            return avb(O_XQT, 16384)[:, fc * 512 + tt * 128:fc * 512 + tt * 128 + 128]

        def emit_sync(e):
            e.dma_start(out=cst[:], in_=consts[:, :]).then_inc(SEM["sP0"], 16)
            e.dma_start(out=bcol[:], in_=b_attn_t[:, :]).then_inc(SEM["sP0"], 16)
            for rt in range(4):
                e.dma_start(out=av(O_XT + rt * 4096, 4096),
                            in_=w_ffn[rt * 128:(rt + 1) * 128, :]).then_inc(SEM["sP0"], 16)
            e.dma_start(out=itmp[:, 1024:1024 + 8 * NIDX + 64], in_=a1idx[:, :]).then_inc(SEM["sP0"], 16)
            e.dma_start(out=av(O_R1, 4096),
                        in_=bass.AP(tensor=b_attn.tensor, offset=2 * DM,
                                    ap=[[0, 128], [1, DM]])).then_inc(SEM["sP0"], 16)
            SP.mark("sP0", "p0")
            for b in range(B):
                for tt in range(4):
                    it = b * 4 + tt
                    if it < 2:
                        e.wait_ge(sT, SP.at("sT", "ttr_127"))
                    else:
                        pb_, pt = divmod(it - 2, 4)
                        e.wait_ge(sT, SP.at("sT", f"trq_{pb_}_{pt}"))
                    e.dma_start(out=av(O_XT + (it % 2) * 4096, 4096),
                                in_=xrows[b * 512 + tt * 128: b * 512 + (tt + 1) * 128, :]
                                ).then_inc(SEM[f"sX{it % 2}"], 16)
                    SP.mark(f"sX{it % 2}", f"x_{b}_{tt}")
                for ui, (kind, g) in enumerate(PUNITS):
                    base_col = {"Q": 0, "K": DM, "V": 2 * DM}[kind] + g * 512
                    for fc in range(NF):
                        i = (b * 24 + ui) * NF + fc
                        if i >= 2:
                            j = i - 2
                            pb_, rem = divmod(j, 24 * NF)
                            pui, pfc = divmod(rem, NF)
                            e.wait_ge(sV, SP.at("sV", f"wsp_{pb_}_{pui}_{pfc}"))
                        e.dma_start(out=wbuf[:, (i % 2) * 512:(i % 2 + 1) * 512],
                                    in_=w_attn[fc * 128:(fc + 1) * 128,
                                               base_col:base_col + 512]).then_inc(SEM[f"sW{i % 2}"], 16)
                        SP.mark(f"sW{i % 2}", f"w_{b}_{ui}_{fc}")
                if b == 0:
                    e.wait_ge(sGP, SP.at("sGP", "g1"))
                for jc in range(8):
                    i = b * 8 + jc
                    if i >= 2:
                        j = i - 2
                        e.wait_ge(sV, SP.at("sV", f"fevl_{j // 8}_{j % 8}"))
                    e.dma_start(out=av(O_WSX + (i % 2) * 512, 512),
                                in_=bass.AP(tensor=g1s_out.tensor, offset=jc * 512,
                                            ap=[[0, 128], [1, 512]])).then_inc(SEM[f"sWX{i % 2}"], 16)
                    SP.mark(f"sWX{i % 2}", f"wsx_{b}_{jc}")
                    for fc in range(NF):
                        iw = (b * 8 + jc) * NF + fc
                        if iw >= 4:
                            jw = iw - 4
                            pb_, rem = divmod(jw, 8 * NF)
                            pjc, pfc = divmod(rem, NF)
                            e.wait_ge(sT, SP.at("sT", f"fmm_{pb_}_{pjc}_{pfc}"))
                        e.dma_start(out=wqs[:, (iw % 4) * 512:(iw % 4 + 1) * 512],
                                    in_=bass.AP(tensor=g1_out.tensor,
                                                offset=jc * (DM * 512) + fc * 128 * 512,
                                                ap=[[512, 128], [1, 512]])).then_inc(SEM[f"sWQ{iw % 4}"], 16)
                        SP.mark(f"sWQ{iw % 4}", f"wq_{b}_{jc}_{fc}")
            for hl in range(4 if phases >= 3 else 0):
                for s in range(8):
                    if s >= 2:
                        e.wait_ge(sV, SP.at("sV", f"sb_{KT_LAST[(hl, s - 2, 1)]}"))
                    elif hl > 0:
                        e.wait_ge(sV, SP.at("sV", f"sb_{KT_LAST[(hl - 1, 6 + s, 1)]}"))
                    else:
                        e.wait_ge(sT, SP.at("sT", "fgrp_1_7"))
                        W_(e, "sHT", "htabst")
                    e.dma_start(out=av(P3_A + (s % 2) * AW, AW),
                                in_=bass.AP(tensor=htab.tensor,
                                            offset=(hl * 8 + s) * TABW,
                                            ap=[[1, 128], [1, AW]])).then_inc(SEM[f"sAS{s % 2}"], 16)
                    SP.mark(f"sAS{s % 2}", f"astrip_{hl}_{s}")
            e.wait_ge(sV, SP.at("sV", "r1free"))
            if phases < 4:
                wait_final(e)
                return
            e.dma_start(out=av(O_R1, 4096),
                        in_=bass.AP(tensor=b_proj.tensor, offset=0,
                                    ap=[[0, 128], [1, DM]])).then_inc(SEM["sBP"], 16)
            SP.mark("sBP", "bproj")
            first_tiles = True
            for bbp in range(B):
                for mc in range(8):
                    for ch in range(32):
                        i = (bbp * 8 + mc) * 32 + ch
                        if i >= 2:
                            e.wait_ge(sV, SP.at("sV", f"wpsp_{i - 2}"))
                        e.dma_start(out=wbuf[:, (i % 2) * 512:(i % 2 + 1) * 512],
                                    in_=w_proj[ch * 128:(ch + 1) * 128,
                                               mc * 512:(mc + 1) * 512]).then_inc(SEM[f"sWP{i % 2}"], 16)
                        SP.mark(f"sWP{i % 2}", f"wp_{i}")
                    for tt in range(4):
                        iv = (bbp * 8 + mc) * 4 + tt
                        if first_tiles:
                            e.wait_ge(sS, SP.at("sS", f"exp_{NKT - 1}"))
                            first_tiles = False
                        if iv >= 2:
                            e.wait_ge(sV, SP.at("sV", f"pe_{iv - 2}"))
                        row0 = bbp * 512 + tt * 128
                        e.dma_start(out=av(P5_X + (iv % 2) * 512, 512),
                                    in_=xrows[row0:row0 + 128, mc * 512:(mc + 1) * 512]).then_inc(SEM[f"sP5{iv % 2}"], 16)
                        e.dma_start(out=av(P5_F + (iv % 2) * 512, 512),
                                    in_=ffnscr[row0:row0 + 128, mc * 512:(mc + 1) * 512]).then_inc(SEM[f"sP5{iv % 2}"], 16)
                        SP.mark(f"sP5{iv % 2}", f"p4in_{iv}")
            wait_final(e)

        def emit_gpsimd(e):
            W_(e, "sP0", "p0")
            e.iota(out=A.bitcast(I32)[0:32, ITAB:ITAB + TABW], pattern=[[1, TABW]], base=0, channel_multiplier=0)
            e.tensor_copy(out=av(TB_D, TABW)[0:32, :], in_=A.bitcast(I32)[0:32, ITAB:ITAB + TABW]).then_inc(sGP, 1)
            SP.mark("sGP", "iota")
            W_(e, "sWT0", "wqst_126")
            W_(e, "sWT1", "wqst_127")
            W_(e, "sG1S", "g1ready")
            e.collective_compute("AllGather", Alu.bypass,
                                 replica_groups=[list(range(NCORE))],
                                 ins=[g1_in[:, :]], outs=[g1_out[:, :]]).then_inc(sGP, 1)
            e.collective_compute("AllGather", Alu.bypass,
                                 replica_groups=[list(range(NCORE))],
                                 ins=[g1s_in[:, :]], outs=[g1s_out[:, :]]).then_inc(sGP, 1)
            SP.mark("sGP", "g1")
            if phases >= 2:
                for b in range(B):
                    W_(e, "sST0", f"st_{b}_22_3")
                    W_(e, "sST1", f"st_{b}_23_3")
                    for h in range(4):
                        u = h * B + b
                        e.collective_compute("AllGather", Alu.bypass,
                                             replica_groups=[list(range(NCORE))],
                                             ins=[a1_in[:, u * (NCORE * KVQ):(u + 1) * (NCORE * KVQ)]],
                                             outs=[a1o[u][:, :]]).then_inc(sGP, 1)
                        SP.mark("sGP", f"a1g_{h}_{b}")
            if phases < 3:
                if dbg:
                    wait_final(e)
                    SP.mark("sGP", "gpend")
                    e.wait_ge(sGP, SP.at("sGP", "gpend"))
                    for name, ap_out in dbg.items():
                        sap = A[:, 0:24576] if name == "p3kvq" else (
                            A[:, 24576:40960] if name == "p3tail" else src_map[name][:, :])
                        e.dma_start(out=ap_out[:, :], in_=sap).then_inc(SEM["sDBG"], 16)
                    SP.mark("sDBG", "dbg")
                    W_(e, "sDBG", "dbg")
                return

            v512s = [a1o[u2].rearrange("a b -> (a b)").rearrange("(n r) -> n r", r=512) for u2 in range(8)]
            v128s = [a1o[u2].rearrange("a b -> (a b)").rearrange("(n r) -> n r", r=128) for u2 in range(8)]
            for hl in range(4):
                for bb in range(B):
                    u = hl * B + bb
                    e.wait_ge(sGP, SP.at("sGP", f"a1g_{hl}_{bb}"))
                    if hl > 0:
                        e.wait_ge(sT, SP.at("sT", f"pv_{KT_LAST[(hl - 1, 7, bb)]}"))
                    elif bb == 0:
                        e.wait_ge(sT, SP.at("sT", "trq_1_3"))
                    else:
                        e.wait_ge(sT, SP.at("sT", "grp_1_23"))
                    base = P3_KVQ + bb * 12288
                    ic0 = 1024 + u * NIDX
                    for r in range(8):
                        e.indirect_dma_start(
                            out=av(base, 4096)[:, r * 512:(r + 1) * 512],
                            out_offset=None, in_=v512s[u][:, :],
                            in_offset=bass.IndirectOffsetOnAxis(
                                ap=itmp[:, ic0 + r:ic0 + r + 1], axis=0)).then_inc(SEM[f"sKV{bb}"], 16)
                        e.indirect_dma_start(
                            out=av(base + 8192, 4096)[:, r * 512:(r + 1) * 512],
                            out_offset=None, in_=v512s[u][:, :],
                            in_offset=bass.IndirectOffsetOnAxis(
                                ap=itmp[:, ic0 + 8 + r:ic0 + 9 + r], axis=0)).then_inc(SEM[f"sKV{bb}"], 16)
                        for kt in range(4):
                            e.indirect_dma_start(
                                out=av(base + 4096, 4096)[:, (r * 4 + kt) * 128:(r * 4 + kt + 1) * 128],
                                out_offset=None, in_=v128s[u][:, :],
                                in_offset=bass.IndirectOffsetOnAxis(
                                    ap=itmp[:, ic0 + 16 + r * 4 + kt:ic0 + 17 + r * 4 + kt],
                                    axis=0)).then_inc(SEM[f"sKV{bb}"], 16)
                    SP.mark(f"sKV{bb}", f"kvq_{hl}_{bb}")
            if phases < 4:
                if dbg:
                    wait_final(e)
                    SP.mark("sGP", "gpend")
                    e.wait_ge(sGP, SP.at("sGP", "gpend"))
                    for name, ap_out in dbg.items():
                        sap = A[:, 0:24576] if name == "p3kvq" else (
                            A[:, 24576:40960] if name == "p3tail" else src_map[name][:, :])
                        e.dma_start(out=ap_out[:, :], in_=sap).then_inc(SEM["sDBG"], 16)
                    SP.mark("sDBG", "dbg")
                    W_(e, "sDBG", "dbg")
                return
            HSZ = B * Dh * S
            for hl in range(4):
                W_(e, "sOS", f"ostu_{hl * 16 + 14}")
                W_(e, "sOS1", f"ostu_{hl * 16 + 15}")
                e.collective_compute("AllGather", Alu.bypass,
                                     replica_groups=[list(range(NCORE))],
                                     ins=[o_scr[:, hl * HSZ:(hl + 1) * HSZ]],
                                     outs=[o_gp[hl][:, :]]).then_inc(sGP, 1)
                e.collective_compute("AllGather", Alu.bypass,
                                     replica_groups=[list(range(NCORE))],
                                     ins=[o_scr_lo[:, hl * HSZ:(hl + 1) * HSZ]],
                                     outs=[o_gpl[hl][:, :]]).then_inc(sGP, 1)
                SP.mark("sGP", f"og_{hl}")
            vgos = [o_gp[h].rearrange("a b -> (a b)").rearrange("(n r) -> n r", r=512) for h in range(4)]
            vgols = [o_gpl[h].rearrange("a b -> (a b)").rearrange("(n r) -> n r", r=512) for h in range(4)]
            for bbp in range(B):
                e.wait_ge(sGP, SP.at("sGP", "og_3"))
                e.wait_ge(sT, SP.at("sT", f"pv_{NKT - 1}" if bbp == 0 else "pmw_255"))
                for ch in range(32):
                    hl = ch % 4
                    icx = 1024 + 8 * NIDX + bbp * 32 + ch
                    e.indirect_dma_start(
                        out=A.bitcast(BF16)[:, ch * 512:(ch + 1) * 512],
                        out_offset=None, in_=vgos[hl][:, :],
                        in_offset=bass.IndirectOffsetOnAxis(
                            ap=itmp[:, icx:icx + 1], axis=0)).then_inc(SEM[f"sKV{bbp}"], 16)
                    e.indirect_dma_start(
                        out=A.bitcast(BF16)[:, 16384 + ch * 512:16384 + (ch + 1) * 512],
                        out_offset=None, in_=vgols[hl][:, :],
                        in_offset=bass.IndirectOffsetOnAxis(
                            ap=itmp[:, icx:icx + 1], axis=0)).then_inc(SEM[f"sKV{bbp}"], 16)
                SP.mark(f"sKV{bbp}", f"old_{bbp}")
            if dbg:
                wait_final(e)
                SP.mark("sGP", "gpend")
                e.wait_ge(sGP, SP.at("sGP", "gpend"))
                for name, ap_out in dbg.items():
                    sap = A[:, 0:24576] if name == "p3kvq" else (
                        A[:, 24576:40960] if name == "p3tail" else src_map[name][:, :])
                    e.dma_start(out=ap_out[:, :], in_=sap).then_inc(SEM["sDBG"], 16)
                SP.mark("sDBG", "dbg")
                W_(e, "sDBG", "dbg")

        def emit_scalar(e):
            for i in range(128):
                rt, fc = divmod(i, NF)
                e.wait_ge(sV, SP.at("sV", f"tev_{i}"))
                e.dma_start(out=bass.AP(tensor=g1_in.tensor,
                                        offset=fc * 128 * 512 + rt * 128,
                                        ap=[[512, 128], [1, 128]]),
                            in_=wtr[:, (i % 2) * 128:(i % 2) * 128 + 128]).then_inc(SEM[f"sWT{i % 2}"], 16)
                SP.mark(f"sWT{i % 2}", f"wqst_{i}")
            e.wait_ge(sV, SP.at("sV", "wscale"))
            for rt in range(4):
                e.dma_start(out=bass.AP(tensor=g1s_in.tensor, offset=rt * 128,
                                        ap=[[1, 128], [1, 1]]),
                            in_=col(CL_WS + rt)).then_inc(SEM["sG1S"], 16)
            SP.mark("sG1S", "g1ready")
            e.wait_ge(sV, SP.at("sV", "htab"))
            e.dma_start(out=htab[:, :], in_=av(TB_V, TABW)[0:32, :]).then_inc(SEM["sHT"], 16)
            SP.mark("sHT", "htabst")
            for b in range(B):
                for tt in range(4):
                    e.wait_ge(sV, SP.at("sV", f"var_{b}_{tt}"))
                    e.activation(out=col(CL_R), in_=col(CL_T0), func=Act.Sqrt).then_inc(sS, 1)
                    SP.mark("sS", f"sqrt_{b}_{tt}")
                for ui, (kind, g) in enumerate(PUNITS):
                    for q4 in range(4):
                        e.wait_ge(sV, SP.at("sV", f"ev_{b}_{ui}_{q4}"))
                        src = av(O_EV + (ui % 2) * 2048 + q4 * 512, 512)
                        if kind in ("K", "Q"):
                            hh = g * 4 + q4
                            part = 0 if kind == "K" else 2
                            e.dma_start(out=bass.AP(tensor=a1_in.tensor,
                                                    offset=a1_off(hh // 4, hh % 4, b, part),
                                                    ap=[[512, 128], [1, 512]]),
                                        in_=src).then_inc(SEM[f"sST{ui % 2}"], 16)
                        else:
                            tt = q4
                            for h4 in range(4):
                                hh = g * 4 + h4
                                e.dma_start(out=bass.AP(tensor=a1_in.tensor,
                                                        offset=a1_off(hh // 4, hh % 4, b, 1) + tt * 128 * Dh,
                                                        ap=[[Dh, 128], [1, 128]]),
                                            in_=src[:, h4 * 128:(h4 + 1) * 128]).then_inc(SEM[f"sST{ui % 2}"], 16)
                        SP.mark(f"sST{ui % 2}", f"st_{b}_{ui}_{q4}")
                for jc in range(8):
                    for tt in range(4):
                        e.wait_ge(sV, SP.at("sV", f"fev_{b}_{jc}_{tt}"))
                        e.dma_start(out=ffnscr[b * 512 + tt * 128: b * 512 + (tt + 1) * 128,
                                               jc * 512:(jc + 1) * 512],
                                    in_=av(O_EV + (jc % 2) * 2048 + tt * 512, 512)
                                    ).then_inc(SEM[f"sST{jc % 2}"], 16)
                        SP.mark(f"sST{jc % 2}", f"fst_{b}_{jc}_{tt}")
            if phases < 3:
                mark_final()
                return
            pend_ostu = []

            def ostu_op(u, hl, s, bb):
                e.wait_ge(sV, SP.at("sV", f"osp_{u}"))
                off = (hl * B + bb) * Dh * S + s * 512
                oeo = P3_OE + (u % 2) * 1024
                osem = "sOS" if u % 2 == 0 else "sOS1"
                e.dma_start(out=bass.AP(tensor=o_scr.tensor, offset=off,
                                        ap=[[S, 128], [1, 512]]),
                            in_=avb(oeo + 512, 512)).then_inc(SEM[osem], 16)
                e.dma_start(out=bass.AP(tensor=o_scr_lo.tensor, offset=off,
                                        ap=[[S, 128], [1, 512]]),
                            in_=avb(oeo + 768, 512)).then_inc(SEM[osem], 16)
                SP.mark(osem, f"ostu_{u}")

            for n, (hl, s, bb, ki) in enumerate(KT):
                while pend_ostu and n >= pend_ostu[0][4] + 8:
                    ostu_op(*pend_ostu.pop(0)[:4])
                e.wait_ge(sV, SP.at("sV", f"sb_{n}"))
                if n >= 6:
                    e.wait_ge(sV, SP.at("sV", f"da_{n - 6}"))
                    e.wait_ge(sT, SP.at("sT", f"pv_{n - 6}"))
                e.activation(out=av(P3_P + (n % 6) * 512, 512),
                             in_=av(P3_SB + (n % 2) * 512, 512),
                             func=Act.Exp).then_inc(sS, 1)
                SP.mark("sS", f"exp_{n}")
                if ki == NKS[s] - 1:
                    pend_ostu.append((UNIT_OF[(hl, s, bb)], hl, s, bb, n))
            for it in pend_ostu:
                ostu_op(*it[:4])
            SP.mark("sOS", "oscr_done")
            SP.mark("sOS1", "oscr_done")
            if phases < 4:
                mark_final()
                return
            for i in range(64):
                bbp, r8 = divmod(i, 32)
                mc, tt = divmod(r8, 4)
                row0 = bbp * 512 + tt * 128
                e.wait_ge(sV, SP.at("sV", f"pe_{i}"))
                e.dma_start(out=out[row0:row0 + 128, mc * 512:(mc + 1) * 512],
                            in_=av(P4_EV + (i % 4) * 512, 512)).then_inc(SEM[f"sY{i % 4}"], 16)
                SP.mark(f"sY{i % 4}", f"p4st_{i}")
            mark_final()

        def fp16_round(e, v):
            t1 = itmp[:, 0:512]
            t1f = itmp.bitcast(F32)[:, 0:512]
            t2 = itmp[:, 512:1024]
            t2f = itmp.bitcast(F32)[:, 512:1024]
            e.tensor_scalar(out=t1, in0=v.bitcast(I32), scalar1=23, scalar2=255,
                            op0=Alu.logical_shift_right, op1=Alu.bitwise_and)
            e.tensor_scalar(out=t2, in0=t1, scalar1=-1, scalar2=150, op0=Alu.mult, op1=Alu.add)
            e.tensor_scalar(out=t2, in0=t2, scalar1=23, scalar2=None, op0=Alu.logical_shift_left)
            e.tensor_tensor(out=v, in0=v, in1=t2f, op=Alu.mult)
            e.tensor_scalar(out=v, in0=v, scalar1=col(CL_C52), scalar2=None, op0=Alu.add)
            e.tensor_scalar(out=v, in0=v, scalar1=col(CL_C52), scalar2=None, op0=Alu.subtract)
            e.tensor_scalar(out=t2, in0=t1, scalar1=104, scalar2=None, op0=Alu.add)
            e.tensor_scalar(out=t2, in0=t2, scalar1=23, scalar2=None, op0=Alu.logical_shift_left)
            e.tensor_tensor(out=v, in0=v, in1=t2f, op=Alu.mult)
            e.tensor_scalar(out=t1, in0=v.bitcast(I32), scalar1=23, scalar2=255,
                            op0=Alu.logical_shift_right, op1=Alu.bitwise_and)
            e.tensor_scalar(out=t1, in0=t1, scalar1=10, scalar2=103,
                            op0=Alu.subtract, op1=Alu.max)
            e.tensor_scalar(out=t2, in0=t1, scalar1=-1, scalar2=254, op0=Alu.mult, op1=Alu.add)
            e.tensor_scalar(out=t2, in0=t2, scalar1=23, scalar2=None, op0=Alu.logical_shift_left)
            e.tensor_scalar(out=t1, in0=t1, scalar1=23, scalar2=None, op0=Alu.logical_shift_left)
            e.tensor_tensor(out=v, in0=v, in1=t2f, op=Alu.mult)
            e.tensor_scalar(out=v, in0=v, scalar1=col(CL_CALL), scalar2=None, op0=Alu.add)
            e.tensor_scalar(out=v, in0=v, scalar1=col(CL_CALL), scalar2=None, op0=Alu.subtract)
            e.tensor_tensor(out=v, in0=v, in1=t1f, op=Alu.mult)

        def emit_vector(e):
            W_(e, "sP0", "p0")
            e.tensor_scalar(out=col(CL_FLAG), in0=onescol, scalar1=float(2.0 ** -30),
                            scalar2=1.0, op0=Alu.add, op1=Alu.is_gt)
            e.tensor_scalar(out=col(CL_C52), in0=col(CL_FLAG), scalar1=float(C52),
                            scalar2=None, op0=Alu.mult)
            e.tensor_scalar(out=col(CL_CALL), in0=col(CL_FLAG), scalar1=-1.0,
                            scalar2=1.0, op0=Alu.mult, op1=Alu.add)
            e.tensor_tensor(out=col(CL_CALL), in0=col(CL_CALL), in1=cst[:, 129:130], op=Alu.mult)
            e.tensor_tensor(out=col(CL_CALL), in0=col(CL_CALL), in1=col(CL_C52),
                            op=Alu.add)
            e.memset(av(P3_ONE, 128)[0:1, :], 1.0).then_inc(sV, 1)
            SP.mark("sV", "setup")
            e.wait_ge(sV, SP.at("sV", "setup"))
            W_(e, "sP0", "p0")
            for rt in range(4):
                e.tensor_reduce(out=col(CL_WS + rt), in_=av(O_XT + rt * 4096, 4096),
                                axis=AX, op=Alu.max, apply_absolute_value=True)
            e.tensor_scalar(out=cols[:, CL_WS:CL_WS + 4], in0=cols[:, CL_WS:CL_WS + 4],
                            scalar1=float(1.0 / QMAX), scalar2=None, op0=Alu.mult).then_inc(sV, 1)
            SP.mark("sV", "wscale")
            e.wait_ge(sV, SP.at("sV", "wscale"))
            e.reciprocal(out=cols[:, CL_WR:CL_WR + 4], in_=cols[:, CL_WS:CL_WS + 4]).then_inc(sV, 1)
            SP.mark("sV", "wsrec")
            e.wait_ge(sV, SP.at("sV", "wsrec"))
            for rt in range(4):
                wt = av(O_XT + rt * 4096, 4096)
                e.tensor_scalar(out=wt, in0=wt, scalar1=col(CL_WR + rt), scalar2=None, op0=Alu.mult)
                e.tensor_scalar(out=wt, in0=wt, scalar1=col(CL_CALL), scalar2=None, op0=Alu.add)
                e.tensor_scalar(out=wt, in0=wt, scalar1=col(CL_CALL), scalar2=None,
                                op0=Alu.subtract).then_inc(sV, 1)
            SP.mark("sV", "wqvals")
            e.wait_ge(sGP, SP.at("sGP", "iota"))
            td = av(TB_D, TABW)[0:32, :]
            tv = av(TB_V, TABW)[0:32, :]
            e.tensor_scalar(out=td, in0=td, scalar1=cst[0:32, 131:132], scalar2=None,
                            op0=Alu.subtract)
            e.tensor_scalar(out=tv, in0=td, scalar1=cst[0:32, 130:131], scalar2=None,
                            op0=Alu.mult).then_inc(sV, 1)
            SP.mark("sV", "rawbias")
            e.wait_ge(sV, SP.at("sV", "rawbias"))
            ib = A.bitcast(I32)[0:32, ITAB:ITAB + TABW]
            ibf = av(ITAB, TABW)[0:32, :]
            e.tensor_scalar(out=ib, in0=tv.bitcast(I32), scalar1=23, scalar2=255,
                            op0=Alu.logical_shift_right, op1=Alu.bitwise_and)
            e.tensor_scalar(out=ib, in0=ib, scalar1=-1, scalar2=150, op0=Alu.mult, op1=Alu.add)
            e.tensor_scalar(out=ib, in0=ib, scalar1=23, scalar2=None, op0=Alu.logical_shift_left)
            e.tensor_tensor(out=tv, in0=tv, in1=ibf, op=Alu.mult)
            e.tensor_scalar(out=tv, in0=tv, scalar1=col(CL_C52)[0:32, :], scalar2=None, op0=Alu.add)
            e.tensor_scalar(out=tv, in0=tv, scalar1=col(CL_C52)[0:32, :], scalar2=None,
                            op0=Alu.subtract)
            e.tensor_scalar(out=ib, in0=ib, scalar1=23, scalar2=None, op0=Alu.logical_shift_right)
            e.tensor_scalar(out=ib, in0=ib, scalar1=-1, scalar2=254, op0=Alu.mult, op1=Alu.add)
            e.tensor_scalar(out=ib, in0=ib, scalar1=23, scalar2=None, op0=Alu.logical_shift_left)
            e.tensor_tensor(out=tv, in0=tv, in1=ibf, op=Alu.mult)
            e.tensor_scalar(out=tv, in0=tv, scalar1=-M_BOUND, scalar2=None, op0=Alu.add)
            e.tensor_scalar(out=td, in0=td, scalar1=0.0, scalar2=None, op0=Alu.is_gt)
            e.scalar_tensor_tensor(out=tv, in0=td, scalar=BIG_NEG, in1=tv,
                                   op0=Alu.mult, op1=Alu.add).then_inc(sV, 1)
            SP.mark("sV", "htab")
            for g, job in enumerate(TRJOBS):
                if job[0] == "wq":
                    e.wait_ge(sT, SP.at("sT", f"ttr_{g}"))
                    if g >= 2:
                        W_(e, f"sWT{g % 2}", f"wqst_{g - 2}")
                    e.tensor_copy(out=wtr[:, (g % 2) * 128:(g % 2) * 128 + 128],
                                  in_=psc[g % 2][:, 0:128]).then_inc(sV, 1)
                    SP.mark("sV", f"tev_{g}")
                elif job[0] == "xn":
                    _, b, tt, fc = job
                    if fc == 0:
                        emit_ln_tile(e, b, tt)
                    e.wait_ge(sT, SP.at("sT", f"ttr_{g}"))
                    if g == 128:
                        W_(e, "sHT", "htabst")
                    hi = xnt_hi(fc, tt * 128, 128)
                    lo = xnt_lo(fc, tt * 128, 128)
                    e.tensor_copy(out=hi, in_=psc[g % 2][:, 0:128])
                    e.tensor_tensor(out=lo, in0=psc[g % 2][:, 0:128], in1=hi,
                                    op=Alu.subtract).then_inc(sV, 1)
                    SP.mark("sV", f"tev_{g}")
                else:
                    _, b, tt, fc = job
                    e.wait_ge(sT, SP.at("sT", f"ttr_{g}"))
                    e.tensor_copy(out=xqt(fc, tt), in_=psc[g % 2][:, 0:128]).then_inc(sV, 1)
                    SP.mark("sV", f"tev_{g}")
                if job[0] == "xq" and job[2] == 3 and job[3] == NF - 1:
                    b = job[1]
                    emit_p1_proj_vector(e, b)
                    emit_ffn_vector(e, b)
            SP.mark("sV", "r1free")
            if phases >= 3:
                emit_p3_vector(e)
            if phases >= 4:
                emit_p4_vector(e)

        def emit_ln_tile(e, b, tt):
            it = b * 4 + tt
            xt = av(O_XT + (it % 2) * 4096, 4096)
            xc = av(O_XC, 4096)
            W_(e, f"sX{(b * 4 + tt) % 2}", f"x_{b}_{tt}")
            e.reduce_sum(out=col(CL_M), in_=xt, axis=AX)
            e.tensor_scalar(out=col(CL_M), in0=col(CL_M), scalar1=float(1.0 / DM),
                            scalar2=None, op0=Alu.mult).then_inc(sV, 1)
            SP.mark("sV", f"mean_{b}_{tt}")
            e.wait_ge(sV, SP.at("sV", f"mean_{b}_{tt}"))
            if it > 0:
                pb_, pt = divmod(it - 1, 4)
                e.wait_ge(sT, SP.at("sT", f"trn_{pb_}_{pt}"))
            e.tensor_scalar(out=xc, in0=xt, scalar1=col(CL_M), scalar2=None, op0=Alu.subtract)
            e.tensor_tensor(out=xt, in0=xc, in1=xc, op=Alu.mult)
            e.reduce_sum(out=col(CL_T0), in_=xt, axis=AX)
            e.tensor_scalar(out=col(CL_T0), in0=col(CL_T0), scalar1=float(1.0 / DM),
                            scalar2=float(EPS), op0=Alu.mult, op1=Alu.add).then_inc(sV, 1)
            SP.mark("sV", f"var_{b}_{tt}")
            e.wait_ge(sS, SP.at("sS", f"sqrt_{b}_{tt}"))
            e.reciprocal(out=col(CL_R2), in_=col(CL_R)).then_inc(sV, 1)
            SP.mark("sV", f"rec_{b}_{tt}")
            e.wait_ge(sV, SP.at("sV", f"rec_{b}_{tt}"))
            e.tensor_scalar(out=xc, in0=xc, scalar1=col(CL_R2), scalar2=None,
                            op0=Alu.mult).then_inc(sV, 1)
            SP.mark("sV", f"xn_{b}_{tt}")
            e.tensor_reduce(out=col(CL_RA), in_=xc, axis=AX, op=Alu.max,
                            apply_absolute_value=True)
            e.tensor_scalar(out=col(CL_AS + it), in0=col(CL_RA),
                            scalar1=float(1.0 / QMAX), scalar2=None,
                            op0=Alu.mult).then_inc(sV, 1)
            SP.mark("sV", f"amax_{b}_{tt}")
            e.wait_ge(sV, SP.at("sV", f"amax_{b}_{tt}"))
            e.reciprocal(out=col(CL_RA), in_=col(CL_RA))
            e.tensor_scalar(out=col(CL_RA), in0=col(CL_RA), scalar1=float(QMAX),
                            scalar2=None, op0=Alu.mult).then_inc(sV, 1)
            SP.mark("sV", f"ra_{b}_{tt}")
            e.wait_ge(sV, SP.at("sV", f"ra_{b}_{tt}"))
            e.tensor_scalar(out=xt, in0=xc, scalar1=col(CL_RA), scalar2=None, op0=Alu.mult)
            e.tensor_scalar(out=xt, in0=xt, scalar1=col(CL_CALL), scalar2=None, op0=Alu.add)
            e.tensor_scalar(out=xt, in0=xt, scalar1=col(CL_CALL), scalar2=None,
                            op0=Alu.subtract).then_inc(sV, 1)
            SP.mark("sV", f"xq_{b}_{tt}")

        def emit_p1_proj_vector(e, b):
            for ui, (kind, g) in enumerate(PUNITS):
                for fc in range(NF):
                    i = (b * 24 + ui) * NF + fc
                    W_(e, f"sW{i % 2}", f"w_{b}_{ui}_{fc}")
                    if i >= 2:
                        j = i - 2
                        pb2, rem2 = divmod(j, 24 * NF)
                        pui2, pfc2 = divmod(rem2, NF)
                        e.wait_ge(sT, SP.at("sT", f"pmm_{pb2}_{pui2}_{pfc2}"))
                    wb = wbuf[:, (i % 2) * 512:(i % 2 + 1) * 512]
                    wh = whl[:, (i % 2) * 512:(i % 2 + 1) * 512]
                    wl = whl[:, 1024 + (i % 2) * 512:1024 + (i % 2 + 1) * 512]
                    e.tensor_copy(out=wh, in_=wb)
                    e.tensor_tensor(out=wl, in0=wb, in1=wh, op=Alu.subtract).then_inc(sV, 1)
                    SP.mark("sV", f"wsp_{b}_{ui}_{fc}")
                e.wait_ge(sT, SP.at("sT", f"grp_{b}_{ui}"))
                if kind == "V":
                    W_(e, "sP0", "p0")
                qi = QIDX[("proj", b, ui)]
                if qi >= 1:
                    pass
                evi = b * 32 + ui
                if evi >= 2:
                    prev = evi - 2
                    pb_, pui = divmod(prev, 32)
                    if pui < 24:
                        W_(e, f"sST{pui % 2}", f"st_{pb_}_{pui}_3")
                    else:
                        W_(e, f"sST{pui % 2}", f"fst_{pb_}_{pui - 24}_3")
                for q4 in range(4):
                    dv = av(O_EV + (ui % 2) * 2048 + q4 * 512, 512)
                    if kind in ("K", "Q"):
                        hh = g * 4 + q4
                        bci = {"Q": hh, "K": 32 + hh}[kind]
                        e.tensor_scalar(out=dv, in0=quad[q4][:],
                                        scalar1=bcol[:, bci:bci + 1], scalar2=None,
                                        op0=Alu.add).then_inc(sV, 1)
                    else:
                        e.tensor_tensor(out=dv, in0=quad[q4][:],
                                        in1=av(O_R1, 4096)[:, g * 512:(g + 1) * 512],
                                        op=Alu.add).then_inc(sV, 1)
                    SP.mark("sV", f"ev_{b}_{ui}_{q4}")

        def emit_ffn_vector(e, b):
            for jc in range(8):
                e.wait_ge(sT, SP.at("sT", f"fgrp_{b}_{jc}"))
                W_(e, f"sWX{(b * 8 + jc) % 2}", f"wsx_{b}_{jc}")
                evi = b * 32 + 24 + jc
                prev = evi - 2
                pb_, pui = divmod(prev, 32)
                if pui < 24:
                    W_(e, f"sST{pui % 2}", f"st_{pb_}_{pui}_3")
                else:
                    W_(e, f"sST{pui % 2}", f"fst_{pb_}_{pui - 24}_3")
                for tt in range(4):
                    ev = av(O_EV + (jc % 2) * 2048 + tt * 512, 512)
                    e.tensor_scalar(out=ev, in0=quad[tt][:],
                                    scalar1=col(CL_AS + b * 4 + tt), scalar2=None, op0=Alu.mult)
                    e.tensor_tensor(out=ev, in0=ev,
                                    in1=av(O_WSX + ((b * 8 + jc) % 2) * 512, 512), op=Alu.mult)
                    fp16_round(e, ev)
                    e.tensor_scalar(out=ev, in0=ev, scalar1=1.0, scalar2=None,
                                    op0=Alu.mult).then_inc(sV, 1)
                    SP.mark("sV", f"fev_{b}_{jc}_{tt}")
                SP.mark("sV", f"fevl_{b}_{jc}")

        def emit_p3_vector(e):
            def da_op(m):
                hl, s, b, ki = KT[m]
                u = UNIT_OF[(hl, s, b)]
                e.wait_ge(sS, SP.at("sS", f"exp_{m}"))
                if ki == 0:
                    if u >= 1:
                        e.wait_ge(sT, SP.at("sT", f"dmm_{u - 1}"))
                    e.tensor_copy(out=av(P3_DA, 512),
                                  in_=av(P3_P + (m % 6) * 512, 512)).then_inc(sV, 1)
                else:
                    e.tensor_tensor(out=av(P3_DA, 512), in0=av(P3_DA, 512),
                                    in1=av(P3_P + (m % 6) * 512, 512), op=Alu.add).then_inc(sV, 1)
                SP.mark("sV", f"da_{m}")
                if ki == NKS[s] - 1:
                    unit_tail(u)

            def unit_tail(u):
                e.wait_ge(sT, SP.at("sT", f"dmm_{u}"))
                e.reciprocal(out=av(P3_RD, 512)[0:1, :], in_=pd[0:1, :]).then_inc(sV, 1)
                SP.mark("sV", f"rdr_{u}")
                e.wait_ge(sT, SP.at("sT", f"bmm_{u}"))
                e.tensor_copy(out=av(P3_RD, 512), in_=pd[:]).then_inc(sV, 1)
                SP.mark("sV", f"rdb_{u}")
                if u >= 2:
                    W_(e, "sOS" if u % 2 == 0 else "sOS1", f"ostu_{u - 2}")
                elif u == 0:
                    W_(e, "sST0", "fst_1_6_3")
                    W_(e, "sST1", "fst_1_7_3")
                oeo = P3_OE + (u % 2) * 1024
                oe = av(oeo, 512)
                e.tensor_tensor(out=oe, in0=po[:], in1=av(P3_RD, 512), op=Alu.mult)
                e.tensor_copy(out=avb(oeo + 512, 512), in_=oe)
                e.tensor_tensor(out=avb(oeo + 768, 512), in0=oe,
                                in1=avb(oeo + 512, 512), op=Alu.subtract).then_inc(sV, 1)
                SP.mark("sV", f"osp_{u}")

            for n, (hl, s, b, ki) in enumerate(KT):
                if ki == 0 and b == 0:
                    W_(e, f"sAS{s % 2}", f"astrip_{hl}_{s}")
                if n == 0:
                    W_(e, "sST0", "fst_1_6_3")
                    W_(e, "sST1", "fst_1_7_3")
                e.wait_ge(sT, SP.at("sT", f"smm_{n}"))
                if n >= 2:
                    e.wait_ge(sS, SP.at("sS", f"exp_{n - 2}"))
                astrip = av(P3_A + (s % 2) * AW, AW)
                u0 = ki * 128 + 511
                slv = astrip[:, 511::-1] if u0 == 511 else astrip[:, u0:u0 - 512:-1]
                e.scalar_tensor_tensor(out=av(P3_SB + (n % 2) * 512, 512),
                                       in0=p3pool[n % 6][:], scalar=float(SCALE),
                                       in1=slv, op0=Alu.mult, op1=Alu.add).then_inc(sV, 1)
                SP.mark("sV", f"sb_{n}")
                if n >= 1:
                    da_op(n - 1)
            da_op(NKT - 1)

        def emit_p4_vector(e):
            for bbp in range(B):
                for mc in range(8):
                    for ch in range(32):
                        i = (bbp * 8 + mc) * 32 + ch
                        W_(e, f"sWP{i % 2}", f"wp_{i}")
                        if i >= 2:
                            e.wait_ge(sT, SP.at("sT", f"pmw_{i - 2}"))
                        wb = wbuf[:, (i % 2) * 512:(i % 2 + 1) * 512]
                        wh = whl[:, (i % 2) * 512:(i % 2 + 1) * 512]
                        wl = whl[:, 1024 + (i % 2) * 512:1024 + (i % 2 + 1) * 512]
                        e.tensor_copy(out=wh, in_=wb)
                        e.tensor_tensor(out=wl, in0=wb, in1=wh, op=Alu.subtract).then_inc(sV, 1)
                        SP.mark("sV", f"wpsp_{i}")
                    for tt in range(4):
                        iv = (bbp * 8 + mc) * 4 + tt
                        e.wait_ge(sT, SP.at("sT", f"ymm_{iv}"))
                        W_(e, f"sP5{iv % 2}", f"p4in_{iv}")
                        W_(e, "sBP", "bproj")
                        if iv >= 4:
                            W_(e, f"sY{iv % 4}", f"p4st_{iv - 4}")
                        ev = av(P4_EV + (iv % 4) * 512, 512)
                        e.tensor_tensor(out=ev, in0=quad[tt][:],
                                        in1=av(O_R1, 4096)[:, mc * 512:(mc + 1) * 512], op=Alu.add)
                        e.tensor_tensor(out=ev, in0=av(P5_X + (iv % 2) * 512, 512), in1=ev, op=Alu.add)
                        e.tensor_tensor(out=ev, in0=ev, in1=av(P5_F + (iv % 2) * 512, 512),
                                        op=Alu.add).then_inc(sV, 1)
                        SP.mark("sV", f"pe_{iv}")

        def emit_tensor(e):
            e.wait_ge(sV, SP.at("sV", "wqvals"))
            for g, job in enumerate(TRJOBS):
                if g >= 2:
                    e.wait_ge(sV, SP.at("sV", f"tev_{g - 2}"))
                if job[0] == "wq":
                    rt, fc = job[1], job[2]
                    src = av(O_XT + rt * 4096, 4096)[:, fc * 128:(fc + 1) * 128]
                elif job[0] == "xn":
                    _, b, tt, fc = job
                    if fc == 0:
                        e.wait_ge(sV, SP.at("sV", f"xn_{b}_{tt}"))
                    src = av(O_XC, 4096)[:, fc * 128:(fc + 1) * 128]
                else:
                    _, b, tt, fc = job
                    it = b * 4 + tt
                    if fc == 0:
                        e.wait_ge(sV, SP.at("sV", f"xq_{b}_{tt}"))
                    src = av(O_XT + (it % 2) * 4096, 4096)[:, fc * 128:(fc + 1) * 128]
                e.transpose(psc[g % 2][:, 0:128], src, ident).then_inc(sT, 1)
                SP.mark("sT", f"ttr_{g}")
                if job[0] == "xn" and job[3] == NF - 1:
                    SP.mark("sT", f"trn_{job[1]}_{job[2]}")
                if job[0] == "xq" and job[3] == NF - 1:
                    SP.mark("sT", f"trq_{job[1]}_{job[2]}")
                    if job[2] == 3:
                        b = job[1]
                        emit_p1_proj_tensor(e, b)
                        emit_ffn_tensor(e, b)
            if phases >= 3:
                emit_p3_tensor(e)
            if phases >= 4:
                emit_p4_tensor(e)

        def emit_p1_proj_tensor(e, b):
            for ui, (kind, g) in enumerate(PUNITS):
                qi = QIDX[("proj", b, ui)]
                if qi >= 1:
                    pu = QUSERS[qi - 1]
                    wait_quad_evacs(e, pu)
                for fc in range(NF):
                    i = (b * 24 + ui) * NF + fc
                    e.wait_ge(sV, SP.at("sV", f"wsp_{b}_{ui}_{fc}"))
                    wh = whl[:, (i % 2) * 512:(i % 2 + 1) * 512]
                    wl = whl[:, 1024 + (i % 2) * 512:1024 + (i % 2 + 1) * 512]
                    first, last = fc == 0, fc == NF - 1
                    if kind in ("K", "Q"):
                        rh = xnt_hi(fc)
                        rl = xnt_lo(fc)
                        for q4 in range(4):
                            lwh = wh[:, q4 * 128:(q4 + 1) * 128]
                            lwl = wl[:, q4 * 128:(q4 + 1) * 128]
                            e.matmul(quad[q4][:], lwh, rh, start=first, stop=False)
                            e.matmul(quad[q4][:], lwh, rl, start=False, stop=False)
                            mm = e.matmul(quad[q4][:], lwl, rh, start=False, stop=last)
                            if q4 == 3:
                                mm.then_inc(sT, 1)
                    else:
                        for q4 in range(4):
                            lh = xnt_hi(fc, q4 * 128, 128)
                            ll = xnt_lo(fc, q4 * 128, 128)
                            e.matmul(quad[q4][:], lh, wh, start=first, stop=False)
                            e.matmul(quad[q4][:], lh, wl, start=False, stop=False)
                            mm = e.matmul(quad[q4][:], ll, wh, start=False, stop=last)
                            if q4 == 3:
                                mm.then_inc(sT, 1)
                    SP.mark("sT", f"pmm_{b}_{ui}_{fc}")
                SP.mark("sT", f"grp_{b}_{ui}")

        def wait_quad_evacs(e, user):
            tag = user[0]
            if tag == "proj":
                _, b, ui = user
                for q4 in range(4):
                    e.wait_ge(sV, SP.at("sV", f"ev_{b}_{ui}_{q4}"))
            elif tag == "ffn":
                _, b, jc = user
                for q4 in range(4):
                    e.wait_ge(sV, SP.at("sV", f"fev_{b}_{jc}_{q4}"))
            else:
                _, si, mc = user
                for q4 in range(4):
                    e.wait_ge(sV, SP.at("sV", f"yev_{si}_{mc}_{q4}"))

        def emit_ffn_tensor(e, b):
            for jc in range(8):
                qi = QIDX[("ffn", b, jc)]
                pu = QUSERS[qi - 1]
                wait_quad_evacs(e, pu)
                for fc in range(NF):
                    i = (b * 8 + jc) * NF + fc
                    W_(e, f"sWQ{i % 4}", f"wq_{b}_{jc}_{fc}")
                    rhsw = wqs[:, (i % 4) * 512:(i % 4 + 1) * 512]
                    first, last = fc == 0, fc == NF - 1
                    for q4 in range(4):
                        mm = e.matmul(quad[q4][:], xqt(fc, q4), rhsw, start=first, stop=last)
                    mm.then_inc(sT, 1)
                    SP.mark("sT", f"fmm_{b}_{jc}_{fc}")
                SP.mark("sT", f"fgrp_{b}_{jc}")

        def emit_p3_tensor(e):
            LB = 4

            def pv_op(m):
                hl, s, b, ki = KT[m]
                u = UNIT_OF[(hl, s, b)]
                nk = NKS[s]
                vb = av(P3_KVQ + b * 12288 + 4096, 4096)
                if ki == 0 and u >= 1:
                    e.wait_ge(sV, SP.at("sV", f"osp_{u - 1}"))
                e.wait_ge(sS, SP.at("sS", f"exp_{m}"))
                e.matmul(po[:], vb[:, ki * 128:(ki + 1) * 128],
                         av(P3_P + (m % 6) * 512, 512),
                         start=(ki == 0), stop=(ki == nk - 1)).then_inc(sT, 1)
                SP.mark("sT", f"pv_{m}")
                if ki == nk - 1:
                    e.wait_ge(sV, SP.at("sV", f"da_{m}"))
                    if u >= 1:
                        e.wait_ge(sV, SP.at("sV", f"rdb_{u - 1}"))
                    e.matmul(pd[0:1, :], onescol, av(P3_DA, 512),
                             start=True, stop=True).then_inc(sT, 1)
                    SP.mark("sT", f"dmm_{u}")
                    e.wait_ge(sV, SP.at("sV", f"rdr_{u}"))
                    e.matmul(pd[:], av(P3_ONE, 128)[0:1, :], av(P3_RD, 512)[0:1, :],
                             start=True, stop=True).then_inc(sT, 1)
                    SP.mark("sT", f"bmm_{u}")

            for n, (hl, s, b, ki) in enumerate(KT):
                if n - LB >= 0:
                    pv_op(n - LB)
                base = P3_KVQ + b * 12288
                kb = av(base, 4096)
                qs = av(base + 8192, 4096)[:, s * 512:(s + 1) * 512]
                if ki == 0:
                    W_(e, f"sKV{b}", f"kvq_{hl}_{b}")
                if n >= 6:
                    e.wait_ge(sV, SP.at("sV", f"sb_{n - 6}"))
                elif n < 2:
                    e.wait_ge(sV, SP.at("sV", f"tev_{NTR - 2 + n}"))
                else:
                    e.wait_ge(sV, SP.at("sV", f"fev_1_7_{n - 2}"))
                e.matmul(p3pool[n % 6][:], kb[:, ki * 128:(ki + 1) * 128], qs,
                         start=True, stop=True).then_inc(sT, 1)
                SP.mark("sT", f"smm_{n}")
            for m in range(NKT - LB, NKT):
                pv_op(m)

        def emit_p4_tensor(e):
            for bbp in range(B):
                W_(e, f"sKV{bbp}", f"old_{bbp}")
                for mc in range(8):
                    for ch in range(32):
                        i = (bbp * 8 + mc) * 32 + ch
                        e.wait_ge(sV, SP.at("sV", f"wpsp_{i}"))
                        if ch == 0:
                            for tt in range(4):
                                iv = (bbp * 8 + mc) * 4 + tt
                                if iv >= 4:
                                    e.wait_ge(sV, SP.at("sV", f"pe_{iv - 4}"))
                        wh = whl[:, (i % 2) * 512:(i % 2 + 1) * 512]
                        wl = whl[:, 1024 + (i % 2) * 512:1024 + (i % 2 + 1) * 512]
                        first, last = ch == 0, ch == 31
                        for tt in range(4):
                            co = ch * 512 + tt * 128
                            oh = A.bitcast(BF16)[:, co:co + 128]
                            ol = A.bitcast(BF16)[:, 16384 + co:16384 + co + 128]
                            e.matmul(quad[tt][:], oh, wh, start=first, stop=False)
                            e.matmul(quad[tt][:], oh, wl, start=False, stop=False)
                            mm = e.matmul(quad[tt][:], ol, wh, start=False, stop=last)
                            if last:
                                mm.then_inc(sT, 1)
                                SP.mark("sT", f"ymm_{(bbp * 8 + mc) * 4 + tt}")
                        if not last:
                            mm.then_inc(sT, 1)
                        SP.mark("sT", f"pmw_{i}")

        SP.begin(1)
        emit_sync(MockEng(SP, SEMS))
        emit_gpsimd(MockEng(SP, SEMS, auto="sGP"))
        emit_scalar(MockEng(SP, SEMS, auto="sS"))
        emit_vector(MockEng(SP, SEMS, auto="sV"))
        emit_tensor(MockEng(SP, SEMS))

        SP.begin(2)

        AUTOSEM = {"sGP": sGP, "sS": sS, "sV": sV}

        def wrap(eng, auto=None):
            ah = AUTOSEM.get(auto)
            state = {"pend": None}

            def resolve():
                p = state["pend"]
                if p is not None:
                    state["pend"] = None
                    if not p[1][0]:
                        SP.inc(auto, 1)
                        p[0].then_inc(ah, 1)

            class W:
                def __getattr__(self2, name):
                    if name == "wait_ge":
                        def wg(sem, val):
                            resolve()
                            if val > 0:
                                eng.wait_ge(sem, val)
                        return wg
                    meth = getattr(eng, name)

                    def call(*a, **k):
                        resolve()
                        if ah is not None and name not in _NOCHAIN:
                            cnt = SP.counts.get(auto, 0)
                            if cnt > 0:
                                eng.wait_ge(ah, cnt)
                        inst = meth(*a, **k)
                        had = [False]
                        if ah is not None and name not in _NOCHAIN:
                            state["pend"] = (inst, had)

                        class Chain:
                            def then_inc(self3, sem, n=1):
                                had[0] = True
                                SP.inc(SEMS[id(sem)], n)
                                inst.then_inc(sem, n)
                                return self3
                        return Chain()
                    return call
            return W()

        block.sync(lambda eng: emit_sync(wrap(eng)))
        block.gpsimd(lambda eng: emit_gpsimd(wrap(eng, "sGP")))
        block.scalar(lambda eng: emit_scalar(wrap(eng, "sS")))
        block.vector(lambda eng: emit_vector(wrap(eng, "sV")))
        block.tensor(lambda eng: emit_tensor(wrap(eng)))

    return nc


def kernel(L0_attn_hidden_states, L0_attn_w_attn, L0_attn_b_attn,
           L0_attn_w_proj, L0_attn_b_proj, L0_ffn_weight):
    x = np.ascontiguousarray(np.asarray(L0_attn_hidden_states, dtype=np.float32))
    w_attn = np.ascontiguousarray(np.asarray(L0_attn_w_attn, dtype=np.float32))
    b_attn = np.ascontiguousarray(np.asarray(L0_attn_b_attn, dtype=np.float32)).reshape(1, -1)
    w_proj = np.ascontiguousarray(np.asarray(L0_attn_w_proj, dtype=np.float32))
    b_proj = np.ascontiguousarray(np.asarray(L0_attn_b_proj, dtype=np.float32)).reshape(1, -1)
    w_ffn = np.ascontiguousarray(np.asarray(L0_ffn_weight, dtype=np.float32))

    nc = build_nc(tuple(DEBUG_OUTS), PHASES)
    in_maps = []
    for c in range(NCORE):
        xr = np.concatenate([x[0, 512 * c:512 * c + 512, :],
                             x[1, 512 * c:512 * c + 512, :]], axis=0)
        idx = np.zeros((128, 8 * NIDX + 64), dtype=np.int32)
        p = np.arange(128, dtype=np.int64)
        for bbp in range(B):
            for ch in range(32):
                idx[:, 8 * NIDX + bbp * 32 + ch] = \
                    (ch // 4) * 2048 + bbp * 1024 + c + 8 * p
        for hl in range(4):
            for b in range(B):
                u = hl * B + b
                for r in range(NCORE):
                    base = r * (NCORE * KVQ) + c * KVQ
                    idx[:, u * NIDX + r] = base // 512 + p
                    idx[:, u * NIDX + 8 + r] = (base + 2 * Dh * 512) // 512 + p
                    for kt in range(4):
                        idx[:, u * NIDX + 16 + r * 4 + kt] = \
                            (base + Dh * 512) // 128 + kt * 128 + p
        in_maps.append({
            "xrows": np.ascontiguousarray(xr),
            "w_attn": w_attn,
            "w_proj": w_proj,
            "w_ffn": np.ascontiguousarray(w_ffn[512 * c:512 * c + 512, :]),
            "b_attn": b_attn,
            "b_attn_t": np.ascontiguousarray(b_attn.reshape(96, 128).T),
            "b_proj": b_proj,
            "consts": _consts_block(c),
            "a1idx": idx,
        })
    res = run_bass_kernel_spmd(nc, in_maps, core_ids=list(range(NCORE)))
    outp = np.empty((2, 4096, 4096), dtype=np.float32)
    for c in range(NCORE):
        o = res.results[c]["out"]
        outp[0, 512 * c:512 * c + 512, :] = o[0:512]
        outp[1, 512 * c:512 * c + 512, :] = o[512:1024]
    if DEBUG_OUTS:
        kernel.debug = [{k: np.asarray(res.results[c]["dbg_" + k]) for k in DEBUG_OUTS}
                        for c in range(NCORE)]
    return outp
